# version 15
# baseline (speedup 1.0000x reference)
import math
import numpy as np
import concourse.bass as bass
import concourse.mybir as mybir
from concourse.bass_utils import run_bass_kernel_spmd
from contextlib import ExitStack

F32 = mybir.dt.float32
BF16 = mybir.dt.bfloat16
I32 = mybir.dt.int32
ALU = mybir.AluOpType
ACT = mybir.ActivationFunctionType
AX = mybir.AxisListType.X

B, S, D = 2, 4096, 2048
NH, NKV, LORA = 16, 4, 512
DR, DN, DVh, DQK = 64, 128, 128, 192
HF = 5632
TOWN = 1024
NKCH = D // 128
SCL = 1.0 / math.sqrt(DQK)
EPS = 1e-5
ROPE_BASE = 10000.0
NW1CH = HF // 128

EVENG = [0, 1, 2, 3, 8, 9, 10, 11]
ODDG = [4, 5, 6, 7, 12, 13, 14, 15]
QR_PAIR = {}
for _i, _h in enumerate(EVENG):
    QR_PAIR[_h] = (_i, 0)
for _i, _h in enumerate(ODDG):
    QR_PAIR[_h] = (_i, 64)


def owned_tiles(c):
    return list(range(4 * c, 4 * c + 4)) + list(range(4 * (7 - c), 4 * (7 - c) + 4))


class Arena:
    def __init__(self, ap, n):
        self.ap, self.n, self.live = ap, n, {}

    def alloc(self, name, elems, align=32, at=None):
        assert name not in self.live, name
        elems = -(-elems // align) * align
        if at is not None:
            for so, se in self.live.values():
                assert at + elems <= so or at >= so + se, \
                    f"arena overlap: {name}@{at}+{elems} vs {so}+{se}"
            assert at + elems <= self.n
            self.live[name] = (at, elems)
            return
        off = 0
        for so, se in sorted(self.live.values()):
            if off + elems <= so:
                break
            off = max(off, so + se)
        assert off + elems <= self.n, f"arena OOM: {name} {elems}@{off} live={self.live}"
        self.live[name] = (off, elems)

    def free(self, name):
        del self.live[name]

    def bf(self, name, lo=0, hi=None):
        off, n = self.live[name]
        return self.ap[:, off + lo: off + (n if hi is None else hi)]

    r = bf


class Bld:

    def __init__(self, phases, dbg):
        self.phases, self.dbg = phases, dbg
        self.nc = bass.Bass("TRN2", target_bir_lowering=False)
        self.sem, self.cnt = {}, {}
        self.dbg_t = {}
        self._tp_evac_sv = [0, 0]

    def mksem(self, *names):
        for n in names:
            self.sem[n] = self.stack.enter_context(self.nc.semaphore(n))
            self.cnt[n] = 0

    def inc(self, instr, s, n=1):
        instr.then_inc(self.sem[s], n)
        self.cnt[s] += n
        return self.cnt[s]

    def wait(self, eng, s, val=None):
        v = self.cnt[s] if val is None else val
        if v > 0:
            eng.wait_ge(self.sem[s], v)

    def _chain(self, eng, sname, fn, *a, **kw):
        if self.cnt[sname] > 0:
            eng.wait_ge(self.sem[sname], self.cnt[sname])
        instr = fn(*a, **kw)
        instr.then_inc(self.sem[sname], 1)
        self.cnt[sname] += 1
        return self.cnt[sname]

    def V(self, op, *a, **kw):
        return self._chain(self.nc.vector, "sv", getattr(self.nc.vector, op), *a, **kw)

    def SC(self, op, *a, **kw):
        return self._chain(self.nc.scalar, "ss", getattr(self.nc.scalar, op), *a, **kw)

    def G(self, op, *a, **kw):
        return self._chain(self.nc.gpsimd, "sg", getattr(self.nc.gpsimd, op), *a, **kw)

    def MM(self, out, lhsT, rhs, start, stop, inc=False):
        instr = self.nc.tensor.matmul(out, lhsT, rhs, start=start, stop=stop)
        if inc:
            return self.inc(instr, "st", 1)
        return None

    def DMA(self, q, s, out, in_):
        instr = q.dma_start(out, in_)
        instr.then_inc(self.sem[s], 16)
        self.cnt[s] += 16
        return self.cnt[s]

    def dbg_out(self, name, shape, dtype=F32):
        if name not in self.dbg_t:
            self.dbg_t[name] = self.nc.dram_tensor(name, shape, dtype,
                                                   kind="ExternalOutput").ap()
        return self.dbg_t[name]

    def dbg_dump(self, name, shape, src_ap, waits=()):
        o = self.dbg_out(name, shape)
        cast = src_ap.dtype != F32
        e = self.nc.gpsimd if cast else self.nc.sync
        s = "dbgo" if cast else "dout"
        for w in waits:
            self.wait(e, w)
        self.DMA(e, s, o[:, :], src_ap)

    def build(self):
        nc = self.nc
        self.x_own = nc.dram_tensor("x_own", [TOWN, D], F32, kind="ExternalInput").ap()
        self.x_batch = nc.dram_tensor("x_batch", [S, D], F32, kind="ExternalInput").ap()
        self.tabk = nc.dram_tensor("tabk", [128, 2 * S], F32, kind="ExternalInput").ap()
        self.tabq = nc.dram_tensor("tabq", [128, 2 * TOWN], F32, kind="ExternalInput").ap()
        self.meta = nc.dram_tensor("meta", [1, 4], I32, kind="ExternalInput").ap()
        self.wq = nc.dram_tensor("wq", [24, 128, NKCH, 128], F32, kind="ExternalInput").ap()
        self.wckv = nc.dram_tensor("wckv", [4, 128, NKCH, 128], F32, kind="ExternalInput").ap()
        self.wk = nc.dram_tensor("wk", [LORA, 768], F32, kind="ExternalInput").ap()
        self.wv = nc.dram_tensor("wv", [LORA, 512], F32, kind="ExternalInput").ap()
        self.wo = nc.dram_tensor("wo", [D, D], F32, kind="ExternalInput").ap()
        self.w1 = nc.dram_tensor("w1", [HF, D], F32, kind="ExternalInput").ap()
        self.w2 = nc.dram_tensor("w2", [HF, D], F32, kind="ExternalInput").ap()
        self.w3 = nc.dram_tensor("w3", [D, HF], F32, kind="ExternalInput").ap()
        self.out = nc.dram_tensor("out", [TOWN, D], F32, kind="ExternalOutput").ap()
        self.h_dram = nc.dram_tensor("h_scratch", [TOWN, HF], F32).ap()
        self.attn_dram = nc.dram_tensor("attn_scratch", [TOWN, D], BF16).ap()

        with ExitStack() as stack:
            self.stack = stack
            ARENA_N = 89344
            FA_N = 7168
            IA_N = 512
            at = stack.enter_context(nc.sbuf_tensor("arena", [128, ARENA_N], BF16))
            self.A = Arena(at[:, :], ARENA_N)
            ft = stack.enter_context(nc.sbuf_tensor("farena", [128, FA_N], F32))
            self.FA = Arena(ft[:, :], FA_N)
            it = stack.enter_context(nc.sbuf_tensor("iarena", [128, IA_N], I32))
            self.IA = Arena(it[:, :], IA_N)
            self.consts = stack.enter_context(nc.sbuf_tensor("consts", [128, 320], BF16))
            self.stats = stack.enter_context(nc.sbuf_tensor("stats", [128, 288], F32))
            self.meta_sb = stack.enter_context(nc.sbuf_tensor("meta_sb", [1, 4], I32))
            self.psc0 = stack.enter_context(nc.psum_tensor("psc0", [128, 1024], F32))
            self.psc1 = stack.enter_context(nc.psum_tensor("psc1", [128, 1024], F32))
            self.ppv = stack.enter_context(nc.psum_tensor("ppv", [128, 1536], F32))
            self.pst = stack.enter_context(nc.psum_tensor("pst", [128, 512], F32))
            self.mksem("sv", "ss", "sg", "st", "dxb0", "dxb1", "dmeta", "dwg",
                       "dws0", "dws1", "dwsy0", "dwsy1", "dho0", "dho1",
                       "dhi0", "dhi1", "dout", "do0", "do1", "dw3a", "dw3b", "dbgo",
                       "sp3", "dwq0", "dwq1")
            self.I_bf = self.consts[:, 0:128]
            self.R = self.consts[:, 128:192]
            self.TRI = self.consts[:, 192:320]
            self.phase0()
            if self.phases >= 1:
                self.phase1()
            if self.phases >= 2:
                self.phase2()
            if self.phases >= 3:
                self.phase3()
            if self.phases >= 4:
                self.phase4()
            if self.phases >= 5:
                self.phase5()
            if self.phases >= 6:
                self.phase6()
            if self.phases >= 7:
                self.phase7()
            self.wait(nc.sync, "dout")
            self.wait(nc.gpsimd, "dbgo")
            self.wait(nc.gpsimd, "dwg")
        return nc

    S_ONE = 0
    S_OROW = 1
    S_LN = 4
    S_AM = 12
    S_SXQ = 16
    S_SXI = 24
    S_NB = 32
    S_QM = 48
    S_KM = 64
    S_QMR = 68
    S_KMR = 76
    S_PR = 80
    S_UHM = 96
    S_HM = 97
    S_LI = 98
    S_AW = 100
    S_SW = 104
    S_C1 = 116
    S_C2 = 124
    S_HAM = 132
    S_SXH = 140
    S_S3 = 148
    S_T0 = 156
    S_ONESROW = 160

    def st(self, col, n=1):
        return self.stats[:, col:col + n]

    def fence(self, engs=None):
        nc = self.nc
        engs = engs or (nc.sync, nc.gpsimd, nc.tensor, nc.vector, nc.scalar)
        for e in engs:
            for s in ("sv", "ss", "sg", "dbgo", "dout"):
                self.wait(e, s)
            if e is not nc.tensor:
                self.wait(e, "st")

    def emit_round(self, z, r0i, r0f, t1, out, t2=None):
        SC, V = self.SC, self.V
        se, ve = self.nc.scalar, self.nc.vector
        self.wait(se, "sv")
        SC("activation", r0i, z, ACT.Copy)
        SC("activation", r0f, r0i, ACT.Copy)
        self.wait(ve, "ss")
        V("scalar_tensor_tensor", t1, z, -0.5, r0f, ALU.add, ALU.is_gt)
        lt = z if t2 is None else t2
        V("scalar_tensor_tensor", lt, z, 0.5, r0f, ALU.add, ALU.is_lt)
        V("tensor_sub", t1, t1, lt)
        return V("tensor_add", out, r0f, t1)

    def phase0(self):
        nc, A, FA, V, SC, G, MM = self.nc, self.A, self.FA, self.V, self.SC, self.G, self.MM
        st, pst = self.st, self.pst
        sy, te, ve, se, gp = nc.sync, nc.tensor, nc.vector, nc.scalar, nc.gpsimd

        self.DMA(sy, "dmeta", self.meta_sb[:, :], self.meta[:, :])

        G("memset", st(self.S_ONE), 1.0)
        G("memset", self.stats[0:1, self.S_ONESROW:self.S_ONESROW + 128], 1.0)
        FA.alloc("cset", 128)
        cs = FA.r("cset")
        G("memset", cs, 1.0)
        G("affine_select", cs, cs, [[1, 128]], ALU.is_equal, 0.0,
          base=0, channel_multiplier=-1)
        self.wait(ve, "sg")
        V("tensor_copy", self.I_bf, cs)
        self.wait(gp, "sv")
        G("memset", cs, 1.0)
        G("affine_select", cs, cs, [[1, 128]], ALU.is_ge, 0.0,
          base=0, channel_multiplier=-1)
        self.wait(ve, "sg")
        V("tensor_copy", self.TRI, cs)
        self.wait(gp, "sv")
        G("memset", cs[:, 0:64], 0.0)
        for bp in (0, 64):
            G("memset", cs[bp:bp + 32, 32:64], 1.0)
            G("affine_select", cs[bp:bp + 32, 32:64], cs[bp:bp + 32, 32:64],
              [[1, 32]], ALU.is_equal, 0.0, base=0, channel_multiplier=-1)
            G("memset", cs[bp + 32:bp + 64, 0:32], -1.0)
            G("affine_select", cs[bp + 32:bp + 64, 0:32], cs[bp + 32:bp + 64, 0:32],
              [[1, 32]], ALU.is_equal, 0.0, base=0, channel_multiplier=-1)
        self.wait(ve, "sg")
        V("tensor_copy", self.R, cs[:, 0:64])
        G("memset", st(self.S_HM), 1.0)
        G("affine_select", st(self.S_HM), st(self.S_HM), [[0, 1]], ALU.is_ge, 0.0,
          base=63, channel_multiplier=-1)
        self.wait(ve, "sg")
        V("tensor_scalar", st(self.S_UHM), st(self.S_HM), -1.0, None, ALU.mult)
        V("tensor_scalar", st(self.S_UHM), st(self.S_UHM), 1.0, None, ALU.add)
        FA.free("cset")

        A.alloc("tab_sink", S, at=0)
        A.alloc("tab_cosk", S, at=4096)
        A.alloc("tab_sinq", TOWN, at=8192)
        A.alloc("tab_cosq", TOWN, at=9216)
        self.wait(te, "dmeta")
        self.wait(te, "sg")

    def phase1(self):
        nc, A, V, SC, G, MM = self.nc, self.A, self.V, self.SC, self.G, self.MM
        st, pst, psc0 = self.st, self.pst, self.psc0
        sy, te, ve, se, gp = nc.sync, nc.tensor, nc.vector, nc.scalar, nc.gpsimd

        self.fence()
        A.alloc("ckv", 4 * 4096, at=10240)
        A.alloc("wckv", 4 * 2048, at=26624)
        A.alloc("xnbf", 2 * D, at=34816)
        A.alloc("xngrp", 2 * 8192, at=38912)
        A.alloc("junkb", 2048, at=87296)
        self.FA.alloc("xtm", 2 * D)

        for m in range(4):
            self.DMA(gp, "dwg", A.bf("wckv", m * 2048, (m + 1) * 2048),
                     self.wckv[m].rearrange("p k j -> p (k j)"))
        dwg_ckv = self.cnt["dwg"]
        self.DMA(gp, "dbgo", A.bf("tab_sink"), self.tabk[:, 0:S])
        self.DMA(gp, "dbgo", A.bf("tab_cosk"), self.tabk[:, S:2 * S])
        self.DMA(gp, "dbgo", A.bf("tab_sinq"), self.tabq[:, 0:TOWN])
        self.DMA(gp, "dbgo", A.bf("tab_cosq"), self.tabq[:, TOWN:2 * TOWN])
        self.tab_cnt = self.cnt["dbgo"]

        xn_done_ss = [0] * 32
        x_dma_cnt = [0] * 32
        tp1_st = [0] * 32
        evac_sv = {}

        def issue_x(gt):
            par = gt % 2
            if gt >= 2:
                sy.wait_ge(self.sem["ss"], xn_done_ss[gt - 2])
                sy.wait_ge(self.sem["dxb%d" % par], x_dma_cnt[gt - 2])
            x_dma_cnt[gt] = self.DMA(sy, "dxb%d" % par,
                                     self.FA.r("xtm", par * D, (par + 1) * D),
                                     self.x_batch[gt * 128:(gt + 1) * 128, :])

        issue_x(0)
        issue_x(1)

        for g in range(8):
            for tt in range(4):
                gt = g * 4 + tt
                par = gt % 2
                xt = self.FA.r("xtm", par * D, (par + 1) * D)
                xnb = A.bf("xnbf", par * D, (par + 1) * D)
                ve.wait_ge(self.sem["dxb%d" % par], x_dma_cnt[gt])
                se.wait_ge(self.sem["dxb%d" % par], x_dma_cnt[gt])
                if gt >= 2:
                    se.wait_ge(self.sem["st"], tp1_st[gt - 2])
                xn_done_ss[gt] = self.ln_tile(xt, xnb)
                if gt + 2 < 32:
                    issue_x(gt + 2)
                self.wait(te, "ss", xn_done_ss[gt])
                self.transpose_2048(xnb, self._grp_dst_fn(g % 2, tt))
                tp1_st[gt] = self.cnt["st"]
            te.wait_ge(self.sem["ss"], max(self._tp_evac_sv))
            gsl_lo, gsl_hi = g * 512, (g + 1) * 512
            xng = A.bf("xngrp", (g % 2) * 8192, ((g % 2) + 1) * 8192)
            for m in range(4):
                slot = (m % 2)
                ps = psc0[:, slot * 512:slot * 512 + 512]
                if g == 0 and m == 0:
                    te.wait_ge(self.sem["dwg"], dwg_ckv)
                if (g, m) in evac_sv:
                    pass
                prev = (g, m - 2) if m >= 2 else ((g - 1, m + 2) if g > 0 else None)
                if prev in evac_sv:
                    te.wait_ge(self.sem["sv"], evac_sv[prev])
                stc = None
                for kk in range(NKCH):
                    stc = MM(ps, A.bf("wckv", m * 2048 + kk * 128, m * 2048 + (kk + 1) * 128),
                             xng[:, kk * 512:(kk + 1) * 512],
                             kk == 0, kk == NKCH - 1, inc=(kk == NKCH - 1))
                self.wait(ve, "st", stc)
                evac_sv[(g, m)] = V("tensor_copy",
                                    A.bf("ckv", m * 4096 + gsl_lo, m * 4096 + gsl_hi), ps)
        self._p1_last_evacs = evac_sv
        self._p1_tail = (xn_done_ss[30], xn_done_ss[31], x_dma_cnt[30], x_dma_cnt[31])
        A.free("xngrp")
        A.free("wckv")

        if self.dbg and self.phases < 2:
            self.dbg_dump("dbg_ckv", [128, 4 * 4096], A.bf("ckv"), waits=("sv",))

    def ln_tile(self, xt, xnb_out):
        st, V, SC = self.st, self.V, self.SC
        ve, se = self.nc.vector, self.nc.scalar
        ssum, ssq = st(self.S_LN), st(self.S_LN + 1)
        mean, var = st(self.S_LN + 2), st(self.S_LN + 3)
        rstd = st(self.S_LN + 4)
        V("reduce_sum", ssum, xt, AX)
        self.wait(se, "sv")
        SC("activation", self.A.bf("junkb"), xt, ACT.Square, accum_out=ssq)
        V("tensor_scalar", mean, ssum, 1.0 / D, None, ALU.mult)
        self.wait(ve, "ss")
        V("tensor_scalar", var, ssq, 1.0 / D, None, ALU.mult)
        V("tensor_mul", ssq, mean, mean)
        V("tensor_sub", var, var, ssq)
        V("tensor_scalar", var, var, EPS, None, ALU.add)
        self.wait(se, "sv")
        SC("activation", var, var, ACT.Sqrt)
        self.wait(ve, "ss")
        V("reciprocal", rstd, var)
        V("tensor_scalar", xt, xt, mean, None, ALU.subtract)
        self.wait(se, "sv")
        return SC("activation", xnb_out, xt, ACT.Copy, bias=0.0, scale=rstd)

    def tp_group(self, blocks, dst):
        te, se = self.nc.tensor, self.nc.scalar
        MM, SC = self.MM, self.SC
        buf = self._tp_toggle
        self._tp_toggle ^= 1
        ps = self.pst if buf == 0 else self.ppv
        if self._tp_evac_sv[buf]:
            te.wait_ge(self.sem["ss"], self._tp_evac_sv[buf])
        stc = None
        n = len(blocks)
        for kk, b in enumerate(blocks):
            stc = MM(ps[:, kk * 128:(kk + 1) * 128], b, self.I_bf,
                     True, True, inc=(kk == n - 1))
        self.wait(se, "st", stc)
        src = ps[:, 0:n * 128]
        if len(dst.shape) == 3:
            src = src.rearrange("p (a b) -> p a b", a=n)
        self._tp_evac_sv[buf] = SC("activation", dst, src, ACT.Copy)
        return self._tp_evac_sv[buf]

    def transpose_2048(self, src_bf, dst_fn):
        for kb in range(0, NKCH, 4):
            self.tp_group([src_bf[:, (kb + kk) * 128:(kb + kk + 1) * 128]
                           for kk in range(4)], dst_fn(kb))

    _tp_toggle = 0

    def _grp_dst_fn(self, par, tt):
        base = self.A.bf("xngrp", par * 8192, (par + 1) * 8192)
        r = base.rearrange("p (k c) -> p k c", c=512)

        def fn(kb):
            return r[:, kb:kb + 4, tt * 128:(tt + 1) * 128]
        return fn

    def phase2(self):
        nc, A, V, SC, G, MM = self.nc, self.A, self.V, self.SC, self.G, self.MM
        st, pst, psc0, psc1 = self.st, self.pst, self.psc0, self.psc1
        sy, te, ve, se, gp = nc.sync, nc.tensor, nc.vector, nc.scalar, nc.gpsimd
        self.ixq_dram = nc.dram_tensor("ixq_scratch", [128, 16384], BF16).ap()

        A.alloc("xnown", 16384, at=56576)
        A.alloc("ixq", 16384, at=38912)
        A.alloc("ixbf", 2048, at=26624)
        FA = self.FA
        FA.alloc("zq", 512)
        FA.alloc("rqf", 512)
        FA.alloc("rqt", 512)
        self.IA.alloc("ri", 512)
        self.wait(se, "st")

        def own_dst(name, j):
            base = A.bf(name)
            r = base.rearrange("p (k c) -> p k c", c=1024)

            def fn(kb):
                return r[:, kb:kb + 4, j * 128:(j + 1) * 128]
            return fn

        x_dma = [0] * 8
        xn_ss = [0] * 8
        cons_ss = [0] * 8
        tpx_st = [0] * 8
        tpi_st = [0] * 8

        p1t = self._p1_tail

        def issue_own(j):
            par = j % 2
            if j >= 2:
                sy.wait_ge(self.sem["ss"], cons_ss[j - 2])
                sy.wait_ge(self.sem["dxb%d" % par], x_dma[j - 2])
            else:
                sy.wait_ge(self.sem["ss"], p1t[j])
                sy.wait_ge(self.sem["dxb%d" % par], p1t[2 + j])
            x_dma[j] = self.DMA(sy, "dxb%d" % par,
                                self.FA.r("xtm", par * D, (par + 1) * D),
                                self.x_own[j * 128:(j + 1) * 128, :])

        issue_own(0)
        issue_own(1)
        for j in range(8):
            par = j % 2
            xt = self.FA.r("xtm", par * D, (par + 1) * D)
            xnb = A.bf("xnbf", par * D, (par + 1) * D)
            ve.wait_ge(self.sem["dxb%d" % par], x_dma[j])
            se.wait_ge(self.sem["dxb%d" % par], x_dma[j])
            if j >= 2:
                se.wait_ge(self.sem["st"], tpx_st[j - 2])
            xn_ss[j] = self.ln_tile(xt, xnb)
            am, am2, rr = st(self.S_AM), st(self.S_AM + 1), st(self.S_AM + 2)
            comb = st(self.S_T0)
            self.wait(ve, "ss")
            V("reduce_max", am, xt, AX, apply_absolute_value=True)
            V("tensor_mul", am, am, st(self.S_LN + 4))
            V("tensor_scalar", am2, am, EPS, None, ALU.max)
            V("reciprocal", rr, am2)
            V("tensor_scalar", st(self.S_SXI + j), am2, 1.0 / 127.0, None, ALU.mult)
            V("tensor_mul", comb, rr, st(self.S_LN + 4))
            V("tensor_scalar", comb, comb, 127.0, None, ALU.mult)
            rq_sv = 0
            if j >= 1:
                ve.wait_ge(self.sem["st"], tpi_st[j - 1])
            for ss_ in range(4):
                sl = slice(ss_ * 512, (ss_ + 1) * 512)
                self.wait(se, "sv")
                SC("activation", FA.r("zq"), xt[:, sl], ACT.Copy,
                   bias=0.0, scale=comb)
                rq_sv = self.emit_round(FA.r("zq"), self.IA.r("ri"), FA.r("rqf"),
                                        FA.r("rqt"), A.bf("ixbf")[:, sl])
            cons_ss[j] = self.cnt["ss"]
            if j + 2 < 8:
                issue_own(j + 2)
            self.wait(te, "ss", xn_ss[j])
            self.transpose_2048(xnb, own_dst("xnown", j))
            tpx_st[j] = self.cnt["st"]
            te.wait_ge(self.sem["sv"], rq_sv)
            self.transpose_2048(A.bf("ixbf"), own_dst("ixq", j))
            tpi_st[j] = self.cnt["st"]
            self.p5_tick(1)
        self.wait(gp, "ss")
        self.DMA(gp, "dwg", self.ixq_dram[:, :], A.bf("ixq"))
        ixq_spill_cnt = self.cnt["dwg"]
        A.free("ixq")
        A.free("ixbf")
        A.free("xnbf")
        A.free("junkb")
        FA.free("xtm")
        FA.free("zq")
        FA.free("rqf")
        FA.free("rqt")
        self.IA.free("ri")
        self.fence()

        if self.dbg:
            self.dbg_dump("dbg_xnown", [128, 16384], A.bf("xnown"), waits=("sv",))

        A.alloc("knope", 16384, at=72960)
        A.alloc("krope", 8192, at=43264)
        A.alloc("vaug", 16640, at=26624)
        A.alloc("wk_b", 4 * 768, at=51456)
        A.alloc("wv_b", 4 * 512, at=54528)
        self.p5_init()
        gp.wait_ge(self.sem["dwg"], ixq_spill_cnt)
        for r in range(4):
            self.DMA(gp, "dwg", A.bf("wk_b", r * 768, (r + 1) * 768),
                     self.wk[r * 128:(r + 1) * 128, :])
        for r in range(4):
            self.DMA(gp, "dwg", A.bf("wv_b", r * 512, (r + 1) * 512),
                     self.wv[r * 128:(r + 1) * 128, :])
        dwg_kv = self.cnt["dwg"]
        te.wait_ge(self.sem["dwg"], dwg_kv)
        self.wait(te, "sv")

        ev_prev = None
        for m in range(6):
            for n in range(8):
                slot = ((m * 8 + n) % 2)
                ps = psc0[:, slot * 512:(slot + 1) * 512]
                if ev_prev is not None and (m * 8 + n) >= 2:
                    te.wait_ge(self.sem["sv"], ev_two_ago)
                stc = None
                for kk in range(4):
                    stc = MM(ps, A.bf("wk_b", kk * 768 + m * 128, kk * 768 + (m + 1) * 128),
                             A.bf("ckv", kk * 4096 + n * 512, kk * 4096 + (n + 1) * 512),
                             kk == 0, kk == 3, inc=(kk == 3))
                self.wait(ve, "st", stc)
                if m < 4:
                    dst = A.bf("knope", m * 4096 + n * 512, m * 4096 + (n + 1) * 512)
                else:
                    dst = A.bf("krope", (m - 4) * 4096 + n * 512, (m - 4) * 4096 + (n + 1) * 512)
                ev = V("tensor_copy", dst, ps)
                ev_two_ago = ev_prev if ev_prev is not None else ev
                ev_prev = ev
                self.p5_tick(2)
        V("memset", A.bf("vaug"), 1.0)
        vr = A.bf("vaug").rearrange("p (t g c) -> p t g c", g=4, c=130)
        for t in range(32):
            slot = t % 2
            ps = psc1[:, slot * 512:(slot + 1) * 512]
            if t >= 2:
                te.wait_ge(self.sem["sv"], ev_two_ago)
            stc = None
            for kk in range(4):
                stc = MM(ps, A.bf("ckv", kk * 4096 + t * 128, kk * 4096 + (t + 1) * 128),
                         A.bf("wv_b", kk * 512, (kk + 1) * 512),
                         kk == 0, kk == 3, inc=(kk == 3))
            self.wait(ve, "st", stc)
            ev = V("tensor_copy", vr[:, t, :, 0:128],
                   ps.rearrange("p (g c) -> p g c", g=4))
            ev_two_ago = ev_prev
            ev_prev = ev
            self.p5_tick(2)
        A.free("ckv")
        A.free("wk_b")
        A.free("wv_b")

        FA.alloc("rsc1", 512)
        FA.alloc("rsc2", 512)
        self.wait(te, "sv")
        self.wait(te, "ss")
        ve.wait_ge(self.sem["dbgo"], self.tab_cnt)
        r2rd = [0, 0]
        tgl = 0
        for gpair in range(2):
            for half in range(2):
                po = half * 64
                for n in range(8):
                    b = tgl
                    tgl ^= 1
                    ps_ = (pst if b == 0 else self.ppv)[po:po + 64, 0:512]
                    kslice = A.bf("krope", gpair * 4096 + n * 512,
                                  gpair * 4096 + (n + 1) * 512)[po:po + 64, :]
                    if r2rd[b]:
                        te.wait_ge(self.sem["sv"], r2rd[b])
                    stc = MM(ps_, self.R[po:po + 64, :], kslice,
                             True, True, inc=True)
                    cosb = A.bf("tab_cosk", n * 512, (n + 1) * 512)[po:po + 64, :]
                    sinb = A.bf("tab_sink", n * 512, (n + 1) * 512)[po:po + 64, :]
                    r1 = FA.r("rsc1")[po:po + 64, :]
                    r2 = FA.r("rsc2")[po:po + 64, :]
                    V("tensor_mul", r1, kslice, cosb)
                    self.wait(ve, "st", stc)
                    r2rd[b] = V("tensor_mul", r2, ps_, sinb)
                    V("tensor_add", kslice, r1, r2)
        A.free("tab_sink")
        A.free("tab_cosk")
        if self.dbg:
            self.dbg_dump("dbg_knope", [128, 16384], A.bf("knope"), waits=("sv",))
            self.dbg_dump("dbg_krope", [128, 8192], A.bf("krope"), waits=("sv",))
            self.dbg_dump("dbg_vaug", [128, 16640], A.bf("vaug"), waits=("sv",))

        A.alloc("qnope", 16384, at=10240)
        A.alloc("qrope", 8192, at=0)
        A.alloc("wqbuf", 2 * 2048, at=51456)
        wq_dma = [0] * 24
        q_done_st = [0] * 24

        def issue_wq(m):
            par = m % 2
            if m >= 2:
                gp.wait_ge(self.sem["st"], q_done_st[m - 2])
                gp.wait_ge(self.sem["dwq%d" % par], wq_dma[m - 2])
            else:
                self.wait(gp, "sv")
                self.wait(gp, "ss")
                self.wait(gp, "st")
            wq_dma[m] = self.DMA(gp, "dwq%d" % par,
                                 A.bf("wqbuf", par * 2048, (par + 1) * 2048),
                                 self.wq[m].rearrange("p k j -> p (k j)"))

        issue_wq(0)
        issue_wq(1)
        self.wait(te, "ss")
        self.wait(te, "sv")
        pfree = [0, 0]
        for m in range(24):
            par = m % 2
            psb = psc0 if par == 0 else psc1
            te.wait_ge(self.sem["dwq%d" % par], wq_dma[m])
            if pfree[par]:
                te.wait_ge(self.sem["ss"], pfree[par])
            stc = None
            for n in range(2):
                ps = psb[:, n * 512:(n + 1) * 512]
                for kk in range(NKCH):
                    stc = MM(ps, A.bf("wqbuf", par * 2048 + kk * 128, par * 2048 + (kk + 1) * 128),
                             A.bf("xnown", kk * 1024 + n * 512, kk * 1024 + (n + 1) * 512),
                             kk == 0, kk == NKCH - 1, inc=(kk == NKCH - 1 and n == 1))
            q_done_st[m] = self.cnt["st"]
            if m + 2 < 24:
                issue_wq(m + 2)
            self.wait(se, "st", q_done_st[m])
            if m < 16:
                pfree[par] = SC("activation", A.bf("qnope", m * 1024, (m + 1) * 1024),
                                psb[:, 0:1024], ACT.Copy, bias=0.0, scale=SCL)
            else:
                p = m - 16
                pfree[par] = SC("activation", A.bf("qrope", p * 1024, (p + 1) * 1024),
                                psb[:, 0:1024], ACT.Copy, bias=0.0, scale=SCL)
            self.p5_tick(3)
        self.wait(te, "ss")
        self.wait(te, "sv")
        r2rd = [0, 0]
        tgl = 0
        for p in range(8):
            for half in range(2):
                po = half * 64
                for n in range(2):
                    b = tgl
                    tgl ^= 1
                    ps_ = (pst if b == 0 else self.ppv)[po:po + 64, 0:512]
                    qsl = A.bf("qrope", p * 1024 + n * 512,
                               p * 1024 + (n + 1) * 512)[po:po + 64, :]
                    if r2rd[b]:
                        te.wait_ge(self.sem["sv"], r2rd[b])
                    stc2 = MM(ps_, self.R[po:po + 64, :], qsl, True, True, inc=True)
                    cosb = A.bf("tab_cosq", n * 512, (n + 1) * 512)[po:po + 64, :]
                    sinb = A.bf("tab_sinq", n * 512, (n + 1) * 512)[po:po + 64, :]
                    r1 = FA.r("rsc1")[po:po + 64, :]
                    r2 = FA.r("rsc2")[po:po + 64, :]
                    V("tensor_mul", r1, qsl, cosb)
                    self.wait(ve, "st", stc2)
                    r2rd[b] = V("tensor_mul", r2, ps_, sinb)
                    V("tensor_add", qsl, r1, r2)
        A.free("xnown")
        A.free("wqbuf")
        A.free("tab_sinq")
        A.free("tab_cosq")
        FA.free("rsc1")
        FA.free("rsc2")

        self.p5_drain()
        self.wait(ve, "ss")
        for h in range(16):
            V("reduce_max", st(self.S_QM + h), A.bf("qnope", h * 1024, (h + 1) * 1024),
              AX, apply_absolute_value=True)
        for p in range(8):
            V("reduce_max", st(self.S_QMR + p), A.bf("qrope", p * 1024, (p + 1) * 1024),
              AX, apply_absolute_value=True)
        for g in range(4):
            V("reduce_max", st(self.S_KM + g), A.bf("knope", g * 4096, (g + 1) * 4096),
              AX, apply_absolute_value=True)
        for gpair in range(2):
            V("reduce_max", st(self.S_KMR + gpair),
              A.bf("krope", gpair * 4096, (gpair + 1) * 4096),
              AX, apply_absolute_value=True)
        for h in range(16):
            V("tensor_mul", st(self.S_PR + h), st(self.S_QM + h), st(self.S_KM + h // 4))
        for h in range(16):
            p, po = QR_PAIR[h]
            rp = st(self.S_T0)
            V("tensor_mul", rp, st(self.S_QMR + p), st(self.S_KMR + p // 4))
            V("tensor_mul", rp, rp, st(self.S_HM if po == 0 else self.S_UHM))
            V("tensor_add", st(self.S_PR + h), st(self.S_PR + h), rp)
        self.wait(te, "sv")
        stc = MM(self.pst[0:1, 0:16], st(self.S_ONE), self.stats[:, self.S_PR:self.S_PR + 16],
                 True, True, inc=True)
        self.wait(ve, "st", stc)
        V("tensor_scalar", self.stats[0:1, self.S_PR:self.S_PR + 16],
          self.pst[0:1, 0:16], -1.0, None, ALU.mult)
        self.wait(te, "sv")
        stc = MM(self.pst[:, 0:16], self.stats[0:1, self.S_ONESROW:self.S_ONESROW + 128],
                 self.stats[0:1, self.S_PR:self.S_PR + 16], True, True, inc=True)
        self.wait(ve, "st", stc)
        V("tensor_copy", self.stats[:, self.S_NB:self.S_NB + 16], self.pst[:, 0:16])
        if self.dbg:
            self.dbg_dump("dbg_q", [128, 16384], A.bf("qnope"), waits=("ss",))
            self.dbg_dump("dbg_qr", [128, 8192], A.bf("qrope"), waits=("sv",))
            nb = self.dbg_out("dbg_nb", [128, 16])
            self.wait(sy, "sv")
            self.DMA(sy, "dout", nb[:, :], self.stats[:, self.S_NB:self.S_NB + 16])

    def phase3(self):
        nc, A, V, SC, G, MM = self.nc, self.A, self.V, self.SC, self.G, self.MM
        st, pst = self.st, self.pst
        sy, te, ve, se, gp = nc.sync, nc.tensor, nc.vector, nc.scalar, nc.gpsimd
        A.alloc("ppan", 16896, at=51456)
        A.alloc("atst", 256, at=68352)
        self.fence()
        dat_base = (self.cnt["dho0"], self.cnt["dho1"])
        dat_cnt = {u: dat_base[u % 2] + 16 * (u // 2 + 1) for u in range(128)}

        self.wait(te, "dmeta")
        self.wait(se, "dmeta")
        self.wait(ve, "dmeta")
        rt = te.alloc_register("rt")
        rs = se.alloc_register("rs")
        rv = ve.alloc_register("rv")
        te.reg_load(rt, self.meta_sb[0:1, 0:1])
        se.reg_load(rs, self.meta_sb[0:1, 0:1])
        ve.reg_load(rv, self.meta_sb[0:1, 0:1])

        base = dict(self.cnt)

        def plan(c):
            tiles = owned_tiles(c)
            NKT = tiles[-1] + 1
            sj = [next(j for j in range(8) if tiles[j] >= kt) for kt in range(NKT)]
            poff = [0] * NKT
            acc = 0
            for kt in range(NKT):
                poff[kt] = acc
                acc += (8 - sj[kt]) * 128
            p = dict(tiles=tiles, NKT=NKT, sj=sj, poff=poff,
                     scores_st={}, pvj_st={}, exp_ss={}, mask_sv={}, evj_sv={})
            cst, css, csv = base["st"], base["ss"], base["sv"]
            for h in range(16):
                for kt in range(NKT):
                    cst += 1
                    p["scores_st"][(h, kt)] = cst
                    css += 1
                    p["exp_ss"][(h, kt)] = css
                    if kt in tiles:
                        csv += 1
                        p["mask_sv"][(h, kt)] = csv
                for j in range(8):
                    cst += 1
                    p["pvj_st"][(h, j)] = cst
                    csv += 2
                    p["evj_sv"][(h, j)] = csv
            p["end"] = dict(st=cst, ss=css, sv=csv)
            return p

        plans = [plan(c) for c in range(4)]
        ceil = dict(st=max(p["end"]["st"] for p in plans),
                    ss=max(p["end"]["ss"] for p in plans),
                    sv=max(p["end"]["sv"] for p in plans))

        def q_rhs(h, col0, w, which):
            if which == "n":
                return A.bf("qnope", h * 1024 + col0, h * 1024 + col0 + w)
            pq, po = QR_PAIR[h]
            return A.bf("qrope", pq * 1024 + col0, pq * 1024 + col0 + w)[po:po + 64, :]

        for c in range(4):
            p = plans[c]
            tiles, NKT, sj, poff = p["tiles"], p["NKT"], p["sj"], p["poff"]

            self.cnt = dict(base)
            with te.If_eq(rt, c):
                for h in range(16):
                    g = h // 4
                    goff = 64 * (g % 2)
                    for kt in range(NKT):
                        u = h * NKT + kt
                        buf = u % 2
                        psb = (self.psc0 if buf == 0 else self.psc1)
                        j0 = sj[kt]
                        col0, w = j0 * 128, (8 - j0) * 128
                        if u >= 2:
                            te.wait_ge(self.sem["ss"],
                                       p["exp_ss"][(h, kt - 2) if kt >= 2 else (h - 1, NKT - (2 - kt))])
                        kn = A.bf("knope", g * 4096 + kt * 128, g * 4096 + (kt + 1) * 128)
                        kr = A.bf("krope", (g // 2) * 4096 + kt * 128,
                                  (g // 2) * 4096 + (kt + 1) * 128)[goff:goff + 64, :]
                        nchunk = 1 if w <= 512 else 2
                        for ci in range(nchunk):
                            off = ci * 512
                            nw = min(512, w - off)
                            MM(psb[:, off:off + nw], kn,
                               q_rhs(h, col0 + off, nw, "n"), True, False)
                            MM(psb[:, off:off + nw], kr,
                               q_rhs(h, col0 + off, nw, "r"), False, True,
                               inc=(ci == nchunk - 1))
                        assert self.cnt["st"] == p["scores_st"][(h, kt)]
                    for j in range(8):
                        te.wait_ge(self.sem["ss"], p["exp_ss"][(h, tiles[j])])
                        if j in tiles or True:
                            te.wait_ge(self.sem["sv"], p["mask_sv"][(h, tiles[j])])
                        if j >= 3:
                            te.wait_ge(self.sem["sv"], p["evj_sv"][(h, j - 3)])
                        elif h > 0:
                            te.wait_ge(self.sem["sv"],
                                       p["evj_sv"][(h - 1, {0: 6, 1: 7, 2: 5}[j])])
                        bank = self.ppv[:, (j % 3) * 512:(j % 3) * 512 + 129]
                        for kt in range(tiles[j] + 1):
                            MM(bank,
                               A.bf("ppan", poff[kt] + (j - sj[kt]) * 128,
                                    poff[kt] + (j - sj[kt] + 1) * 128),
                               A.bf("vaug", kt * 520 + g * 130,
                                    kt * 520 + g * 130 + 129),
                               start=(kt == 0), stop=(kt == tiles[j]),
                               inc=(kt == tiles[j]))
                        assert self.cnt["st"] == p["pvj_st"][(h, j)]
                te.wait_ge(self.sem["st"], self.cnt["st"])
                for _ in range(ceil["st"] - self.cnt["st"]):
                    te.nop(nofuse=True).then_inc(self.sem["st"], 1)
                te.nop(nofuse=True).then_inc(self.sem["sp3"], 1)


        te.end_ifs()
        te.nop(nofuse=True)
        for c in range(4):
            p = plans[c]
            tiles, NKT, sj, poff = p["tiles"], p["NKT"], p["sj"], p["poff"]

            self.cnt = dict(base)
            with se.If_eq(rs, c):
                for h in range(16):
                    if h > 0:
                        se.wait_ge(self.sem["st"], p["pvj_st"][(h - 1, 7)])
                    for kt in range(NKT):
                        buf = (h * NKT + kt) % 2
                        psb = (self.psc0 if buf == 0 else self.psc1)
                        j0 = sj[kt]
                        w = (8 - j0) * 128
                        se.wait_ge(self.sem["st"], p["scores_st"][(h, kt)])
                        SC("activation", A.bf("ppan", poff[kt], poff[kt] + w),
                           psb[:, 0:w], ACT.Exp,
                           bias=self.stats[:, self.S_NB + h:self.S_NB + h + 1],
                           scale=1.0)
                        assert self.cnt["ss"] == p["exp_ss"][(h, kt)]
                se.wait_ge(self.sem["ss"], self.cnt["ss"])
                for _ in range(ceil["ss"] - self.cnt["ss"]):
                    se.nop(nofuse=True).then_inc(self.sem["ss"], 1)
                se.nop(nofuse=True).then_inc(self.sem["sp3"], 1)


        se.end_ifs()
        se.nop(nofuse=True)
        for c in range(4):
            p = plans[c]
            tiles, NKT, sj, poff = p["tiles"], p["NKT"], p["sj"], p["poff"]

            self.cnt = dict(base)
            with ve.If_eq(rv, c):
                for h in range(16):
                    for kt in range(NKT):
                        if kt not in tiles:
                            continue
                        ve.wait_ge(self.sem["ss"], p["exp_ss"][(h, kt)])
                        blk = A.bf("ppan", poff[kt], poff[kt] + 128)
                        V("tensor_mul", blk, blk, self.TRI)
                        assert self.cnt["sv"] == p["mask_sv"][(h, kt)]
                    for j in range(8):
                        u = h * 8 + j
                        par = u % 2
                        ve.wait_ge(self.sem["st"], p["pvj_st"][(h, j)])
                        if u >= 2:
                            ve.wait_ge(self.sem["dho%d" % par], dat_cnt[u - 2])
                        bank = self.ppv[:, (j % 3) * 512:(j % 3) * 512 + 129]
                        V("reciprocal", st(self.S_LI), bank[:, 128:129])
                        V("tensor_scalar",
                          A.bf("atst", par * 128, (par + 1) * 128),
                          bank[:, 0:128], st(self.S_LI), None, ALU.mult)
                        assert self.cnt["sv"] == p["evj_sv"][(h, j)]
                ve.wait_ge(self.sem["sv"], self.cnt["sv"])
                for _ in range(ceil["sv"] - self.cnt["sv"]):
                    ve.nop(nofuse=True).then_inc(self.sem["sv"], 1)
                ve.nop(nofuse=True).then_inc(self.sem["sp3"], 1)

        ve.end_ifs()
        ve.nop(nofuse=True)

        ev_uniform = lambda h, j: base["sv"] + 24 * h + 8 + 2 * (j + 1)
        for p in plans:
            for h in range(16):
                for j in range(8):
                    assert p["evj_sv"][(h, j)] == ev_uniform(h, j)
        for u in range(128):
            h, j = divmod(u, 8)
            par = u % 2
            gp.wait_ge(self.sem["sv"], ev_uniform(h, j))
            got = self.DMA(gp, "dho%d" % par,
                           self.attn_dram[j * 128:(j + 1) * 128,
                                          h * 128:(h + 1) * 128],
                           A.bf("atst", par * 128, (par + 1) * 128))
            assert got == dat_cnt[u], (u, got, dat_cnt[u])

        self.cnt.update(ceil)
        self.cnt["sp3"] = 3
        for e in (te, se, ve, sy, gp):
            e.wait_ge(self.sem["sp3"], 3)
        te.nop()
        se.nop()
        ve.nop()
        A.free("ppan")
        A.free("atst")
        A.free("knope")
        A.free("krope")
        A.free("vaug")
        A.free("qnope")
        A.free("qrope")

    def phase4(self):
        nc, A, V, SC, G, MM = self.nc, self.A, self.V, self.SC, self.G, self.MM
        st, pst, psc0, psc1 = self.st, self.pst, self.psc0, self.psc1
        sy, te, ve, se, gp = nc.sync, nc.tensor, nc.vector, nc.scalar, nc.gpsimd
        A.alloc("attn_fm", 16384, at=16384)
        A.alloc("atld", 4096, at=49152)
        self.fence()
        self.wait(sy, "dho0")
        self.wait(sy, "dho1")
        ar_dma = [0] * 8
        tr_done_st = [0] * 8

        def issue_ar(j):
            par = j % 2
            if j >= 2:
                sy.wait_ge(self.sem["st"], tr_done_st[j - 2])
                sy.wait_ge(self.sem["dxb%d" % par], ar_dma[j - 2])
            ar_dma[j] = self.DMA(sy, "dxb%d" % par,
                                 A.bf("atld", par * 2048, (par + 1) * 2048),
                                 self.attn_dram[j * 128:(j + 1) * 128, :])

        issue_ar(0)
        issue_ar(1)
        fmr = A.bf("attn_fm").rearrange("p (h c) -> p h c", c=1024)
        for j in range(8):
            par = j % 2
            te.wait_ge(self.sem["dxb%d" % par], ar_dma[j])
            for hg in range(4):
                self.tp_group(
                    [A.bf("atld", par * 2048 + (hg * 4 + hh) * 128,
                          par * 2048 + (hg * 4 + hh + 1) * 128) for hh in range(4)],
                    fmr[:, hg * 4:hg * 4 + 4, j * 128:(j + 1) * 128])
            tr_done_st[j] = self.cnt["st"]
            if j + 2 < 8:
                issue_ar(j + 2)
        A.alloc("attn_out", 16384, at=0)
        A.alloc("wo_bf", 16384, at=32768)

        wo_dma = [0] * 4
        wo_done_st = [0] * 4
        ev_sv = 0
        ev_prev2 = 0

        def issue_wo(og):
            par = og % 2
            if og >= 2:
                gp.wait_ge(self.sem["st"], wo_done_st[og - 2])
                gp.wait_ge(self.sem["dwq%d" % par], wo_dma[og - 2])
            else:
                self.wait(gp, "sv")
                self.wait(gp, "ss")
            for k in range(NKCH):
                self.DMA(gp, "dwq%d" % par,
                         A.bf("wo_bf", par * 8192 + k * 512, par * 8192 + (k + 1) * 512),
                         self.wo[k * 128:(k + 1) * 128, og * 512:(og + 1) * 512])
            wo_dma[og] = self.cnt["dwq%d" % par]

        issue_wo(0)
        issue_wo(1)
        for og in range(4):
            par = og % 2
            te.wait_ge(self.sem["dwq%d" % par], wo_dma[og])
            for tt in range(8):
                slot = tt % 2
                ps = psc0[:, slot * 512:(slot + 1) * 512]
                if ev_prev2:
                    te.wait_ge(self.sem["sv"], ev_prev2)
                stc = None
                for k in range(NKCH):
                    stc = MM(ps, A.bf("attn_fm", k * 1024 + tt * 128, k * 1024 + (tt + 1) * 128),
                             A.bf("wo_bf", par * 8192 + k * 512, par * 8192 + (k + 1) * 512),
                             k == 0, k == NKCH - 1, inc=(k == NKCH - 1))
                self.wait(ve, "st", stc)
                ev_prev2 = ev_sv
                ev_sv = V("tensor_copy",
                          A.bf("attn_out", tt * 2048 + og * 512, tt * 2048 + (og + 1) * 512),
                          ps)
            wo_done_st[og] = self.cnt["st"]
            if og + 2 < 4:
                issue_wo(og + 2)
        A.free("attn_fm")
        A.free("wo_bf")
        A.free("atld")
        if self.dbg:
            self.dbg_dump("dbg_attnout", [128, 16384], A.bf("attn_out"), waits=("sv",))

    def p5_init(self):
        nc, V, FA, st = self.nc, self.V, self.FA, self.st
        sy, se, gp = nc.sync, nc.scalar, nc.gpsimd
        FA.alloc("wmba", 2 * 1024)
        FA.alloc("wmbb", 2 * 1024)
        FA.alloc("wmbc", 2 * 704)
        V("memset", self.stats[:, self.S_AW:self.S_AW + 3], 0.0)

        def mk(dram, nchunk, parts, width, buf, sems, acc, qeng, red_eng):
            return dict(dram=dram, n=nchunk, parts=parts, w=width, buf=buf,
                        sems=sems, acc=acc, q=qeng, red_eng=red_eng,
                        dcnt=[0] * nchunk, red=[0] * nchunk,
                        cons=[0] * nchunk, nx=0)

        self._p5 = [
            mk(self.w1, 2 * NW1CH, 2, 1024, "wmba", ("dwsy0", "dwsy1"),
               self.S_AW, sy, "v"),
            mk(self.w2, 2 * NW1CH, 2, 1024, "wmbb", ("dw3a", "dw3b"),
               self.S_AW + 1, gp, "s"),
            mk(self.w3, 128, 8, 704, "wmbc", ("dws0", "dws1"),
               self.S_AW + 2, se, "v"),
        ]
        for s_ in self._p5:
            self._p5_issue(s_, 0)
            self._p5_issue(s_, 1)

    def _p5_issue(self, s_, i):
        par = i % 2
        q = s_["q"]
        if i >= 2:
            sem_kind = "sv" if s_["red_eng"] == "v" else "ss"
            q.wait_ge(self.sem[sem_kind], s_["cons"][i - 2])
            q.wait_ge(self.sem[s_["sems"][par]], s_["dcnt"][i - 2])
        else:
            self.wait(q, "sv")
            self.wait(q, "ss")
            self.wait(q, s_["sems"][par])
        p_, w_ = s_["parts"], s_["w"]
        src = s_["dram"][(i // p_) * 128:(i // p_ + 1) * 128,
                         (i % p_) * w_:(i % p_ + 1) * w_]
        s_["dcnt"][i] = self.DMA(q, s_["sems"][par],
                                 self.FA.r(s_["buf"], par * w_, (par + 1) * w_),
                                 src)

    def _p5_step(self, s_):
        V, SC, st, FA = self.V, self.SC, self.st, self.FA
        ve, se = self.nc.vector, self.nc.scalar
        i = s_["nx"]
        s_["nx"] += 1
        par = i % 2
        tmp = st(self.S_T0 + 3)
        bufap = FA.r(s_["buf"], par * s_["w"], (par + 1) * s_["w"])
        if s_["red_eng"] == "v":
            ve.wait_ge(self.sem[s_["sems"][par]], s_["dcnt"][i])
            V("reduce_sum", tmp, bufap, AX, apply_absolute_value=True)
            s_["red"][i] = V("tensor_add", st(s_["acc"]), st(s_["acc"]), tmp)
            s_["cons"][i] = s_["red"][i]
        else:
            tc = st(self.S_T0 + 1 + par)
            se.wait_ge(self.sem[s_["sems"][par]], s_["dcnt"][i])
            if i >= 2:
                se.wait_ge(self.sem["sv"], s_["red"][i - 2])
            s_["cons"][i] = SC("activation", bufap, bufap, ACT.Abs,
                               accum_out=tc)
            ve.wait_ge(self.sem["ss"], s_["cons"][i])
            s_["red"][i] = V("tensor_add", st(s_["acc"]), st(s_["acc"]), tc)
        if i + 2 < s_["n"]:
            self._p5_issue(s_, i + 2)

    def p5_tick(self, n=2):
        if not getattr(self, "_p5", None):
            return
        for _ in range(n):
            live = [s_ for s_ in self._p5 if s_["nx"] < s_["n"]]
            if not live:
                return
            self._p5_step(min(live, key=lambda s_: s_["nx"] / s_["n"]))

    def p5_drain(self):
        if not getattr(self, "_p5", None):
            return
        for s_ in self._p5:
            while s_["nx"] < s_["n"]:
                self._p5_step(s_)
        self._p5 = None

    def phase5(self):
        nc, A, V, SC, G, MM = self.nc, self.A, self.V, self.SC, self.G, self.MM
        st, pst = self.st, self.pst
        sy, te, ve, se, gp = nc.sync, nc.tensor, nc.vector, nc.scalar, nc.gpsimd
        FA = self.FA
        self.wait(te, "sv")
        self.wait(te, "ss")
        stc = MM(pst[0:1, 0:3], st(self.S_ONE), self.stats[:, self.S_AW:self.S_AW + 3],
                 True, True, inc=True)
        self.wait(ve, "st", stc)
        row = self.stats[0:1, self.S_PR:self.S_PR + 16]
        V("tensor_scalar", row[:, 0:3], pst[0:1, 0:3], 1.0 / (HF * D), None, ALU.mult)
        V("tensor_scalar", row[:, 0:3], row[:, 0:3], 1e-5, None, ALU.max)
        V("tensor_scalar", row[:, 3:6], row[:, 0:3], 0.5, None, ALU.mult)
        V("tensor_scalar", row[:, 6:9], row[:, 3:6], -1.0, None, ALU.mult)
        self.wait(te, "sv")
        stc = MM(pst[:, 0:9], self.stats[0:1, self.S_ONESROW:self.S_ONESROW + 128],
                 row[:, 0:9], True, True, inc=True)
        self.wait(ve, "st", stc)
        V("tensor_copy", self.stats[:, self.S_SW:self.S_SW + 9], pst[:, 0:9])
        for j in range(8):
            V("tensor_mul", st(self.S_C1 + j), st(self.S_SXI + j), st(self.S_SW))
            V("tensor_mul", st(self.S_C2 + j), st(self.S_SXI + j), st(self.S_SW + 1))
        FA.free("wmba")
        FA.free("wmbb")
        FA.free("wmbc")

    def phase6(self):
        nc, A, V, SC, G, MM = self.nc, self.A, self.V, self.SC, self.G, self.MM
        st, pst, psc0, psc1 = self.st, self.pst, self.psc0, self.psc1
        sy, te, ve, se, gp = nc.sync, nc.tensor, nc.vector, nc.scalar, nc.gpsimd
        A.alloc("iw1T", 16384, at=16384)
        A.alloc("iw2T", 16384, at=32768)
        A.alloc("ixq", 16384, at=49152)
        A.alloc("wqz", 4096, at=65536)
        A.alloc("wcmpb", 1024, at=69632)
        FA = self.FA
        FA.alloc("wch1", 3 * 1024)
        FA.alloc("wch2", 2 * 1024)
        FA.alloc("ma2", 512)
        FA.alloc("mh0", 512)
        FA.alloc("mh1", 512)
        self.fence()
        self.DMA(gp, "dwg", A.bf("ixq"), self.ixq_dram[:, :])
        ixq_cnt = self.cnt["dwg"]
        te.wait_ge(self.sem["dwg"], ixq_cnt)
        V("memset", self.stats[:, self.S_HAM:self.S_HAM + 8], 0.0)

        wsrc = {1: self.w1, 2: self.w2}
        wchn = {1: "wch1", 2: "wch2"}
        wq_eng = {1: sy, 2: se}
        wq_sem = {1: ("dwsy0", "dwsy1"), 2: ("dws0", "dws1")}
        iwt = {1: "iw1T", 2: "iw2T"}
        thr_col = {1: self.S_SW + 3, 2: self.S_SW + 4}
        nthr_col = {1: self.S_SW + 6, 2: self.S_SW + 7}
        wdma = {1: [0] * 2 * NW1CH, 2: [0] * 2 * NW1CH}
        wcons_sv = {1: [0] * 2 * NW1CH, 2: [0] * 2 * NW1CH}
        wtrans_sv = {}
        qz_last_st = [0, 0]
        mm_done_st = {}

        nslot = {1: 3, 2: 2}

        def issue_w(w, mm):
            slot = mm % nslot[w]
            par = mm % 2
            e = wq_eng[w]
            ns = nslot[w]
            if mm >= ns:
                e.wait_ge(self.sem["sv"], wcons_sv[w][mm - ns])
            if mm >= 2:
                e.wait_ge(self.sem[wq_sem[w][par]], wdma[w][mm - 2])
            if mm < ns:
                self.wait(e, "sv")
                self.wait(e, wq_sem[w][par])
            m, hf = divmod(mm, 2)
            wdma[w][mm] = self.DMA(e, wq_sem[w][par],
                                   FA.r(wchn[w], slot * 1024, (slot + 1) * 1024),
                                   wsrc[w][m * 128:(m + 1) * 128,
                                           hf * 1024:(hf + 1) * 1024])

        for w in (1, 2):
            for _i in range(nslot[w]):
                issue_w(w, _i)

        h_spill = {}
        for og in range(11):
            iwpar = og % 2
            for w in (1, 2):
                for rc in range(4):
                    m = og * 4 + rc
                    qz = A.bf("wqz", (m % 2) * 2048, (m % 2 + 1) * 2048)
                    if qz_last_st[m % 2]:
                        ve.wait_ge(self.sem["st"], qz_last_st[m % 2])
                    for hf in range(2):
                        mm = 2 * m + hf
                        slot = mm % nslot[w]
                        par = mm % 2
                        ve.wait_ge(self.sem[wq_sem[w][par]], wdma[w][mm])
                        wc = FA.r(wchn[w], slot * 1024, (slot + 1) * 1024)
                        V("tensor_scalar", A.bf("wcmpb"), wc,
                          self.stats[:, nthr_col[w]:nthr_col[w] + 1], None, ALU.is_lt)
                        wcons_sv[w][mm] = V(
                            "scalar_tensor_tensor", qz[:, hf * 1024:(hf + 1) * 1024],
                            wc, self.stats[:, thr_col[w]:thr_col[w] + 1], A.bf("wcmpb"),
                            ALU.is_gt, ALU.subtract)
                        if mm + nslot[w] < 2 * NW1CH:
                            issue_w(w, mm + nslot[w])
                    dstr = A.bf(iwt[w], iwpar * 8192, (iwpar + 1) * 8192) \
                        .rearrange("p (k c) -> p k c", c=512)
                    te.wait_ge(self.sem["sv"], wcons_sv[w][2 * m + 1])
                    for kb in range(0, NKCH, 4):
                        if og >= 2 and (w, og - 2) in mm_done_st:
                            te.wait_ge(self.sem["st"], mm_done_st[(w, og - 2)])
                        wtrans_sv[(w, m)] = self.tp_group(
                            [qz[:, (kb + kk) * 128:(kb + kk + 1) * 128]
                             for kk in range(4)],
                            dstr[:, kb:kb + 4, rc * 128:(rc + 1) * 128])
                    qz_last_st[m % 2] = self.cnt["st"]
            for tt in range(8):
                slot = tt % 2
                for w, psb in ((1, psc0), (2, psc1)):
                    ps = psb[:, slot * 512:(slot + 1) * 512]
                    te.wait_ge(self.sem["ss"], wtrans_sv[(w, og * 4 + 3)])
                    if (w, og, tt, "evac") in h_spill:
                        pass
                    if tt >= 2:
                        te.wait_ge(self.sem["ss" if w == 1 else "sv"],
                                   h_spill[(w, og, tt - 2)])
                    stc = None
                    for k in range(NKCH):
                        stc = MM(ps, A.bf("ixq", k * 1024 + tt * 128, k * 1024 + (tt + 1) * 128),
                                 A.bf(iwt[w], iwpar * 8192 + k * 512, iwpar * 8192 + (k + 1) * 512),
                                 k == 0, k == NKCH - 1, inc=(k == NKCH - 1))
                    mm_done_st[(w, og, tt)] = self.cnt["st"]
                mm_done_st[(1, og)] = mm_done_st[(1, og, tt)]
                mm_done_st[(2, og)] = mm_done_st[(2, og, tt)]
                self.wait(se, "st", mm_done_st[(1, og, tt)])
                self.wait(se, "sv")
                h_spill[(1, og, tt)] = SC(
                    "activation", self.ppv[:, 512:1024], psc0[:, slot * 512:(slot + 1) * 512],
                    ACT.Silu, bias=0.0,
                    scale=self.stats[:, self.S_C1 + tt:self.S_C1 + tt + 1])
                self.wait(ve, "st", mm_done_st[(2, og, tt)])
                h_spill[(2, og, tt)] = V(
                    "tensor_scalar", FA.r("ma2"), psc1[:, slot * 512:(slot + 1) * 512],
                    self.stats[:, self.S_C2 + tt:self.S_C2 + tt + 1], None, ALU.mult)
                gi = og * 8 + tt
                mhp = gi % 2
                mh = FA.r("mh0" if mhp == 0 else "mh1")
                if gi >= 2:
                    po, ptt = divmod(gi - 2, 8)
                    ve.wait_ge(self.sem["dho%d" % mhp], h_spill.get(("dma", po, ptt), 0))
                self.wait(ve, "ss")
                V("tensor_mul", mh, FA.r("ma2"), self.ppv[:, 512:1024])
                V("reduce_max", st(self.S_T0), mh, AX, apply_absolute_value=True)
                hs = V("tensor_max", st(self.S_HAM + tt), st(self.S_HAM + tt), st(self.S_T0))
                gp.wait_ge(self.sem["sv"], hs)
                h_spill[("dma", og, tt)] = self.DMA(
                    gp, "dho%d" % mhp, self.h_dram[tt * 128:(tt + 1) * 128,
                                                   og * 512:(og + 1) * 512], mh)
        self.wait(ve, "sg")
        for j in range(8):
            V("tensor_scalar", st(self.S_T0), st(self.S_HAM + j), 1e-5, None, ALU.max)
            V("reciprocal", st(self.S_T0 + 1), st(self.S_T0))
            V("tensor_scalar", st(self.S_SXH + j), st(self.S_T0 + 1), 127.0, None, ALU.mult)
            V("tensor_scalar", st(self.S_T0), st(self.S_T0), 1.0 / 127.0, None, ALU.mult)
            V("tensor_mul", st(self.S_S3 + j), st(self.S_T0), st(self.S_SW + 2))
        for nm in ("iw1T", "iw2T", "wqz", "ixq"):
            A.free(nm)
        A.free("wcmpb")
        for nm in ("wch1", "wch2", "ma2", "mh0", "mh1"):
            FA.free(nm)

    def phase7(self):
        nc, A, V, SC, G, MM = self.nc, self.A, self.V, self.SC, self.G, self.MM
        st, pst, psc0 = self.st, self.pst, self.psc0
        sy, te, ve, se, gp = nc.sync, nc.tensor, nc.vector, nc.scalar, nc.gpsimd
        A.alloc("ihq", 2048, at=83968)
        A.alloc("wqz3", 1024, at=86016)
        A.alloc("ih", 45056, at=16384)
        A.alloc("w3T", 22528, at=61440)
        FA = self.FA
        FA.alloc("hrl", 4 * 512)
        FA.alloc("z7p", 2 * 512)
        FA.alloc("f7p", 2 * 512)
        FA.alloc("t7p", 2 * 512)
        self.IA.alloc("i7", 512)
        self.fence()
        for e in (se, sy, ve, te, gp):
            self.wait(e, "dho0")
            self.wait(e, "dho1")
        self.wait(sy, "dwsy0")
        self.wait(sy, "dwsy1")
        self.wait(sy, "dxb0")
        self.wait(sy, "dxb1")
        self.wait(se, "dhi0")
        self.wait(se, "dhi1")

        hdma = {}
        tr_sv = {}
        rnd_sv = {}
        ihr = A.bf("ih").rearrange("p (k c) -> p k c", c=1024)

        def issue_h(og, tt):
            i = og * 8 + tt
            par = i % 2
            q4 = i % 4
            if i >= 2:
                se.wait_ge(self.sem["dhi%d" % par], hdma.get(i - 2, 0))
            if i >= 4:
                se.wait_ge(self.sem["ss"], hdma.get(("cons", i - 4), 0))
            hdma[i] = self.DMA(se, "dhi%d" % par,
                               FA.r("hrl", q4 * 512, (q4 + 1) * 512),
                               self.h_dram[tt * 128:(tt + 1) * 128,
                                           og * 512:(og + 1) * 512])

        for _w in range(4):
            issue_h(_w // 8, _w % 8)
        for og in range(11):
            for tt in range(8):
                i = og * 8 + tt
                par = i % 2
                q4 = i % 4
                se.wait_ge(self.sem["dhi%d" % par], hdma[i])
                if i >= 2:
                    se.wait_ge(self.sem["sv"], rnd_sv[i - 2])
                z7 = FA.r("z7p", par * 512, (par + 1) * 512)
                f7 = FA.r("f7p", par * 512, (par + 1) * 512)
                t7 = FA.r("t7p", par * 512, (par + 1) * 512)
                hdma[("cons", i)] = SC(
                    "activation", z7, FA.r("hrl", q4 * 512, (q4 + 1) * 512),
                    ACT.Copy, bias=0.0,
                    scale=self.stats[:, self.S_SXH + tt:self.S_SXH + tt + 1])
                SC("activation", self.IA.r("i7"), z7, ACT.Copy)
                f7ss = SC("activation", f7, self.IA.r("i7"), ACT.Copy)
                if i >= 4:
                    ve.wait_ge(self.sem["st"], tr_sv[("st", i - 4)])
                ve.wait_ge(self.sem["ss"], f7ss)
                ihq = A.bf("ihq", q4 * 512, (q4 + 1) * 512)
                V("scalar_tensor_tensor", t7, z7, -0.5, f7, ALU.add, ALU.is_gt)
                V("scalar_tensor_tensor", z7, z7, 0.5, f7, ALU.add, ALU.is_lt)
                V("tensor_sub", t7, t7, z7)
                rnd_sv[i] = V("tensor_add", ihq, f7, t7)
                if i + 4 < 88:
                    issue_h((i + 4) // 8, (i + 4) % 8)
                te.wait_ge(self.sem["sv"], rnd_sv[i])
                tr_sv[i] = self.tp_group(
                    [ihq[:, kk * 128:(kk + 1) * 128] for kk in range(4)],
                    ihr[:, og * 4:og * 4 + 4, tt * 128:(tt + 1) * 128])
                tr_sv[("st", i)] = self.cnt["st"]
        FA.free("hrl")
        FA.free("z7p")
        FA.free("f7p")
        FA.free("t7p")
        self.IA.free("i7")

        FA.alloc("wch3", 4 * 512)
        FA.alloc("w3s", 512)
        FA.alloc("xsl", 2 * 512)
        FA.alloc("f1", 512)
        FA.alloc("obuf", 2 * 512)
        w3dma = {}
        w3q_sv = {}
        w3t_st = {}
        out_dma = {}
        w3r = A.bf("w3T").rearrange("p (k c) -> p k c", c=512)
        for og3 in range(4):
            for mr in range(4):
                m = og3 * 4 + mr
                for sl in range(11):
                    s = m * 11 + sl
                    par = s % 2
                    q4 = s % 4
                    if s >= 2:
                        sy.wait_ge(self.sem["dwsy%d" % par], w3dma.get(s - 2, 0))
                    if s >= 4:
                        sy.wait_ge(self.sem["sv"], w3q_sv.get(s - 4, 0))
                    elif s < 2:
                        self.wait(sy, "ss")
                        self.wait(sy, "sv")
                    w3dma[s] = self.DMA(
                        sy, "dwsy%d" % par, FA.r("wch3", q4 * 512, (q4 + 1) * 512),
                        self.w3[m * 128:(m + 1) * 128, sl * 512:(sl + 1) * 512])
                    ve.wait_ge(self.sem["dwsy%d" % par], w3dma[s])
                    wc = FA.r("wch3", q4 * 512, (q4 + 1) * 512)
                    qz = A.bf("wqz3", par * 512, (par + 1) * 512)
                    if s >= 2:
                        ve.wait_ge(self.sem["st"], w3t_st.get(s - 2, 0))
                    V("tensor_scalar", FA.r("w3s"), wc,
                      self.stats[:, self.S_SW + 8:self.S_SW + 9], None, ALU.is_lt)
                    w3q_sv[s] = V(
                        "scalar_tensor_tensor", qz, wc,
                        self.stats[:, self.S_SW + 5:self.S_SW + 6], FA.r("w3s"),
                        ALU.is_gt, ALU.subtract)
                    te.wait_ge(self.sem["sv"], w3q_sv[s])
                    self.tp_group(
                        [qz[:, kk * 128:(kk + 1) * 128] for kk in range(4)],
                        w3r[:, sl * 4:sl * 4 + 4, mr * 128:(mr + 1) * 128])
                    w3t_st[s] = self.cnt["st"]
            self.wait(te, "sv")
            self.wait(te, "ss")
            for tt in range(8):
                i = og3 * 8 + tt
                slot = tt % 2
                ps = psc0[:, slot * 512:(slot + 1) * 512]
                xpar = i % 2
                if i >= 2:
                    sy.wait_ge(self.sem["sv"], out_dma.get(("xcons", i - 2), 0))
                    sy.wait_ge(self.sem["dxb%d" % xpar], out_dma.get(("x", i - 2), 0))
                out_dma[("x", i)] = self.DMA(
                    sy, "dxb%d" % xpar, FA.r("xsl", xpar * 512, (xpar + 1) * 512),
                    self.x_own[tt * 128:(tt + 1) * 128, og3 * 512:(og3 + 1) * 512])
                if i >= 2:
                    te.wait_ge(self.sem["ss"], out_dma.get(("evac", i - 2), 0))
                stc = None
                for k in range(44):
                    stc = MM(ps, ihr[:, k, tt * 128:(tt + 1) * 128],
                             A.bf("w3T", k * 512, (k + 1) * 512),
                             k == 0, k == 43, inc=(k == 43))
                self.wait(se, "st", stc)
                self.wait(se, "sv")
                out_dma[("evac", i)] = SC(
                    "activation", FA.r("f1"), ps, ACT.Copy, bias=0.0,
                    scale=self.stats[:, self.S_S3 + tt:self.S_S3 + tt + 1])
                obp = i % 2
                if i >= 2:
                    ve.wait_ge(self.sem["do%d" % obp], out_dma.get(("out", i - 2), 0))
                self.wait(ve, "ss")
                ve.wait_ge(self.sem["dxb%d" % xpar], out_dma[("x", i)])
                V("tensor_add", FA.r("f1"), FA.r("f1"),
                  A.bf("attn_out", tt * 2048 + og3 * 512, tt * 2048 + (og3 + 1) * 512))
                out_dma[("xcons", i)] = V(
                    "tensor_add", FA.r("obuf", obp * 512, (obp + 1) * 512),
                    FA.r("f1"), FA.r("xsl", xpar * 512, (xpar + 1) * 512))
                sy.wait_ge(self.sem["sv"], out_dma[("xcons", i)])
                out_dma[("out", i)] = self.DMA(
                    sy, "do%d" % obp, self.out[tt * 128:(tt + 1) * 128,
                                               og3 * 512:(og3 + 1) * 512],
                    FA.r("obuf", obp * 512, (obp + 1) * 512))
        self.wait(sy, "do0")
        self.wait(sy, "do1")


def rope_tables():
    import ctypes
    libm = ctypes.CDLL("libm.so.6")
    libm.powf.restype = ctypes.c_float
    libm.powf.argtypes = [ctypes.c_float, ctypes.c_float]
    libm.cosf.restype = ctypes.c_float
    libm.cosf.argtypes = [ctypes.c_float]
    libm.sinf.restype = ctypes.c_float
    libm.sinf.argtypes = [ctypes.c_float]
    ex = np.arange(0, DR, 2, dtype=np.float32) / np.float32(DR)
    inv = np.array([np.float32(1.0) / np.float32(libm.powf(ROPE_BASE, float(e)))
                    for e in ex], dtype=np.float32)
    t32 = np.arange(S, dtype=np.float32)
    fr = (t32[:, None] * inv[None, :]).astype(np.float32)
    cs = np.fromiter((libm.cosf(float(v)) for v in fr.ravel()),
                     np.float32, fr.size).reshape(S, 32)
    sn = np.fromiter((libm.sinf(float(v)) for v in fr.ravel()),
                     np.float32, fr.size).reshape(S, 32)
    sink = np.ascontiguousarray(np.tile(sn.T, (4, 1)))
    cosk = np.ascontiguousarray(np.tile(cs.T, (4, 1)))
    return sink, cosk


def host_prep(inputs):
    Wqkv = np.asarray(inputs["L0_attn_Wqkv"], dtype=np.float32)
    Wk_up = np.asarray(inputs["L0_attn_Wk_up"], dtype=np.float32)
    qn = [h * DQK + d for h in range(NH) for d in range(DN)]
    qr = []
    for p in range(8):
        for hl in (EVENG, ODDG):
            hh = hl[p]
            qr += [hh * DQK + DN + r for r in range(DR)]
    cols = qn + qr + list(range(NH * DQK, NH * DQK + LORA))
    Wp = np.ascontiguousarray(Wqkv[:, cols])
    wq_t = np.ascontiguousarray(
        Wp[:, :24 * 128].reshape(NKCH, 128, 24, 128).transpose(2, 1, 0, 3))
    wckv_t = np.ascontiguousarray(
        Wp[:, 24 * 128:].reshape(NKCH, 128, 4, 128).transpose(2, 1, 0, 3))
    kn = [g * DQK + d for g in range(NKV) for d in range(DN)]
    kr = []
    for gp_ in range(2):
        for half in range(2):
            g = gp_ * 2 + half
            kr += [g * DQK + DN + r for r in range(DR)]
    wk_p = np.ascontiguousarray(Wk_up[:, kn + kr])
    return wq_t, wckv_t, wk_p


def make_in_maps(inputs):
    wq_t, wckv_t, wk_p = host_prep(inputs)
    sink, cosk = rope_tables()
    tabk = np.ascontiguousarray(np.concatenate([sink, cosk], axis=1))
    x = np.asarray(inputs["L0_attn_hidden_states"], dtype=np.float32)
    ims = []
    for core in range(8):
        b, c = core // 4, core % 4
        tiles = owned_tiles(c)
        rows = np.concatenate([np.arange(t * 128, (t + 1) * 128) for t in tiles])
        ims.append(dict(
            x_own=np.ascontiguousarray(x[b][rows]),
            x_batch=np.ascontiguousarray(x[b]),
            tabk=tabk,
            tabq=np.ascontiguousarray(
                np.concatenate([sink[:, rows], cosk[:, rows]], axis=1)),
            meta=np.array([[c, 0, 0, 0]], dtype=np.int32),
            wq=wq_t, wckv=wckv_t, wk=wk_p,
            wv=np.ascontiguousarray(np.asarray(inputs["L0_attn_Wv_up"], dtype=np.float32)),
            wo=np.ascontiguousarray(np.asarray(inputs["L0_attn_Wo"], dtype=np.float32)),
            w1=np.ascontiguousarray(np.asarray(inputs["L0_ffn_w1"], dtype=np.float32)),
            w2=np.ascontiguousarray(np.asarray(inputs["L0_ffn_w2"], dtype=np.float32)),
            w3=np.ascontiguousarray(np.asarray(inputs["L0_ffn_w3"], dtype=np.float32)),
        ))
    return ims


def build_and_run(inputs, phases=8, dbg=False, cores=None):
    ims = make_in_maps(inputs)
    b = Bld(phases, dbg)
    nc = b.build()
    res = run_bass_kernel_spmd(nc, ims, core_ids=list(range(8)))
    return b, res


def kernel(**inputs):
    ims = make_in_maps(inputs)
    b = Bld(8, False)
    nc = b.build()
    res = run_bass_kernel_spmd(nc, ims, core_ids=list(range(8)))
    out = np.zeros((B, S, D), dtype=np.float32)
    for core in range(8):
        bb, c = core // 4, core % 4
        for j, t in enumerate(owned_tiles(c)):
            out[bb, t * 128:(t + 1) * 128, :] = res.results[core]["out"][j * 128:(j + 1) * 128, :]
    return out


# version 17
# speedup vs baseline: 1.0400x; 1.0400x over previous
import math
import numpy as np
import concourse.bass as bass
import concourse.mybir as mybir
from concourse.bass_utils import run_bass_kernel_spmd
from contextlib import ExitStack

F32 = mybir.dt.float32
BF16 = mybir.dt.bfloat16
I32 = mybir.dt.int32
ALU = mybir.AluOpType
ACT = mybir.ActivationFunctionType
AX = mybir.AxisListType.X

B, S, D = 2, 4096, 2048
NH, NKV, LORA = 16, 4, 512
DR, DN, DVh, DQK = 64, 128, 128, 192
HF = 5632
TOWN = 1024
NKCH = D // 128
SCL = 1.0 / math.sqrt(DQK)
EPS = 1e-5
ROPE_BASE = 10000.0
NW1CH = HF // 128

EVENG = [0, 1, 2, 3, 8, 9, 10, 11]
ODDG = [4, 5, 6, 7, 12, 13, 14, 15]
QR_PAIR = {}
for _i, _h in enumerate(EVENG):
    QR_PAIR[_h] = (_i, 0)
for _i, _h in enumerate(ODDG):
    QR_PAIR[_h] = (_i, 64)


def owned_tiles(c):
    return list(range(4 * c, 4 * c + 4)) + list(range(4 * (7 - c), 4 * (7 - c) + 4))


class Arena:
    def __init__(self, ap, n):
        self.ap, self.n, self.live = ap, n, {}

    def alloc(self, name, elems, align=32, at=None):
        assert name not in self.live, name
        elems = -(-elems // align) * align
        if at is not None:
            for so, se in self.live.values():
                assert at + elems <= so or at >= so + se, \
                    f"arena overlap: {name}@{at}+{elems} vs {so}+{se}"
            assert at + elems <= self.n
            self.live[name] = (at, elems)
            return
        off = 0
        for so, se in sorted(self.live.values()):
            if off + elems <= so:
                break
            off = max(off, so + se)
        assert off + elems <= self.n, f"arena OOM: {name} {elems}@{off} live={self.live}"
        self.live[name] = (off, elems)

    def free(self, name):
        del self.live[name]

    def bf(self, name, lo=0, hi=None):
        off, n = self.live[name]
        return self.ap[:, off + lo: off + (n if hi is None else hi)]

    r = bf


class Bld:

    def __init__(self, phases, dbg):
        self.phases, self.dbg = phases, dbg
        self.nc = bass.Bass("TRN2", target_bir_lowering=False)
        self.sem, self.cnt = {}, {}
        self.dbg_t = {}
        self._tp_evac_sv = [0, 0]

    def mksem(self, *names):
        for n in names:
            self.sem[n] = self.stack.enter_context(self.nc.semaphore(n))
            self.cnt[n] = 0

    def inc(self, instr, s, n=1):
        instr.then_inc(self.sem[s], n)
        self.cnt[s] += n
        return self.cnt[s]

    def wait(self, eng, s, val=None):
        v = self.cnt[s] if val is None else val
        if v > 0:
            eng.wait_ge(self.sem[s], v)

    def _chain(self, eng, sname, fn, *a, **kw):
        if self.cnt[sname] > 0:
            eng.wait_ge(self.sem[sname], self.cnt[sname])
        instr = fn(*a, **kw)
        instr.then_inc(self.sem[sname], 1)
        self.cnt[sname] += 1
        return self.cnt[sname]

    def V(self, op, *a, **kw):
        return self._chain(self.nc.vector, "sv", getattr(self.nc.vector, op), *a, **kw)

    def SC(self, op, *a, **kw):
        return self._chain(self.nc.scalar, "ss", getattr(self.nc.scalar, op), *a, **kw)

    def G(self, op, *a, **kw):
        return self._chain(self.nc.gpsimd, "sg", getattr(self.nc.gpsimd, op), *a, **kw)

    def MM(self, out, lhsT, rhs, start, stop, inc=False):
        instr = self.nc.tensor.matmul(out, lhsT, rhs, start=start, stop=stop)
        if inc:
            return self.inc(instr, "st", 1)
        return None

    def DMA(self, q, s, out, in_):
        instr = q.dma_start(out, in_)
        instr.then_inc(self.sem[s], 16)
        self.cnt[s] += 16
        return self.cnt[s]

    def dbg_out(self, name, shape, dtype=F32):
        if name not in self.dbg_t:
            self.dbg_t[name] = self.nc.dram_tensor(name, shape, dtype,
                                                   kind="ExternalOutput").ap()
        return self.dbg_t[name]

    def dbg_dump(self, name, shape, src_ap, waits=()):
        o = self.dbg_out(name, shape)
        cast = src_ap.dtype != F32
        e = self.nc.gpsimd if cast else self.nc.sync
        s = "dbgo" if cast else "dout"
        for w in waits:
            self.wait(e, w)
        self.DMA(e, s, o[:, :], src_ap)

    def build(self):
        nc = self.nc
        self.x_own = nc.dram_tensor("x_own", [TOWN, D], F32, kind="ExternalInput").ap()
        self.x_batch = nc.dram_tensor("x_batch", [S, D], F32, kind="ExternalInput").ap()
        self.tabk = nc.dram_tensor("tabk", [128, 2 * S], F32, kind="ExternalInput").ap()
        self.tabq = nc.dram_tensor("tabq", [128, 2 * TOWN], F32, kind="ExternalInput").ap()
        self.meta = nc.dram_tensor("meta", [1, 4], I32, kind="ExternalInput").ap()
        self.wq = nc.dram_tensor("wq", [24, 128, NKCH, 128], F32, kind="ExternalInput").ap()
        self.wckv = nc.dram_tensor("wckv", [4, 128, NKCH, 128], F32, kind="ExternalInput").ap()
        self.wk = nc.dram_tensor("wk", [LORA, 768], F32, kind="ExternalInput").ap()
        self.wv = nc.dram_tensor("wv", [LORA, 512], F32, kind="ExternalInput").ap()
        self.wo = nc.dram_tensor("wo", [D, D], F32, kind="ExternalInput").ap()
        self.w1 = nc.dram_tensor("w1", [HF, D], F32, kind="ExternalInput").ap()
        self.w2 = nc.dram_tensor("w2", [HF, D], F32, kind="ExternalInput").ap()
        self.w3 = nc.dram_tensor("w3", [D, HF], F32, kind="ExternalInput").ap()
        self.out = nc.dram_tensor("out", [TOWN, D], F32, kind="ExternalOutput").ap()
        self.h_dram = nc.dram_tensor("h_scratch", [TOWN, HF], F32).ap()
        self.attn_dram = nc.dram_tensor("attn_scratch", [TOWN, D], BF16).ap()

        with ExitStack() as stack:
            self.stack = stack
            ARENA_N = 89344
            FA_N = 7168
            IA_N = 512
            at = stack.enter_context(nc.sbuf_tensor("arena", [128, ARENA_N], BF16))
            self.A = Arena(at[:, :], ARENA_N)
            ft = stack.enter_context(nc.sbuf_tensor("farena", [128, FA_N], F32))
            self.FA = Arena(ft[:, :], FA_N)
            it = stack.enter_context(nc.sbuf_tensor("iarena", [128, IA_N], I32))
            self.IA = Arena(it[:, :], IA_N)
            self.consts = stack.enter_context(nc.sbuf_tensor("consts", [128, 320], BF16))
            self.stats = stack.enter_context(nc.sbuf_tensor("stats", [128, 288], F32))
            self.meta_sb = stack.enter_context(nc.sbuf_tensor("meta_sb", [1, 4], I32))
            self.psc0 = stack.enter_context(nc.psum_tensor("psc0", [128, 1024], F32))
            self.psc1 = stack.enter_context(nc.psum_tensor("psc1", [128, 1024], F32))
            self.ppv = stack.enter_context(nc.psum_tensor("ppv", [128, 1536], F32))
            self.pst = stack.enter_context(nc.psum_tensor("pst", [128, 512], F32))
            self.mksem("sv", "ss", "sg", "st", "dxb0", "dxb1", "dmeta", "dwg",
                       "dws0", "dws1", "dwsy0", "dwsy1", "dho0", "dho1",
                       "dhi0", "dhi1", "dout", "do0", "do1", "dw3a", "dw3b", "dbgo",
                       "sp3", "dwq0", "dwq1")
            self.I_bf = self.consts[:, 0:128]
            self.R = self.consts[:, 128:192]
            self.TRI = self.consts[:, 192:320]
            self.phase0()
            if self.phases >= 1:
                self.phase1()
            if self.phases >= 2:
                self.phase2()
            if self.phases >= 3:
                self.phase3()
            if self.phases >= 4:
                self.phase4()
            if self.phases >= 5:
                self.phase5()
            if self.phases >= 6:
                self.phase6()
            if self.phases >= 7:
                self.phase7()
            self.wait(nc.sync, "dout")
            self.wait(nc.gpsimd, "dbgo")
            self.wait(nc.gpsimd, "dwg")
        return nc

    S_ONE = 0
    S_OROW = 1
    S_LN = 4
    S_AM = 12
    S_SXQ = 16
    S_SXI = 24
    S_NB = 32
    S_QM = 48
    S_KM = 64
    S_QMR = 68
    S_KMR = 76
    S_PR = 80
    S_UHM = 96
    S_HM = 97
    S_LI = 98
    S_AW = 100
    S_SW = 104
    S_C1 = 116
    S_C2 = 124
    S_HAM = 132
    S_SXH = 140
    S_S3 = 148
    S_T0 = 156
    S_ONESROW = 160

    def st(self, col, n=1):
        return self.stats[:, col:col + n]

    def fence(self, engs=None):
        nc = self.nc
        engs = engs or (nc.sync, nc.gpsimd, nc.tensor, nc.vector, nc.scalar)
        for e in engs:
            for s in ("sv", "ss", "sg", "dbgo", "dout"):
                self.wait(e, s)
            if e is not nc.tensor:
                self.wait(e, "st")

    def emit_round(self, z, r0i, r0f, t1, out, t2=None):
        SC, V = self.SC, self.V
        se, ve = self.nc.scalar, self.nc.vector
        self.wait(se, "sv")
        SC("activation", r0i, z, ACT.Copy)
        SC("activation", r0f, r0i, ACT.Copy)
        self.wait(ve, "ss")
        V("scalar_tensor_tensor", t1, z, -0.5, r0f, ALU.add, ALU.is_gt)
        lt = z if t2 is None else t2
        V("scalar_tensor_tensor", lt, z, 0.5, r0f, ALU.add, ALU.is_lt)
        V("tensor_sub", t1, t1, lt)
        return V("tensor_add", out, r0f, t1)

    def phase0(self):
        nc, A, FA, V, SC, G, MM = self.nc, self.A, self.FA, self.V, self.SC, self.G, self.MM
        st, pst = self.st, self.pst
        sy, te, ve, se, gp = nc.sync, nc.tensor, nc.vector, nc.scalar, nc.gpsimd

        self.DMA(sy, "dmeta", self.meta_sb[:, :], self.meta[:, :])

        G("memset", st(self.S_ONE), 1.0)
        G("memset", self.stats[0:1, self.S_ONESROW:self.S_ONESROW + 128], 1.0)
        FA.alloc("cset", 128)
        cs = FA.r("cset")
        G("memset", cs, 1.0)
        G("affine_select", cs, cs, [[1, 128]], ALU.is_equal, 0.0,
          base=0, channel_multiplier=-1)
        self.wait(ve, "sg")
        V("tensor_copy", self.I_bf, cs)
        self.wait(gp, "sv")
        G("memset", cs, 1.0)
        G("affine_select", cs, cs, [[1, 128]], ALU.is_ge, 0.0,
          base=0, channel_multiplier=-1)
        self.wait(ve, "sg")
        V("tensor_copy", self.TRI, cs)
        self.wait(gp, "sv")
        G("memset", cs[:, 0:64], 0.0)
        for bp in (0, 64):
            G("memset", cs[bp:bp + 32, 32:64], 1.0)
            G("affine_select", cs[bp:bp + 32, 32:64], cs[bp:bp + 32, 32:64],
              [[1, 32]], ALU.is_equal, 0.0, base=0, channel_multiplier=-1)
            G("memset", cs[bp + 32:bp + 64, 0:32], -1.0)
            G("affine_select", cs[bp + 32:bp + 64, 0:32], cs[bp + 32:bp + 64, 0:32],
              [[1, 32]], ALU.is_equal, 0.0, base=0, channel_multiplier=-1)
        self.wait(ve, "sg")
        V("tensor_copy", self.R, cs[:, 0:64])
        G("memset", st(self.S_HM), 1.0)
        G("affine_select", st(self.S_HM), st(self.S_HM), [[0, 1]], ALU.is_ge, 0.0,
          base=63, channel_multiplier=-1)
        self.wait(ve, "sg")
        V("tensor_scalar", st(self.S_UHM), st(self.S_HM), -1.0, None, ALU.mult)
        V("tensor_scalar", st(self.S_UHM), st(self.S_UHM), 1.0, None, ALU.add)
        FA.free("cset")

        A.alloc("tab_sink", S, at=0)
        A.alloc("tab_cosk", S, at=4096)
        A.alloc("tab_sinq", TOWN, at=8192)
        A.alloc("tab_cosq", TOWN, at=9216)
        self.wait(te, "dmeta")
        self.wait(te, "sg")

    def phase1(self):
        nc, A, V, SC, G, MM = self.nc, self.A, self.V, self.SC, self.G, self.MM
        st, pst, psc0 = self.st, self.pst, self.psc0
        sy, te, ve, se, gp = nc.sync, nc.tensor, nc.vector, nc.scalar, nc.gpsimd

        self.fence()
        A.alloc("ckv", 4 * 4096, at=10240)
        A.alloc("wckv", 4 * 2048, at=26624)
        A.alloc("xnbf", 2 * D, at=34816)
        A.alloc("xngrp", 2 * 8192, at=38912)
        A.alloc("junkb", 2048, at=87296)
        self.FA.alloc("xtm", 2 * D)

        for m in range(4):
            self.DMA(gp, "dwg", A.bf("wckv", m * 2048, (m + 1) * 2048),
                     self.wckv[m].rearrange("p k j -> p (k j)"))
        dwg_ckv = self.cnt["dwg"]
        self.DMA(gp, "dbgo", A.bf("tab_sink"), self.tabk[:, 0:S])
        self.DMA(gp, "dbgo", A.bf("tab_cosk"), self.tabk[:, S:2 * S])
        self.DMA(gp, "dbgo", A.bf("tab_sinq"), self.tabq[:, 0:TOWN])
        self.DMA(gp, "dbgo", A.bf("tab_cosq"), self.tabq[:, TOWN:2 * TOWN])
        self.tab_cnt = self.cnt["dbgo"]

        xn_done_ss = [0] * 32
        x_dma_cnt = [0] * 32
        tp1_st = [0] * 32
        evac_sv = {}

        def issue_x(gt):
            par = gt % 2
            if gt >= 2:
                sy.wait_ge(self.sem["ss"], xn_done_ss[gt - 2])
                sy.wait_ge(self.sem["dxb%d" % par], x_dma_cnt[gt - 2])
            x_dma_cnt[gt] = self.DMA(sy, "dxb%d" % par,
                                     self.FA.r("xtm", par * D, (par + 1) * D),
                                     self.x_batch[gt * 128:(gt + 1) * 128, :])

        issue_x(0)
        issue_x(1)

        for g in range(8):
            for tt in range(4):
                gt = g * 4 + tt
                par = gt % 2
                xt = self.FA.r("xtm", par * D, (par + 1) * D)
                xnb = A.bf("xnbf", par * D, (par + 1) * D)
                ve.wait_ge(self.sem["dxb%d" % par], x_dma_cnt[gt])
                se.wait_ge(self.sem["dxb%d" % par], x_dma_cnt[gt])
                if gt >= 2:
                    se.wait_ge(self.sem["st"], tp1_st[gt - 2])
                xn_done_ss[gt] = self.ln_tile(xt, xnb)
                if gt + 2 < 32:
                    issue_x(gt + 2)
                self.wait(te, "ss", xn_done_ss[gt])
                self.transpose_2048(xnb, self._grp_dst_fn(g % 2, tt))
                tp1_st[gt] = self.cnt["st"]
            te.wait_ge(self.sem["ss"], max(self._tp_evac_sv))
            gsl_lo, gsl_hi = g * 512, (g + 1) * 512
            xng = A.bf("xngrp", (g % 2) * 8192, ((g % 2) + 1) * 8192)
            for m in range(4):
                slot = (m % 2)
                ps = psc0[:, slot * 512:slot * 512 + 512]
                if g == 0 and m == 0:
                    te.wait_ge(self.sem["dwg"], dwg_ckv)
                if (g, m) in evac_sv:
                    pass
                prev = (g, m - 2) if m >= 2 else ((g - 1, m + 2) if g > 0 else None)
                if prev in evac_sv:
                    te.wait_ge(self.sem["sv"], evac_sv[prev])
                stc = None
                for kk in range(NKCH):
                    stc = MM(ps, A.bf("wckv", m * 2048 + kk * 128, m * 2048 + (kk + 1) * 128),
                             xng[:, kk * 512:(kk + 1) * 512],
                             kk == 0, kk == NKCH - 1, inc=(kk == NKCH - 1))
                self.wait(ve, "st", stc)
                evac_sv[(g, m)] = V("tensor_copy",
                                    A.bf("ckv", m * 4096 + gsl_lo, m * 4096 + gsl_hi), ps)
        self._p1_last_evacs = evac_sv
        self._p1_tail = (xn_done_ss[30], xn_done_ss[31], x_dma_cnt[30], x_dma_cnt[31])
        A.free("xngrp")
        A.free("wckv")

        if self.dbg and self.phases < 2:
            self.dbg_dump("dbg_ckv", [128, 4 * 4096], A.bf("ckv"), waits=("sv",))

    def ln_tile(self, xt, xnb_out):
        st, V, SC = self.st, self.V, self.SC
        ve, se = self.nc.vector, self.nc.scalar
        ssum, ssq = st(self.S_LN), st(self.S_LN + 1)
        mean, var = st(self.S_LN + 2), st(self.S_LN + 3)
        rstd = st(self.S_LN + 4)
        V("reduce_sum", ssum, xt, AX)
        self.wait(se, "sv")
        SC("activation", self.A.bf("junkb"), xt, ACT.Square, accum_out=ssq)
        V("tensor_scalar", mean, ssum, 1.0 / D, None, ALU.mult)
        self.wait(ve, "ss")
        V("tensor_scalar", var, ssq, 1.0 / D, None, ALU.mult)
        V("tensor_mul", ssq, mean, mean)
        V("tensor_sub", var, var, ssq)
        V("tensor_scalar", var, var, EPS, None, ALU.add)
        self.wait(se, "sv")
        SC("activation", var, var, ACT.Sqrt)
        self.wait(ve, "ss")
        V("reciprocal", rstd, var)
        V("tensor_scalar", xt, xt, mean, None, ALU.subtract)
        self.wait(se, "sv")
        return SC("activation", xnb_out, xt, ACT.Copy, bias=0.0, scale=rstd)

    def tp_group(self, blocks, dst):
        te, se = self.nc.tensor, self.nc.scalar
        MM, SC = self.MM, self.SC
        buf = self._tp_toggle
        self._tp_toggle ^= 1
        ps = self.pst if buf == 0 else self.ppv
        if self._tp_evac_sv[buf]:
            te.wait_ge(self.sem["ss"], self._tp_evac_sv[buf])
        stc = None
        n = len(blocks)
        for kk, b in enumerate(blocks):
            stc = MM(ps[:, kk * 128:(kk + 1) * 128], b, self.I_bf,
                     True, True, inc=(kk == n - 1))
        self.wait(se, "st", stc)
        src = ps[:, 0:n * 128]
        if len(dst.shape) == 3:
            src = src.rearrange("p (a b) -> p a b", a=n)
        self._tp_evac_sv[buf] = SC("activation", dst, src, ACT.Copy)
        return self._tp_evac_sv[buf]

    def transpose_2048(self, src_bf, dst_fn):
        for kb in range(0, NKCH, 4):
            self.tp_group([src_bf[:, (kb + kk) * 128:(kb + kk + 1) * 128]
                           for kk in range(4)], dst_fn(kb))

    _tp_toggle = 0

    def _grp_dst_fn(self, par, tt):
        base = self.A.bf("xngrp", par * 8192, (par + 1) * 8192)
        r = base.rearrange("p (k c) -> p k c", c=512)

        def fn(kb):
            return r[:, kb:kb + 4, tt * 128:(tt + 1) * 128]
        return fn

    def phase2(self):
        nc, A, V, SC, G, MM = self.nc, self.A, self.V, self.SC, self.G, self.MM
        st, pst, psc0, psc1 = self.st, self.pst, self.psc0, self.psc1
        sy, te, ve, se, gp = nc.sync, nc.tensor, nc.vector, nc.scalar, nc.gpsimd
        self.ixq_dram = nc.dram_tensor("ixq_scratch", [128, 16384], BF16).ap()

        A.alloc("xnown", 16384, at=56576)
        A.alloc("ixq", 16384, at=38912)
        A.alloc("ixbf", 2048, at=26624)
        FA = self.FA
        FA.alloc("zq", 512)
        FA.alloc("rqf", 512)
        FA.alloc("rqt", 512)
        self.IA.alloc("ri", 512)
        self.wait(se, "st")

        def own_dst(name, j):
            base = A.bf(name)
            r = base.rearrange("p (k c) -> p k c", c=1024)

            def fn(kb):
                return r[:, kb:kb + 4, j * 128:(j + 1) * 128]
            return fn

        x_dma = [0] * 8
        xn_ss = [0] * 8
        cons_ss = [0] * 8
        tpx_st = [0] * 8
        tpi_st = [0] * 8

        p1t = self._p1_tail

        def issue_own(j):
            par = j % 2
            if j >= 2:
                sy.wait_ge(self.sem["ss"], cons_ss[j - 2])
                sy.wait_ge(self.sem["dxb%d" % par], x_dma[j - 2])
            else:
                sy.wait_ge(self.sem["ss"], p1t[j])
                sy.wait_ge(self.sem["dxb%d" % par], p1t[2 + j])
            x_dma[j] = self.DMA(sy, "dxb%d" % par,
                                self.FA.r("xtm", par * D, (par + 1) * D),
                                self.x_own[j * 128:(j + 1) * 128, :])

        issue_own(0)
        issue_own(1)
        for j in range(8):
            par = j % 2
            xt = self.FA.r("xtm", par * D, (par + 1) * D)
            xnb = A.bf("xnbf", par * D, (par + 1) * D)
            ve.wait_ge(self.sem["dxb%d" % par], x_dma[j])
            se.wait_ge(self.sem["dxb%d" % par], x_dma[j])
            if j >= 2:
                se.wait_ge(self.sem["st"], tpx_st[j - 2])
            xn_ss[j] = self.ln_tile(xt, xnb)
            am, am2, rr = st(self.S_AM), st(self.S_AM + 1), st(self.S_AM + 2)
            comb = st(self.S_T0)
            self.wait(ve, "ss")
            V("reduce_max", am, xt, AX, apply_absolute_value=True)
            V("tensor_mul", am, am, st(self.S_LN + 4))
            V("tensor_scalar", am2, am, EPS, None, ALU.max)
            V("reciprocal", rr, am2)
            V("tensor_scalar", st(self.S_SXI + j), am2, 1.0 / 127.0, None, ALU.mult)
            V("tensor_mul", comb, rr, st(self.S_LN + 4))
            V("tensor_scalar", comb, comb, 127.0, None, ALU.mult)
            rq_sv = 0
            if j >= 1:
                ve.wait_ge(self.sem["st"], tpi_st[j - 1])
            for ss_ in range(4):
                sl = slice(ss_ * 512, (ss_ + 1) * 512)
                self.wait(se, "sv")
                SC("activation", FA.r("zq"), xt[:, sl], ACT.Copy,
                   bias=0.0, scale=comb)
                rq_sv = self.emit_round(FA.r("zq"), self.IA.r("ri"), FA.r("rqf"),
                                        FA.r("rqt"), A.bf("ixbf")[:, sl])
            cons_ss[j] = self.cnt["ss"]
            if j + 2 < 8:
                issue_own(j + 2)
            self.wait(te, "ss", xn_ss[j])
            self.transpose_2048(xnb, own_dst("xnown", j))
            tpx_st[j] = self.cnt["st"]
            te.wait_ge(self.sem["sv"], rq_sv)
            self.transpose_2048(A.bf("ixbf"), own_dst("ixq", j))
            tpi_st[j] = self.cnt["st"]
        self.wait(gp, "ss")
        self.DMA(gp, "dwg", self.ixq_dram[:, :], A.bf("ixq"))
        ixq_spill_cnt = self.cnt["dwg"]
        A.free("ixq")
        A.free("ixbf")
        A.free("xnbf")
        A.free("junkb")
        FA.free("xtm")
        FA.free("zq")
        FA.free("rqf")
        FA.free("rqt")
        self.IA.free("ri")
        self.fence()

        if self.dbg:
            self.dbg_dump("dbg_xnown", [128, 16384], A.bf("xnown"), waits=("sv",))

        A.alloc("knope", 16384, at=72960)
        A.alloc("krope", 8192, at=43264)
        A.alloc("vaug", 16640, at=26624)
        A.alloc("wk_b", 4 * 768, at=51456)
        A.alloc("wv_b", 4 * 512, at=54528)
        gp.wait_ge(self.sem["dwg"], ixq_spill_cnt)
        for r in range(4):
            self.DMA(gp, "dwg", A.bf("wk_b", r * 768, (r + 1) * 768),
                     self.wk[r * 128:(r + 1) * 128, :])
        for r in range(4):
            self.DMA(gp, "dwg", A.bf("wv_b", r * 512, (r + 1) * 512),
                     self.wv[r * 128:(r + 1) * 128, :])
        dwg_kv = self.cnt["dwg"]
        te.wait_ge(self.sem["dwg"], dwg_kv)
        self.wait(te, "sv")

        ev_prev = None
        for m in range(6):
            for n in range(8):
                slot = ((m * 8 + n) % 2)
                ps = psc0[:, slot * 512:(slot + 1) * 512]
                if ev_prev is not None and (m * 8 + n) >= 2:
                    te.wait_ge(self.sem["sv"], ev_two_ago)
                stc = None
                for kk in range(4):
                    stc = MM(ps, A.bf("wk_b", kk * 768 + m * 128, kk * 768 + (m + 1) * 128),
                             A.bf("ckv", kk * 4096 + n * 512, kk * 4096 + (n + 1) * 512),
                             kk == 0, kk == 3, inc=(kk == 3))
                self.wait(ve, "st", stc)
                if m < 4:
                    dst = A.bf("knope", m * 4096 + n * 512, m * 4096 + (n + 1) * 512)
                else:
                    dst = A.bf("krope", (m - 4) * 4096 + n * 512, (m - 4) * 4096 + (n + 1) * 512)
                ev = V("tensor_copy", dst, ps)
                ev_two_ago = ev_prev if ev_prev is not None else ev
                ev_prev = ev
        V("memset", A.bf("vaug"), 1.0)
        vr = A.bf("vaug").rearrange("p (t g c) -> p t g c", g=4, c=130)
        for t in range(32):
            slot = t % 2
            ps = psc1[:, slot * 512:(slot + 1) * 512]
            if t >= 2:
                te.wait_ge(self.sem["sv"], ev_two_ago)
            stc = None
            for kk in range(4):
                stc = MM(ps, A.bf("ckv", kk * 4096 + t * 128, kk * 4096 + (t + 1) * 128),
                         A.bf("wv_b", kk * 512, (kk + 1) * 512),
                         kk == 0, kk == 3, inc=(kk == 3))
            self.wait(ve, "st", stc)
            ev = V("tensor_copy", vr[:, t, :, 0:128],
                   ps.rearrange("p (g c) -> p g c", g=4))
            ev_two_ago = ev_prev
            ev_prev = ev
        A.free("ckv")
        A.free("wk_b")
        A.free("wv_b")

        FA.alloc("rsc1", 512)
        FA.alloc("rsc2", 512)
        self.wait(te, "sv")
        self.wait(te, "ss")
        ve.wait_ge(self.sem["dbgo"], self.tab_cnt)
        r2rd = [0, 0]
        tgl = 0
        for gpair in range(2):
            for half in range(2):
                po = half * 64
                for n in range(8):
                    b = tgl
                    tgl ^= 1
                    ps_ = (pst if b == 0 else self.ppv)[po:po + 64, 0:512]
                    kslice = A.bf("krope", gpair * 4096 + n * 512,
                                  gpair * 4096 + (n + 1) * 512)[po:po + 64, :]
                    if r2rd[b]:
                        te.wait_ge(self.sem["sv"], r2rd[b])
                    stc = MM(ps_, self.R[po:po + 64, :], kslice,
                             True, True, inc=True)
                    cosb = A.bf("tab_cosk", n * 512, (n + 1) * 512)[po:po + 64, :]
                    sinb = A.bf("tab_sink", n * 512, (n + 1) * 512)[po:po + 64, :]
                    r1 = FA.r("rsc1")[po:po + 64, :]
                    r2 = FA.r("rsc2")[po:po + 64, :]
                    V("tensor_mul", r1, kslice, cosb)
                    self.wait(ve, "st", stc)
                    r2rd[b] = V("tensor_mul", r2, ps_, sinb)
                    V("tensor_add", kslice, r1, r2)
        A.free("tab_sink")
        A.free("tab_cosk")
        if self.dbg:
            self.dbg_dump("dbg_knope", [128, 16384], A.bf("knope"), waits=("sv",))
            self.dbg_dump("dbg_krope", [128, 8192], A.bf("krope"), waits=("sv",))
            self.dbg_dump("dbg_vaug", [128, 16640], A.bf("vaug"), waits=("sv",))

        A.alloc("qnope", 16384, at=10240)
        A.alloc("qrope", 8192, at=0)
        A.alloc("wqbuf", 2 * 2048, at=51456)
        wq_dma = [0] * 24
        q_done_st = [0] * 24

        def issue_wq(m):
            par = m % 2
            if m >= 2:
                gp.wait_ge(self.sem["st"], q_done_st[m - 2])
                gp.wait_ge(self.sem["dwq%d" % par], wq_dma[m - 2])
            else:
                self.wait(gp, "sv")
                self.wait(gp, "ss")
                self.wait(gp, "st")
            wq_dma[m] = self.DMA(gp, "dwq%d" % par,
                                 A.bf("wqbuf", par * 2048, (par + 1) * 2048),
                                 self.wq[m].rearrange("p k j -> p (k j)"))

        issue_wq(0)
        issue_wq(1)
        self.wait(te, "ss")
        self.wait(te, "sv")
        pfree = [0, 0]
        for m in range(24):
            par = m % 2
            psb = psc0 if par == 0 else psc1
            te.wait_ge(self.sem["dwq%d" % par], wq_dma[m])
            if pfree[par]:
                te.wait_ge(self.sem["ss"], pfree[par])
            stc = None
            for n in range(2):
                ps = psb[:, n * 512:(n + 1) * 512]
                for kk in range(NKCH):
                    stc = MM(ps, A.bf("wqbuf", par * 2048 + kk * 128, par * 2048 + (kk + 1) * 128),
                             A.bf("xnown", kk * 1024 + n * 512, kk * 1024 + (n + 1) * 512),
                             kk == 0, kk == NKCH - 1, inc=(kk == NKCH - 1 and n == 1))
            q_done_st[m] = self.cnt["st"]
            if m + 2 < 24:
                issue_wq(m + 2)
            self.wait(se, "st", q_done_st[m])
            if m < 16:
                pfree[par] = SC("activation", A.bf("qnope", m * 1024, (m + 1) * 1024),
                                psb[:, 0:1024], ACT.Copy, bias=0.0, scale=SCL)
            else:
                p = m - 16
                if m == 16:
                    self.wait(se, "sv")
                pfree[par] = SC("activation", A.bf("qrope", p * 1024, (p + 1) * 1024),
                                psb[:, 0:1024], ACT.Copy, bias=0.0, scale=SCL)
        self.wait(te, "ss")
        self.wait(te, "sv")
        self.wait(ve, "ss")
        r2rd = [0, 0]
        tgl = 0
        for p in range(8):
            for half in range(2):
                po = half * 64
                for n in range(2):
                    b = tgl
                    tgl ^= 1
                    ps_ = (pst if b == 0 else self.ppv)[po:po + 64, 0:512]
                    qsl = A.bf("qrope", p * 1024 + n * 512,
                               p * 1024 + (n + 1) * 512)[po:po + 64, :]
                    if r2rd[b]:
                        te.wait_ge(self.sem["sv"], r2rd[b])
                    stc2 = MM(ps_, self.R[po:po + 64, :], qsl, True, True, inc=True)
                    cosb = A.bf("tab_cosq", n * 512, (n + 1) * 512)[po:po + 64, :]
                    sinb = A.bf("tab_sinq", n * 512, (n + 1) * 512)[po:po + 64, :]
                    r1 = FA.r("rsc1")[po:po + 64, :]
                    r2 = FA.r("rsc2")[po:po + 64, :]
                    V("tensor_mul", r1, qsl, cosb)
                    self.wait(ve, "st", stc2)
                    r2rd[b] = V("tensor_mul", r2, ps_, sinb)
                    V("tensor_add", qsl, r1, r2)
        A.free("xnown")
        A.free("wqbuf")
        A.free("tab_sinq")
        A.free("tab_cosq")
        FA.free("rsc1")
        FA.free("rsc2")

        self.wait(ve, "ss")
        for h in range(16):
            V("reduce_max", st(self.S_QM + h), A.bf("qnope", h * 1024, (h + 1) * 1024),
              AX, apply_absolute_value=True)
        for p in range(8):
            V("reduce_max", st(self.S_QMR + p), A.bf("qrope", p * 1024, (p + 1) * 1024),
              AX, apply_absolute_value=True)
        for g in range(4):
            V("reduce_max", st(self.S_KM + g), A.bf("knope", g * 4096, (g + 1) * 4096),
              AX, apply_absolute_value=True)
        for gpair in range(2):
            V("reduce_max", st(self.S_KMR + gpair),
              A.bf("krope", gpair * 4096, (gpair + 1) * 4096),
              AX, apply_absolute_value=True)
        for h in range(16):
            V("tensor_mul", st(self.S_PR + h), st(self.S_QM + h), st(self.S_KM + h // 4))
        for h in range(16):
            p, po = QR_PAIR[h]
            rp = st(self.S_T0)
            V("tensor_mul", rp, st(self.S_QMR + p), st(self.S_KMR + p // 4))
            V("tensor_mul", rp, rp, st(self.S_HM if po == 0 else self.S_UHM))
            V("tensor_add", st(self.S_PR + h), st(self.S_PR + h), rp)
        self.wait(te, "sv")
        stc = MM(self.pst[0:1, 0:16], st(self.S_ONE), self.stats[:, self.S_PR:self.S_PR + 16],
                 True, True, inc=True)
        self.wait(ve, "st", stc)
        V("tensor_scalar", self.stats[0:1, self.S_PR:self.S_PR + 16],
          self.pst[0:1, 0:16], -1.0, None, ALU.mult)
        self.wait(te, "sv")
        stc = MM(self.pst[:, 0:16], self.stats[0:1, self.S_ONESROW:self.S_ONESROW + 128],
                 self.stats[0:1, self.S_PR:self.S_PR + 16], True, True, inc=True)
        self.wait(ve, "st", stc)
        V("tensor_copy", self.stats[:, self.S_NB:self.S_NB + 16], self.pst[:, 0:16])
        if self.dbg:
            self.dbg_dump("dbg_q", [128, 16384], A.bf("qnope"), waits=("ss",))
            self.dbg_dump("dbg_qr", [128, 8192], A.bf("qrope"), waits=("sv",))
            nb = self.dbg_out("dbg_nb", [128, 16])
            self.wait(sy, "sv")
            self.DMA(sy, "dout", nb[:, :], self.stats[:, self.S_NB:self.S_NB + 16])

    def phase3(self):
        nc, A, V, SC, G, MM = self.nc, self.A, self.V, self.SC, self.G, self.MM
        st, pst = self.st, self.pst
        sy, te, ve, se, gp = nc.sync, nc.tensor, nc.vector, nc.scalar, nc.gpsimd
        A.alloc("ppan", 16896, at=51456)
        A.alloc("atst", 256, at=68352)
        self.fence()
        dat_base = (self.cnt["dho0"], self.cnt["dho1"])
        dat_cnt = {u: dat_base[u % 2] + 16 * (u // 2 + 1) for u in range(128)}

        self.wait(te, "dmeta")
        self.wait(se, "dmeta")
        self.wait(ve, "dmeta")
        rt = te.alloc_register("rt")
        rs = se.alloc_register("rs")
        rv = ve.alloc_register("rv")
        te.reg_load(rt, self.meta_sb[0:1, 0:1])
        se.reg_load(rs, self.meta_sb[0:1, 0:1])
        ve.reg_load(rv, self.meta_sb[0:1, 0:1])

        FA = self.FA
        FA.alloc("wmba", 2 * 1024)
        FA.alloc("wmbb", 2 * 1024)
        FA.alloc("wmbc", 2 * 1408)
        V("memset", self.stats[:, self.S_AW:self.S_AW + 3], 0.0)
        wstr = [
            dict(dram=self.w1, n=88, parts=2, w=1024, buf="wmba",
                 sems=("dwsy0", "dwsy1"), acc=self.S_AW, q="sy"),
            dict(dram=self.w2, n=88, parts=2, w=1024, buf="wmbb",
                 sems=("dw3a", "dw3b"), acc=self.S_AW + 1, q="gp"),
            dict(dram=self.w3, n=64, parts=4, w=1408, buf="wmbc",
                 sems=("dws0", "dws1"), acc=self.S_AW + 2, q="sy"),
        ]
        for s_ in wstr:
            b0, b1 = self.cnt[s_["sems"][0]], self.cnt[s_["sems"][1]]
            s_["dpre"] = [(b0 if i % 2 == 0 else b1) + 16 * (i // 2 + 1)
                          for i in range(s_["n"])]

        sched = []
        for h in range(16):
            blk = []
            for k, s_ in enumerate(wstr):
                blk += [(k, i) for i in range((h * s_["n"]) // 16,
                                              ((h + 1) * s_["n"]) // 16)]
            sched.append(blk)
        nch = [len(x) for x in sched]
        pnch = [sum(nch[:h]) for h in range(16)]

        base = dict(self.cnt)
        red_pre = {}
        _cv = base["sv"]
        for h in range(16):
            for ch in sched[h]:
                _cv += 2
                red_pre[ch] = _cv
            _cv += 24

        def plan(c):
            tiles = owned_tiles(c)
            NKT = tiles[-1] + 1
            sj = [next(j for j in range(8) if tiles[j] >= kt) for kt in range(NKT)]
            poff = [0] * NKT
            acc = 0
            for kt in range(NKT):
                poff[kt] = acc
                acc += (8 - sj[kt]) * 128
            p = dict(tiles=tiles, NKT=NKT, sj=sj, poff=poff,
                     scores_st={}, pvj_st={}, exp_ss={}, mask_sv={}, evj_sv={})
            cst, css, csv = base["st"], base["ss"], base["sv"]
            for h in range(16):
                csv += 2 * nch[h]
                for kt in range(NKT):
                    cst += 1
                    p["scores_st"][(h, kt)] = cst
                    css += 1
                    p["exp_ss"][(h, kt)] = css
                    if kt in tiles:
                        csv += 1
                        p["mask_sv"][(h, kt)] = csv
                for j in range(8):
                    cst += 1
                    p["pvj_st"][(h, j)] = cst
                    csv += 2
                    p["evj_sv"][(h, j)] = csv
            p["end"] = dict(st=cst, ss=css, sv=csv)
            return p

        plans = [plan(c) for c in range(4)]
        ceil = dict(st=max(p["end"]["st"] for p in plans),
                    ss=max(p["end"]["ss"] for p in plans),
                    sv=max(p["end"]["sv"] for p in plans))

        def q_rhs(h, col0, w, which):
            if which == "n":
                return A.bf("qnope", h * 1024 + col0, h * 1024 + col0 + w)
            pq, po = QR_PAIR[h]
            return A.bf("qrope", pq * 1024 + col0, pq * 1024 + col0 + w)[po:po + 64, :]

        for c in range(4):
            p = plans[c]
            tiles, NKT, sj, poff = p["tiles"], p["NKT"], p["sj"], p["poff"]

            self.cnt = dict(base)
            with te.If_eq(rt, c):
                for h in range(16):
                    g = h // 4
                    goff = 64 * (g % 2)
                    for kt in range(NKT):
                        u = h * NKT + kt
                        buf = u % 2
                        psb = (self.psc0 if buf == 0 else self.psc1)
                        j0 = sj[kt]
                        col0, w = j0 * 128, (8 - j0) * 128
                        if u >= 2:
                            te.wait_ge(self.sem["ss"],
                                       p["exp_ss"][(h, kt - 2) if kt >= 2 else (h - 1, NKT - (2 - kt))])
                        kn = A.bf("knope", g * 4096 + kt * 128, g * 4096 + (kt + 1) * 128)
                        kr = A.bf("krope", (g // 2) * 4096 + kt * 128,
                                  (g // 2) * 4096 + (kt + 1) * 128)[goff:goff + 64, :]
                        nchunk = 1 if w <= 512 else 2
                        for ci in range(nchunk):
                            off = ci * 512
                            nw = min(512, w - off)
                            MM(psb[:, off:off + nw], kn,
                               q_rhs(h, col0 + off, nw, "n"), True, False)
                            MM(psb[:, off:off + nw], kr,
                               q_rhs(h, col0 + off, nw, "r"), False, True,
                               inc=(ci == nchunk - 1))
                        assert self.cnt["st"] == p["scores_st"][(h, kt)]
                    for j in range(8):
                        te.wait_ge(self.sem["ss"], p["exp_ss"][(h, tiles[j])])
                        if j in tiles or True:
                            te.wait_ge(self.sem["sv"], p["mask_sv"][(h, tiles[j])])
                        if j >= 3:
                            te.wait_ge(self.sem["sv"], p["evj_sv"][(h, j - 3)])
                        elif h > 0:
                            te.wait_ge(self.sem["sv"],
                                       p["evj_sv"][(h - 1, {0: 6, 1: 7, 2: 5}[j])])
                        bank = self.ppv[:, (j % 3) * 512:(j % 3) * 512 + 129]
                        for kt in range(tiles[j] + 1):
                            MM(bank,
                               A.bf("ppan", poff[kt] + (j - sj[kt]) * 128,
                                    poff[kt] + (j - sj[kt] + 1) * 128),
                               A.bf("vaug", kt * 520 + g * 130,
                                    kt * 520 + g * 130 + 129),
                               start=(kt == 0), stop=(kt == tiles[j]),
                               inc=(kt == tiles[j]))
                        assert self.cnt["st"] == p["pvj_st"][(h, j)]
                te.wait_ge(self.sem["st"], self.cnt["st"])
                for _ in range(ceil["st"] - self.cnt["st"]):
                    te.nop(nofuse=True).then_inc(self.sem["st"], 1)
                te.nop(nofuse=True).then_inc(self.sem["sp3"], 1)


        te.end_ifs()
        te.nop(nofuse=True)
        for c in range(4):
            p = plans[c]
            tiles, NKT, sj, poff = p["tiles"], p["NKT"], p["sj"], p["poff"]

            self.cnt = dict(base)
            with se.If_eq(rs, c):
                for h in range(16):
                    if h > 0:
                        se.wait_ge(self.sem["st"], p["pvj_st"][(h - 1, 7)])
                    for kt in range(NKT):
                        buf = (h * NKT + kt) % 2
                        psb = (self.psc0 if buf == 0 else self.psc1)
                        j0 = sj[kt]
                        w = (8 - j0) * 128
                        se.wait_ge(self.sem["st"], p["scores_st"][(h, kt)])
                        SC("activation", A.bf("ppan", poff[kt], poff[kt] + w),
                           psb[:, 0:w], ACT.Exp,
                           bias=self.stats[:, self.S_NB + h:self.S_NB + h + 1],
                           scale=1.0)
                        assert self.cnt["ss"] == p["exp_ss"][(h, kt)]
                se.wait_ge(self.sem["ss"], self.cnt["ss"])
                for _ in range(ceil["ss"] - self.cnt["ss"]):
                    se.nop(nofuse=True).then_inc(self.sem["ss"], 1)
                se.nop(nofuse=True).then_inc(self.sem["sp3"], 1)


        se.end_ifs()
        se.nop(nofuse=True)
        for c in range(4):
            p = plans[c]
            tiles, NKT, sj, poff = p["tiles"], p["NKT"], p["sj"], p["poff"]

            self.cnt = dict(base)
            with ve.If_eq(rv, c):
                for h in range(16):
                    for ch in sched[h]:
                        s_ = wstr[ch[0]]
                        i = ch[1]
                        wpar = i % 2
                        ve.wait_ge(self.sem[s_["sems"][wpar]], s_["dpre"][i])
                        V("reduce_sum", st(self.S_T0 + 3),
                          self.FA.r(s_["buf"], wpar * s_["w"], (wpar + 1) * s_["w"]),
                          AX, apply_absolute_value=True)
                        got = V("tensor_add", st(s_["acc"]), st(s_["acc"]),
                                st(self.S_T0 + 3))
                        assert got == red_pre[ch]
                    for kt in range(NKT):
                        if kt not in tiles:
                            continue
                        ve.wait_ge(self.sem["ss"], p["exp_ss"][(h, kt)])
                        blk = A.bf("ppan", poff[kt], poff[kt] + 128)
                        V("tensor_mul", blk, blk, self.TRI)
                        assert self.cnt["sv"] == p["mask_sv"][(h, kt)]
                    for j in range(8):
                        u = h * 8 + j
                        par = u % 2
                        ve.wait_ge(self.sem["st"], p["pvj_st"][(h, j)])
                        if u >= 2:
                            ve.wait_ge(self.sem["dho%d" % par], dat_cnt[u - 2])
                        bank = self.ppv[:, (j % 3) * 512:(j % 3) * 512 + 129]
                        V("reciprocal", st(self.S_LI), bank[:, 128:129])
                        V("tensor_scalar",
                          A.bf("atst", par * 128, (par + 1) * 128),
                          bank[:, 0:128], st(self.S_LI), None, ALU.mult)
                        assert self.cnt["sv"] == p["evj_sv"][(h, j)]
                ve.wait_ge(self.sem["sv"], self.cnt["sv"])
                for _ in range(ceil["sv"] - self.cnt["sv"]):
                    ve.nop(nofuse=True).then_inc(self.sem["sv"], 1)
                ve.nop(nofuse=True).then_inc(self.sem["sp3"], 1)

        ve.end_ifs()
        ve.nop(nofuse=True)

        ev_uniform = lambda h, j: (base["sv"] + 24 * h + 2 * (pnch[h] + nch[h])
                                   + 8 + 2 * (j + 1))
        for p in plans:
            for h in range(16):
                for j in range(8):
                    assert p["evj_sv"][(h, j)] == ev_uniform(h, j)

        def emit_issues(hh):
            if hh >= 16:
                return
            for ch in sched[hh]:
                s_ = wstr[ch[0]]
                i = ch[1]
                wpar = i % 2
                q = sy if s_["q"] == "sy" else gp
                if i >= 2:
                    q.wait_ge(self.sem["sv"], red_pre[(ch[0], i - 2)])
                    q.wait_ge(self.sem[s_["sems"][wpar]], s_["dpre"][i - 2])
                else:
                    q.wait_ge(self.sem["sv"], base["sv"])
                    q.wait_ge(self.sem["ss"], base["ss"])
                p_, w_ = s_["parts"], s_["w"]
                src = s_["dram"][(i // p_) * 128:(i // p_ + 1) * 128,
                                 (i % p_) * w_:(i % p_ + 1) * w_]
                got = self.DMA(q, s_["sems"][wpar],
                               self.FA.r(s_["buf"], wpar * w_, (wpar + 1) * w_),
                               src)
                assert got == s_["dpre"][i]

        emit_issues(0)
        for u in range(128):
            h, j = divmod(u, 8)
            par = u % 2
            gp.wait_ge(self.sem["sv"], ev_uniform(h, j))
            got = self.DMA(gp, "dho%d" % par,
                           self.attn_dram[j * 128:(j + 1) * 128,
                                          h * 128:(h + 1) * 128],
                           A.bf("atst", par * 128, (par + 1) * 128))
            assert got == dat_cnt[u], (u, got, dat_cnt[u])
            if j == 7:
                emit_issues(h + 1)

        self.cnt.update(ceil)
        self.cnt["sp3"] = 3
        for e in (te, se, ve, sy, gp):
            e.wait_ge(self.sem["sp3"], 3)
        te.nop()
        se.nop()
        ve.nop()
        A.free("ppan")
        A.free("atst")
        A.free("knope")
        A.free("krope")
        A.free("vaug")
        A.free("qnope")
        A.free("qrope")

    def phase4(self):
        nc, A, V, SC, G, MM = self.nc, self.A, self.V, self.SC, self.G, self.MM
        st, pst, psc0, psc1 = self.st, self.pst, self.psc0, self.psc1
        sy, te, ve, se, gp = nc.sync, nc.tensor, nc.vector, nc.scalar, nc.gpsimd
        A.alloc("attn_fm", 16384, at=16384)
        A.alloc("atld", 4096, at=49152)
        self.fence()
        self.wait(sy, "dho0")
        self.wait(sy, "dho1")
        ar_dma = [0] * 8
        tr_done_st = [0] * 8

        def issue_ar(j):
            par = j % 2
            if j >= 2:
                sy.wait_ge(self.sem["st"], tr_done_st[j - 2])
                sy.wait_ge(self.sem["dxb%d" % par], ar_dma[j - 2])
            ar_dma[j] = self.DMA(sy, "dxb%d" % par,
                                 A.bf("atld", par * 2048, (par + 1) * 2048),
                                 self.attn_dram[j * 128:(j + 1) * 128, :])

        issue_ar(0)
        issue_ar(1)
        fmr = A.bf("attn_fm").rearrange("p (h c) -> p h c", c=1024)
        for j in range(8):
            par = j % 2
            te.wait_ge(self.sem["dxb%d" % par], ar_dma[j])
            for hg in range(4):
                self.tp_group(
                    [A.bf("atld", par * 2048 + (hg * 4 + hh) * 128,
                          par * 2048 + (hg * 4 + hh + 1) * 128) for hh in range(4)],
                    fmr[:, hg * 4:hg * 4 + 4, j * 128:(j + 1) * 128])
            tr_done_st[j] = self.cnt["st"]
            if j + 2 < 8:
                issue_ar(j + 2)
        A.alloc("attn_out", 16384, at=0)
        A.alloc("wo_bf", 16384, at=32768)

        wo_dma = [0] * 4
        wo_done_st = [0] * 4
        ev_sv = 0
        ev_prev2 = 0

        def issue_wo(og):
            par = og % 2
            if og >= 2:
                gp.wait_ge(self.sem["st"], wo_done_st[og - 2])
                gp.wait_ge(self.sem["dwq%d" % par], wo_dma[og - 2])
            else:
                self.wait(gp, "sv")
                self.wait(gp, "ss")
            for k in range(NKCH):
                self.DMA(gp, "dwq%d" % par,
                         A.bf("wo_bf", par * 8192 + k * 512, par * 8192 + (k + 1) * 512),
                         self.wo[k * 128:(k + 1) * 128, og * 512:(og + 1) * 512])
            wo_dma[og] = self.cnt["dwq%d" % par]

        issue_wo(0)
        issue_wo(1)
        for og in range(4):
            par = og % 2
            te.wait_ge(self.sem["dwq%d" % par], wo_dma[og])
            for tt in range(8):
                slot = tt % 2
                ps = psc0[:, slot * 512:(slot + 1) * 512]
                if ev_prev2:
                    te.wait_ge(self.sem["sv"], ev_prev2)
                stc = None
                for k in range(NKCH):
                    stc = MM(ps, A.bf("attn_fm", k * 1024 + tt * 128, k * 1024 + (tt + 1) * 128),
                             A.bf("wo_bf", par * 8192 + k * 512, par * 8192 + (k + 1) * 512),
                             k == 0, k == NKCH - 1, inc=(k == NKCH - 1))
                self.wait(ve, "st", stc)
                ev_prev2 = ev_sv
                ev_sv = V("tensor_copy",
                          A.bf("attn_out", tt * 2048 + og * 512, tt * 2048 + (og + 1) * 512),
                          ps)
            wo_done_st[og] = self.cnt["st"]
            if og + 2 < 4:
                issue_wo(og + 2)
        A.free("attn_fm")
        A.free("wo_bf")
        A.free("atld")
        if self.dbg:
            self.dbg_dump("dbg_attnout", [128, 16384], A.bf("attn_out"), waits=("sv",))

    def phase5(self):
        nc, A, V, SC, G, MM = self.nc, self.A, self.V, self.SC, self.G, self.MM
        st, pst = self.st, self.pst
        sy, te, ve, se, gp = nc.sync, nc.tensor, nc.vector, nc.scalar, nc.gpsimd
        FA = self.FA
        self.wait(te, "sv")
        self.wait(te, "ss")
        stc = MM(pst[0:1, 0:3], st(self.S_ONE), self.stats[:, self.S_AW:self.S_AW + 3],
                 True, True, inc=True)
        self.wait(ve, "st", stc)
        row = self.stats[0:1, self.S_PR:self.S_PR + 16]
        V("tensor_scalar", row[:, 0:3], pst[0:1, 0:3], 1.0 / (HF * D), None, ALU.mult)
        V("tensor_scalar", row[:, 0:3], row[:, 0:3], 1e-5, None, ALU.max)
        V("tensor_scalar", row[:, 3:6], row[:, 0:3], 0.5, None, ALU.mult)
        V("tensor_scalar", row[:, 6:9], row[:, 3:6], -1.0, None, ALU.mult)
        self.wait(te, "sv")
        stc = MM(pst[:, 0:9], self.stats[0:1, self.S_ONESROW:self.S_ONESROW + 128],
                 row[:, 0:9], True, True, inc=True)
        self.wait(ve, "st", stc)
        V("tensor_copy", self.stats[:, self.S_SW:self.S_SW + 9], pst[:, 0:9])
        for j in range(8):
            V("tensor_mul", st(self.S_C1 + j), st(self.S_SXI + j), st(self.S_SW))
            V("tensor_mul", st(self.S_C2 + j), st(self.S_SXI + j), st(self.S_SW + 1))
        FA.free("wmba")
        FA.free("wmbb")
        FA.free("wmbc")

    def phase6(self):
        nc, A, V, SC, G, MM = self.nc, self.A, self.V, self.SC, self.G, self.MM
        st, pst, psc0, psc1 = self.st, self.pst, self.psc0, self.psc1
        sy, te, ve, se, gp = nc.sync, nc.tensor, nc.vector, nc.scalar, nc.gpsimd
        A.alloc("iw1T", 16384, at=16384)
        A.alloc("iw2T", 16384, at=32768)
        A.alloc("ixq", 16384, at=49152)
        A.alloc("wqz", 4096, at=65536)
        A.alloc("wcmpb", 1024, at=69632)
        FA = self.FA
        FA.alloc("wch1", 3 * 1024)
        FA.alloc("wch2", 2 * 1024)
        FA.alloc("ma2", 512)
        FA.alloc("mh0", 512)
        FA.alloc("mh1", 512)
        self.fence()
        self.DMA(gp, "dwg", A.bf("ixq"), self.ixq_dram[:, :])
        ixq_cnt = self.cnt["dwg"]
        te.wait_ge(self.sem["dwg"], ixq_cnt)
        V("memset", self.stats[:, self.S_HAM:self.S_HAM + 8], 0.0)

        wsrc = {1: self.w1, 2: self.w2}
        wchn = {1: "wch1", 2: "wch2"}
        wq_eng = {1: sy, 2: se}
        wq_sem = {1: ("dwsy0", "dwsy1"), 2: ("dws0", "dws1")}
        iwt = {1: "iw1T", 2: "iw2T"}
        thr_col = {1: self.S_SW + 3, 2: self.S_SW + 4}
        nthr_col = {1: self.S_SW + 6, 2: self.S_SW + 7}
        wdma = {1: [0] * 2 * NW1CH, 2: [0] * 2 * NW1CH}
        wcons_sv = {1: [0] * 2 * NW1CH, 2: [0] * 2 * NW1CH}
        wtrans_sv = {}
        qz_last_st = [0, 0]
        mm_done_st = {}

        nslot = {1: 3, 2: 2}

        def issue_w(w, mm):
            slot = mm % nslot[w]
            par = mm % 2
            e = wq_eng[w]
            ns = nslot[w]
            if mm >= ns:
                e.wait_ge(self.sem["sv"], wcons_sv[w][mm - ns])
            if mm >= 2:
                e.wait_ge(self.sem[wq_sem[w][par]], wdma[w][mm - 2])
            if mm < ns:
                self.wait(e, "sv")
                self.wait(e, wq_sem[w][par])
            m, hf = divmod(mm, 2)
            wdma[w][mm] = self.DMA(e, wq_sem[w][par],
                                   FA.r(wchn[w], slot * 1024, (slot + 1) * 1024),
                                   wsrc[w][m * 128:(m + 1) * 128,
                                           hf * 1024:(hf + 1) * 1024])

        for w in (1, 2):
            for _i in range(nslot[w]):
                issue_w(w, _i)

        h_spill = {}
        for og in range(11):
            iwpar = og % 2
            for w in (1, 2):
                for rc in range(4):
                    m = og * 4 + rc
                    qz = A.bf("wqz", (m % 2) * 2048, (m % 2 + 1) * 2048)
                    if qz_last_st[m % 2]:
                        ve.wait_ge(self.sem["st"], qz_last_st[m % 2])
                    for hf in range(2):
                        mm = 2 * m + hf
                        slot = mm % nslot[w]
                        par = mm % 2
                        ve.wait_ge(self.sem[wq_sem[w][par]], wdma[w][mm])
                        wc = FA.r(wchn[w], slot * 1024, (slot + 1) * 1024)
                        V("tensor_scalar", A.bf("wcmpb"), wc,
                          self.stats[:, nthr_col[w]:nthr_col[w] + 1], None, ALU.is_lt)
                        wcons_sv[w][mm] = V(
                            "scalar_tensor_tensor", qz[:, hf * 1024:(hf + 1) * 1024],
                            wc, self.stats[:, thr_col[w]:thr_col[w] + 1], A.bf("wcmpb"),
                            ALU.is_gt, ALU.subtract)
                        if mm + nslot[w] < 2 * NW1CH:
                            issue_w(w, mm + nslot[w])
                    dstr = A.bf(iwt[w], iwpar * 8192, (iwpar + 1) * 8192) \
                        .rearrange("p (k c) -> p k c", c=512)
                    te.wait_ge(self.sem["sv"], wcons_sv[w][2 * m + 1])
                    for kb in range(0, NKCH, 4):
                        if og >= 2 and (w, og - 2) in mm_done_st:
                            te.wait_ge(self.sem["st"], mm_done_st[(w, og - 2)])
                        wtrans_sv[(w, m)] = self.tp_group(
                            [qz[:, (kb + kk) * 128:(kb + kk + 1) * 128]
                             for kk in range(4)],
                            dstr[:, kb:kb + 4, rc * 128:(rc + 1) * 128])
                    qz_last_st[m % 2] = self.cnt["st"]
            for tt in range(8):
                slot = tt % 2
                for w, psb in ((1, psc0), (2, psc1)):
                    ps = psb[:, slot * 512:(slot + 1) * 512]
                    te.wait_ge(self.sem["ss"], wtrans_sv[(w, og * 4 + 3)])
                    if (w, og, tt, "evac") in h_spill:
                        pass
                    if tt >= 2:
                        te.wait_ge(self.sem["ss" if w == 1 else "sv"],
                                   h_spill[(w, og, tt - 2)])
                    stc = None
                    for k in range(NKCH):
                        stc = MM(ps, A.bf("ixq", k * 1024 + tt * 128, k * 1024 + (tt + 1) * 128),
                                 A.bf(iwt[w], iwpar * 8192 + k * 512, iwpar * 8192 + (k + 1) * 512),
                                 k == 0, k == NKCH - 1, inc=(k == NKCH - 1))
                    mm_done_st[(w, og, tt)] = self.cnt["st"]
                mm_done_st[(1, og)] = mm_done_st[(1, og, tt)]
                mm_done_st[(2, og)] = mm_done_st[(2, og, tt)]
                self.wait(se, "st", mm_done_st[(1, og, tt)])
                self.wait(se, "sv")
                h_spill[(1, og, tt)] = SC(
                    "activation", self.ppv[:, 512:1024], psc0[:, slot * 512:(slot + 1) * 512],
                    ACT.Silu, bias=0.0,
                    scale=self.stats[:, self.S_C1 + tt:self.S_C1 + tt + 1])
                self.wait(ve, "st", mm_done_st[(2, og, tt)])
                h_spill[(2, og, tt)] = V(
                    "tensor_scalar", FA.r("ma2"), psc1[:, slot * 512:(slot + 1) * 512],
                    self.stats[:, self.S_C2 + tt:self.S_C2 + tt + 1], None, ALU.mult)
                gi = og * 8 + tt
                mhp = gi % 2
                mh = FA.r("mh0" if mhp == 0 else "mh1")
                if gi >= 2:
                    po, ptt = divmod(gi - 2, 8)
                    ve.wait_ge(self.sem["dho%d" % mhp], h_spill.get(("dma", po, ptt), 0))
                self.wait(ve, "ss")
                V("tensor_mul", mh, FA.r("ma2"), self.ppv[:, 512:1024])
                V("reduce_max", st(self.S_T0), mh, AX, apply_absolute_value=True)
                hs = V("tensor_max", st(self.S_HAM + tt), st(self.S_HAM + tt), st(self.S_T0))
                gp.wait_ge(self.sem["sv"], hs)
                h_spill[("dma", og, tt)] = self.DMA(
                    gp, "dho%d" % mhp, self.h_dram[tt * 128:(tt + 1) * 128,
                                                   og * 512:(og + 1) * 512], mh)
        self.wait(ve, "sg")
        for j in range(8):
            V("tensor_scalar", st(self.S_T0), st(self.S_HAM + j), 1e-5, None, ALU.max)
            V("reciprocal", st(self.S_T0 + 1), st(self.S_T0))
            V("tensor_scalar", st(self.S_SXH + j), st(self.S_T0 + 1), 127.0, None, ALU.mult)
            V("tensor_scalar", st(self.S_T0), st(self.S_T0), 1.0 / 127.0, None, ALU.mult)
            V("tensor_mul", st(self.S_S3 + j), st(self.S_T0), st(self.S_SW + 2))
        for nm in ("iw1T", "iw2T", "wqz", "ixq"):
            A.free(nm)
        A.free("wcmpb")
        for nm in ("wch1", "wch2", "ma2", "mh0", "mh1"):
            FA.free(nm)

    def phase7(self):
        nc, A, V, SC, G, MM = self.nc, self.A, self.V, self.SC, self.G, self.MM
        st, pst, psc0 = self.st, self.pst, self.psc0
        sy, te, ve, se, gp = nc.sync, nc.tensor, nc.vector, nc.scalar, nc.gpsimd
        A.alloc("ihq", 2048, at=83968)
        A.alloc("wqz3", 1024, at=86016)
        A.alloc("ih", 45056, at=16384)
        A.alloc("w3T", 22528, at=61440)
        FA = self.FA
        FA.alloc("hrl", 4 * 512)
        FA.alloc("z7p", 4 * 512)
        FA.alloc("f7p", 4 * 512)
        FA.alloc("t7p", 512)
        self.IA.alloc("i7", 512)
        self.fence()
        for e in (se, sy, ve, te, gp):
            self.wait(e, "dho0")
            self.wait(e, "dho1")
        self.wait(sy, "dwsy0")
        self.wait(sy, "dwsy1")
        self.wait(sy, "dxb0")
        self.wait(sy, "dxb1")
        self.wait(se, "dhi0")
        self.wait(se, "dhi1")

        hdma = {}
        tr_sv = {}
        rnd_sv = {}
        ihr = A.bf("ih").rearrange("p (k c) -> p k c", c=1024)

        def issue_h(og, tt):
            i = og * 8 + tt
            par = i % 2
            q4 = i % 4
            if i >= 2:
                se.wait_ge(self.sem["dhi%d" % par], hdma.get(i - 2, 0))
            if i >= 4:
                se.wait_ge(self.sem["ss"], hdma.get(("cons", i - 4), 0))
            hdma[i] = self.DMA(se, "dhi%d" % par,
                               FA.r("hrl", q4 * 512, (q4 + 1) * 512),
                               self.h_dram[tt * 128:(tt + 1) * 128,
                                           og * 512:(og + 1) * 512])

        for _w in range(4):
            issue_h(_w // 8, _w % 8)
        for og in range(11):
            for tt in range(8):
                i = og * 8 + tt
                par = i % 2
                q4 = i % 4
                se.wait_ge(self.sem["dhi%d" % par], hdma[i])
                if i >= 4:
                    se.wait_ge(self.sem["sv"], rnd_sv[i - 4])
                z7 = FA.r("z7p", q4 * 512, (q4 + 1) * 512)
                f7 = FA.r("f7p", q4 * 512, (q4 + 1) * 512)
                t7 = FA.r("t7p")
                hdma[("cons", i)] = SC(
                    "activation", z7, FA.r("hrl", q4 * 512, (q4 + 1) * 512),
                    ACT.Copy, bias=0.0,
                    scale=self.stats[:, self.S_SXH + tt:self.S_SXH + tt + 1])
                SC("activation", self.IA.r("i7"), z7, ACT.Copy)
                f7ss = SC("activation", f7, self.IA.r("i7"), ACT.Copy)
                if i >= 4:
                    ve.wait_ge(self.sem["st"], tr_sv[("st", i - 4)])
                ve.wait_ge(self.sem["ss"], f7ss)
                ihq = A.bf("ihq", q4 * 512, (q4 + 1) * 512)
                V("scalar_tensor_tensor", t7, z7, -0.5, f7, ALU.add, ALU.is_gt)
                V("scalar_tensor_tensor", z7, z7, 0.5, f7, ALU.add, ALU.is_lt)
                V("tensor_sub", t7, t7, z7)
                rnd_sv[i] = V("tensor_add", ihq, f7, t7)
                if i + 4 < 88:
                    issue_h((i + 4) // 8, (i + 4) % 8)
                te.wait_ge(self.sem["sv"], rnd_sv[i])
                tr_sv[i] = self.tp_group(
                    [ihq[:, kk * 128:(kk + 1) * 128] for kk in range(4)],
                    ihr[:, og * 4:og * 4 + 4, tt * 128:(tt + 1) * 128])
                tr_sv[("st", i)] = self.cnt["st"]
        FA.free("hrl")
        FA.free("z7p")
        FA.free("f7p")
        FA.free("t7p")
        self.IA.free("i7")

        FA.alloc("wch3", 4 * 512)
        FA.alloc("w3s", 512)
        FA.alloc("xsl", 2 * 512)
        FA.alloc("f1", 2 * 512)
        FA.alloc("obuf", 2 * 512)
        w3dma = {}
        w3q_sv = {}
        w3t_st = {}
        out_dma = {}
        w3r = A.bf("w3T").rearrange("p (k c) -> p k c", c=512)
        for og3 in range(4):
            for mr in range(4):
                m = og3 * 4 + mr
                for sl in range(11):
                    s = m * 11 + sl
                    par = s % 2
                    q4 = s % 4
                    if s >= 2:
                        sy.wait_ge(self.sem["dwsy%d" % par], w3dma.get(s - 2, 0))
                    if s >= 4:
                        sy.wait_ge(self.sem["sv"], w3q_sv.get(s - 4, 0))
                    elif s < 2:
                        self.wait(sy, "ss")
                        self.wait(sy, "sv")
                    w3dma[s] = self.DMA(
                        sy, "dwsy%d" % par, FA.r("wch3", q4 * 512, (q4 + 1) * 512),
                        self.w3[m * 128:(m + 1) * 128, sl * 512:(sl + 1) * 512])
                    ve.wait_ge(self.sem["dwsy%d" % par], w3dma[s])
                    wc = FA.r("wch3", q4 * 512, (q4 + 1) * 512)
                    qz = A.bf("wqz3", par * 512, (par + 1) * 512)
                    if s >= 2:
                        ve.wait_ge(self.sem["st"], w3t_st.get(s - 2, 0))
                    V("tensor_scalar", FA.r("w3s"), wc,
                      self.stats[:, self.S_SW + 8:self.S_SW + 9], None, ALU.is_lt)
                    w3q_sv[s] = V(
                        "scalar_tensor_tensor", qz, wc,
                        self.stats[:, self.S_SW + 5:self.S_SW + 6], FA.r("w3s"),
                        ALU.is_gt, ALU.subtract)
                    te.wait_ge(self.sem["sv"], w3q_sv[s])
                    self.tp_group(
                        [qz[:, kk * 128:(kk + 1) * 128] for kk in range(4)],
                        w3r[:, sl * 4:sl * 4 + 4, mr * 128:(mr + 1) * 128])
                    w3t_st[s] = self.cnt["st"]
            self.wait(te, "sv")
            self.wait(te, "ss")
            for tt in range(8):
                i = og3 * 8 + tt
                slot = tt % 2
                ps = psc0[:, slot * 512:(slot + 1) * 512]
                xpar = i % 2
                if i >= 2:
                    sy.wait_ge(self.sem["sv"], out_dma.get(("xcons", i - 2), 0))
                    sy.wait_ge(self.sem["dxb%d" % xpar], out_dma.get(("x", i - 2), 0))
                out_dma[("x", i)] = self.DMA(
                    sy, "dxb%d" % xpar, FA.r("xsl", xpar * 512, (xpar + 1) * 512),
                    self.x_own[tt * 128:(tt + 1) * 128, og3 * 512:(og3 + 1) * 512])
                if i >= 2:
                    te.wait_ge(self.sem["ss"], out_dma.get(("evac", i - 2), 0))
                stc = None
                for k in range(44):
                    stc = MM(ps, ihr[:, k, tt * 128:(tt + 1) * 128],
                             A.bf("w3T", k * 512, (k + 1) * 512),
                             k == 0, k == 43, inc=(k == 43))
                fpar = i % 2
                f1ap = FA.r("f1", fpar * 512, (fpar + 1) * 512)
                self.wait(se, "st", stc)
                if i >= 2:
                    se.wait_ge(self.sem["sv"], out_dma.get(("xcons", i - 2), 0))
                out_dma[("evac", i)] = SC(
                    "activation", f1ap, ps, ACT.Copy, bias=0.0,
                    scale=self.stats[:, self.S_S3 + tt:self.S_S3 + tt + 1])
                obp = i % 2
                if i >= 2:
                    ve.wait_ge(self.sem["do%d" % obp], out_dma.get(("out", i - 2), 0))
                ve.wait_ge(self.sem["ss"], out_dma[("evac", i)])
                ve.wait_ge(self.sem["dxb%d" % xpar], out_dma[("x", i)])
                V("tensor_add", f1ap, f1ap,
                  A.bf("attn_out", tt * 2048 + og3 * 512, tt * 2048 + (og3 + 1) * 512))
                out_dma[("xcons", i)] = V(
                    "tensor_add", FA.r("obuf", obp * 512, (obp + 1) * 512),
                    f1ap, FA.r("xsl", xpar * 512, (xpar + 1) * 512))
                sy.wait_ge(self.sem["sv"], out_dma[("xcons", i)])
                out_dma[("out", i)] = self.DMA(
                    sy, "do%d" % obp, self.out[tt * 128:(tt + 1) * 128,
                                               og3 * 512:(og3 + 1) * 512],
                    FA.r("obuf", obp * 512, (obp + 1) * 512))
        self.wait(sy, "do0")
        self.wait(sy, "do1")


def rope_tables():
    import ctypes
    libm = ctypes.CDLL("libm.so.6")
    libm.powf.restype = ctypes.c_float
    libm.powf.argtypes = [ctypes.c_float, ctypes.c_float]
    libm.cosf.restype = ctypes.c_float
    libm.cosf.argtypes = [ctypes.c_float]
    libm.sinf.restype = ctypes.c_float
    libm.sinf.argtypes = [ctypes.c_float]
    ex = np.arange(0, DR, 2, dtype=np.float32) / np.float32(DR)
    inv = np.array([np.float32(1.0) / np.float32(libm.powf(ROPE_BASE, float(e)))
                    for e in ex], dtype=np.float32)
    t32 = np.arange(S, dtype=np.float32)
    fr = (t32[:, None] * inv[None, :]).astype(np.float32)
    cs = np.fromiter((libm.cosf(float(v)) for v in fr.ravel()),
                     np.float32, fr.size).reshape(S, 32)
    sn = np.fromiter((libm.sinf(float(v)) for v in fr.ravel()),
                     np.float32, fr.size).reshape(S, 32)
    sink = np.ascontiguousarray(np.tile(sn.T, (4, 1)))
    cosk = np.ascontiguousarray(np.tile(cs.T, (4, 1)))
    return sink, cosk


def host_prep(inputs):
    Wqkv = np.asarray(inputs["L0_attn_Wqkv"], dtype=np.float32)
    Wk_up = np.asarray(inputs["L0_attn_Wk_up"], dtype=np.float32)
    qn = [h * DQK + d for h in range(NH) for d in range(DN)]
    qr = []
    for p in range(8):
        for hl in (EVENG, ODDG):
            hh = hl[p]
            qr += [hh * DQK + DN + r for r in range(DR)]
    cols = qn + qr + list(range(NH * DQK, NH * DQK + LORA))
    Wp = np.ascontiguousarray(Wqkv[:, cols])
    wq_t = np.ascontiguousarray(
        Wp[:, :24 * 128].reshape(NKCH, 128, 24, 128).transpose(2, 1, 0, 3))
    wckv_t = np.ascontiguousarray(
        Wp[:, 24 * 128:].reshape(NKCH, 128, 4, 128).transpose(2, 1, 0, 3))
    kn = [g * DQK + d for g in range(NKV) for d in range(DN)]
    kr = []
    for gp_ in range(2):
        for half in range(2):
            g = gp_ * 2 + half
            kr += [g * DQK + DN + r for r in range(DR)]
    wk_p = np.ascontiguousarray(Wk_up[:, kn + kr])
    return wq_t, wckv_t, wk_p


def make_in_maps(inputs):
    wq_t, wckv_t, wk_p = host_prep(inputs)
    sink, cosk = rope_tables()
    tabk = np.ascontiguousarray(np.concatenate([sink, cosk], axis=1))
    x = np.asarray(inputs["L0_attn_hidden_states"], dtype=np.float32)
    ims = []
    for core in range(8):
        b, c = core // 4, core % 4
        tiles = owned_tiles(c)
        rows = np.concatenate([np.arange(t * 128, (t + 1) * 128) for t in tiles])
        ims.append(dict(
            x_own=np.ascontiguousarray(x[b][rows]),
            x_batch=np.ascontiguousarray(x[b]),
            tabk=tabk,
            tabq=np.ascontiguousarray(
                np.concatenate([sink[:, rows], cosk[:, rows]], axis=1)),
            meta=np.array([[c, 0, 0, 0]], dtype=np.int32),
            wq=wq_t, wckv=wckv_t, wk=wk_p,
            wv=np.ascontiguousarray(np.asarray(inputs["L0_attn_Wv_up"], dtype=np.float32)),
            wo=np.ascontiguousarray(np.asarray(inputs["L0_attn_Wo"], dtype=np.float32)),
            w1=np.ascontiguousarray(np.asarray(inputs["L0_ffn_w1"], dtype=np.float32)),
            w2=np.ascontiguousarray(np.asarray(inputs["L0_ffn_w2"], dtype=np.float32)),
            w3=np.ascontiguousarray(np.asarray(inputs["L0_ffn_w3"], dtype=np.float32)),
        ))
    return ims


def build_and_run(inputs, phases=8, dbg=False, cores=None):
    ims = make_in_maps(inputs)
    b = Bld(phases, dbg)
    nc = b.build()
    res = run_bass_kernel_spmd(nc, ims, core_ids=list(range(8)))
    return b, res


def kernel(**inputs):
    ims = make_in_maps(inputs)
    b = Bld(8, False)
    nc = b.build()
    res = run_bass_kernel_spmd(nc, ims, core_ids=list(range(8)))
    out = np.zeros((B, S, D), dtype=np.float32)
    for core in range(8):
        bb, c = core // 4, core % 4
        for j, t in enumerate(owned_tiles(c)):
            out[bb, t * 128:(t + 1) * 128, :] = res.results[core]["out"][j * 128:(j + 1) * 128, :]
    return out


# version 18
# speedup vs baseline: 1.0487x; 1.0084x over previous
import math
import numpy as np
import concourse.bass as bass
import concourse.mybir as mybir
from concourse.bass_utils import run_bass_kernel_spmd
from contextlib import ExitStack

F32 = mybir.dt.float32
BF16 = mybir.dt.bfloat16
I32 = mybir.dt.int32
ALU = mybir.AluOpType
ACT = mybir.ActivationFunctionType
AX = mybir.AxisListType.X

B, S, D = 2, 4096, 2048
NH, NKV, LORA = 16, 4, 512
DR, DN, DVh, DQK = 64, 128, 128, 192
HF = 5632
TOWN = 1024
NKCH = D // 128
SCL = 1.0 / math.sqrt(DQK)
EPS = 1e-5
ROPE_BASE = 10000.0
NW1CH = HF // 128

EVENG = [0, 1, 2, 3, 8, 9, 10, 11]
ODDG = [4, 5, 6, 7, 12, 13, 14, 15]
QR_PAIR = {}
for _i, _h in enumerate(EVENG):
    QR_PAIR[_h] = (_i, 0)
for _i, _h in enumerate(ODDG):
    QR_PAIR[_h] = (_i, 64)


def owned_tiles(c):
    return list(range(4 * c, 4 * c + 4)) + list(range(4 * (7 - c), 4 * (7 - c) + 4))


class Arena:
    def __init__(self, ap, n):
        self.ap, self.n, self.live = ap, n, {}

    def alloc(self, name, elems, align=32, at=None):
        assert name not in self.live, name
        elems = -(-elems // align) * align
        if at is not None:
            for so, se in self.live.values():
                assert at + elems <= so or at >= so + se, \
                    f"arena overlap: {name}@{at}+{elems} vs {so}+{se}"
            assert at + elems <= self.n
            self.live[name] = (at, elems)
            return
        off = 0
        for so, se in sorted(self.live.values()):
            if off + elems <= so:
                break
            off = max(off, so + se)
        assert off + elems <= self.n, f"arena OOM: {name} {elems}@{off} live={self.live}"
        self.live[name] = (off, elems)

    def free(self, name):
        del self.live[name]

    def bf(self, name, lo=0, hi=None):
        off, n = self.live[name]
        return self.ap[:, off + lo: off + (n if hi is None else hi)]

    r = bf


class Bld:

    def __init__(self, phases, dbg):
        self.phases, self.dbg = phases, dbg
        self.nc = bass.Bass("TRN2", target_bir_lowering=False)
        self.sem, self.cnt = {}, {}
        self.dbg_t = {}
        self._tp_evac_sv = [0, 0]

    def mksem(self, *names):
        for n in names:
            self.sem[n] = self.stack.enter_context(self.nc.semaphore(n))
            self.cnt[n] = 0

    def inc(self, instr, s, n=1):
        instr.then_inc(self.sem[s], n)
        self.cnt[s] += n
        return self.cnt[s]

    def wait(self, eng, s, val=None):
        v = self.cnt[s] if val is None else val
        if v > 0:
            eng.wait_ge(self.sem[s], v)

    def _chain(self, eng, sname, fn, *a, **kw):
        if self.cnt[sname] > 0:
            eng.wait_ge(self.sem[sname], self.cnt[sname])
        instr = fn(*a, **kw)
        instr.then_inc(self.sem[sname], 1)
        self.cnt[sname] += 1
        return self.cnt[sname]

    def V(self, op, *a, **kw):
        return self._chain(self.nc.vector, "sv", getattr(self.nc.vector, op), *a, **kw)

    def SC(self, op, *a, **kw):
        return self._chain(self.nc.scalar, "ss", getattr(self.nc.scalar, op), *a, **kw)

    def G(self, op, *a, **kw):
        return self._chain(self.nc.gpsimd, "sg", getattr(self.nc.gpsimd, op), *a, **kw)

    def MM(self, out, lhsT, rhs, start, stop, inc=False):
        instr = self.nc.tensor.matmul(out, lhsT, rhs, start=start, stop=stop)
        if inc:
            return self.inc(instr, "st", 1)
        return None

    def DMA(self, q, s, out, in_):
        instr = q.dma_start(out, in_)
        instr.then_inc(self.sem[s], 16)
        self.cnt[s] += 16
        return self.cnt[s]

    def dbg_out(self, name, shape, dtype=F32):
        if name not in self.dbg_t:
            self.dbg_t[name] = self.nc.dram_tensor(name, shape, dtype,
                                                   kind="ExternalOutput").ap()
        return self.dbg_t[name]

    def dbg_dump(self, name, shape, src_ap, waits=()):
        o = self.dbg_out(name, shape)
        cast = src_ap.dtype != F32
        e = self.nc.gpsimd if cast else self.nc.sync
        s = "dbgo" if cast else "dout"
        for w in waits:
            self.wait(e, w)
        self.DMA(e, s, o[:, :], src_ap)

    def build(self):
        nc = self.nc
        self.x_own = nc.dram_tensor("x_own", [TOWN, D], F32, kind="ExternalInput").ap()
        self.x_batch = nc.dram_tensor("x_batch", [S, D], F32, kind="ExternalInput").ap()
        self.tabk = nc.dram_tensor("tabk", [128, 2 * S], F32, kind="ExternalInput").ap()
        self.tabq = nc.dram_tensor("tabq", [128, 2 * TOWN], F32, kind="ExternalInput").ap()
        self.meta = nc.dram_tensor("meta", [1, 4], I32, kind="ExternalInput").ap()
        self.wq = nc.dram_tensor("wq", [24, 128, NKCH, 128], F32, kind="ExternalInput").ap()
        self.wckv = nc.dram_tensor("wckv", [4, 128, NKCH, 128], F32, kind="ExternalInput").ap()
        self.wk = nc.dram_tensor("wk", [LORA, 768], F32, kind="ExternalInput").ap()
        self.wv = nc.dram_tensor("wv", [LORA, 512], F32, kind="ExternalInput").ap()
        self.wo = nc.dram_tensor("wo", [D, D], F32, kind="ExternalInput").ap()
        self.w1 = nc.dram_tensor("w1", [HF, D], F32, kind="ExternalInput").ap()
        self.w2 = nc.dram_tensor("w2", [HF, D], F32, kind="ExternalInput").ap()
        self.w3 = nc.dram_tensor("w3", [D, HF], F32, kind="ExternalInput").ap()
        self.out = nc.dram_tensor("out", [TOWN, D], F32, kind="ExternalOutput").ap()
        self.h_dram = nc.dram_tensor("h_scratch", [TOWN, HF], F32).ap()
        self.attn_dram = nc.dram_tensor("attn_scratch", [TOWN, D], BF16).ap()

        with ExitStack() as stack:
            self.stack = stack
            ARENA_N = 89344
            FA_N = 7168
            IA_N = 512
            at = stack.enter_context(nc.sbuf_tensor("arena", [128, ARENA_N], BF16))
            self.A = Arena(at[:, :], ARENA_N)
            ft = stack.enter_context(nc.sbuf_tensor("farena", [128, FA_N], F32))
            self.FA = Arena(ft[:, :], FA_N)
            it = stack.enter_context(nc.sbuf_tensor("iarena", [128, IA_N], I32))
            self.IA = Arena(it[:, :], IA_N)
            self.consts = stack.enter_context(nc.sbuf_tensor("consts", [128, 320], BF16))
            self.stats = stack.enter_context(nc.sbuf_tensor("stats", [128, 288], F32))
            self.meta_sb = stack.enter_context(nc.sbuf_tensor("meta_sb", [1, 4], I32))
            self.psc0 = stack.enter_context(nc.psum_tensor("psc0", [128, 1024], F32))
            self.psc1 = stack.enter_context(nc.psum_tensor("psc1", [128, 1024], F32))
            self.ppv = stack.enter_context(nc.psum_tensor("ppv", [128, 1536], F32))
            self.pst = stack.enter_context(nc.psum_tensor("pst", [128, 512], F32))
            self.mksem("sv", "ss", "sg", "st", "dxb0", "dxb1", "dmeta", "dwg",
                       "dws0", "dws1", "dwsy0", "dwsy1", "dho0", "dho1",
                       "dhi0", "dhi1", "dout", "do0", "do1", "dw3a", "dw3b", "dbgo",
                       "sp3", "dwq0", "dwq1")
            self.I_bf = self.consts[:, 0:128]
            self.R = self.consts[:, 128:192]
            self.TRI = self.consts[:, 192:320]
            self.phase0()
            if self.phases >= 1:
                self.phase1()
            if self.phases >= 2:
                self.phase2()
            if self.phases >= 3:
                self.phase3()
            if self.phases >= 4:
                self.phase4()
            if self.phases >= 5:
                self.phase5()
            if self.phases >= 6:
                self.phase6()
            if self.phases >= 7:
                self.phase7()
            self.wait(nc.sync, "dout")
            self.wait(nc.gpsimd, "dbgo")
            self.wait(nc.gpsimd, "dwg")
        return nc

    S_ONE = 0
    S_OROW = 1
    S_LN = 4
    S_AM = 12
    S_SXQ = 16
    S_SXI = 24
    S_NB = 32
    S_QM = 48
    S_KM = 64
    S_QMR = 68
    S_KMR = 76
    S_PR = 80
    S_UHM = 96
    S_HM = 97
    S_LI = 98
    S_AW = 100
    S_SW = 104
    S_C1 = 116
    S_C2 = 124
    S_HAM = 132
    S_SXH = 140
    S_S3 = 148
    S_T0 = 156
    S_ONESROW = 160

    def st(self, col, n=1):
        return self.stats[:, col:col + n]

    def fence(self, engs=None):
        nc = self.nc
        engs = engs or (nc.sync, nc.gpsimd, nc.tensor, nc.vector, nc.scalar)
        for e in engs:
            for s in ("sv", "ss", "sg", "dbgo", "dout"):
                self.wait(e, s)
            if e is not nc.tensor:
                self.wait(e, "st")

    def emit_round(self, z, r0i, r0f, t1, out, t2=None):
        SC, V = self.SC, self.V
        se, ve = self.nc.scalar, self.nc.vector
        self.wait(se, "sv")
        SC("activation", r0i, z, ACT.Copy)
        SC("activation", r0f, r0i, ACT.Copy)
        self.wait(ve, "ss")
        V("scalar_tensor_tensor", t1, z, -0.5, r0f, ALU.add, ALU.is_gt)
        lt = z if t2 is None else t2
        V("scalar_tensor_tensor", lt, z, 0.5, r0f, ALU.add, ALU.is_lt)
        V("tensor_sub", t1, t1, lt)
        return V("tensor_add", out, r0f, t1)

    def phase0(self):
        nc, A, FA, V, SC, G, MM = self.nc, self.A, self.FA, self.V, self.SC, self.G, self.MM
        st, pst = self.st, self.pst
        sy, te, ve, se, gp = nc.sync, nc.tensor, nc.vector, nc.scalar, nc.gpsimd

        self.DMA(sy, "dmeta", self.meta_sb[:, :], self.meta[:, :])

        G("memset", st(self.S_ONE), 1.0)
        G("memset", self.stats[0:1, self.S_ONESROW:self.S_ONESROW + 128], 1.0)
        FA.alloc("cset", 128)
        cs = FA.r("cset")
        G("memset", cs, 1.0)
        G("affine_select", cs, cs, [[1, 128]], ALU.is_equal, 0.0,
          base=0, channel_multiplier=-1)
        self.wait(ve, "sg")
        V("tensor_copy", self.I_bf, cs)
        self.wait(gp, "sv")
        G("memset", cs, 1.0)
        G("affine_select", cs, cs, [[1, 128]], ALU.is_ge, 0.0,
          base=0, channel_multiplier=-1)
        self.wait(ve, "sg")
        V("tensor_copy", self.TRI, cs)
        self.wait(gp, "sv")
        G("memset", cs[:, 0:64], 0.0)
        for bp in (0, 64):
            G("memset", cs[bp:bp + 32, 32:64], 1.0)
            G("affine_select", cs[bp:bp + 32, 32:64], cs[bp:bp + 32, 32:64],
              [[1, 32]], ALU.is_equal, 0.0, base=0, channel_multiplier=-1)
            G("memset", cs[bp + 32:bp + 64, 0:32], -1.0)
            G("affine_select", cs[bp + 32:bp + 64, 0:32], cs[bp + 32:bp + 64, 0:32],
              [[1, 32]], ALU.is_equal, 0.0, base=0, channel_multiplier=-1)
        self.wait(ve, "sg")
        V("tensor_copy", self.R, cs[:, 0:64])
        G("memset", st(self.S_HM), 1.0)
        G("affine_select", st(self.S_HM), st(self.S_HM), [[0, 1]], ALU.is_ge, 0.0,
          base=63, channel_multiplier=-1)
        self.wait(ve, "sg")
        V("tensor_scalar", st(self.S_UHM), st(self.S_HM), -1.0, None, ALU.mult)
        V("tensor_scalar", st(self.S_UHM), st(self.S_UHM), 1.0, None, ALU.add)
        FA.free("cset")

        A.alloc("tab_sink", S, at=0)
        A.alloc("tab_cosk", S, at=4096)
        A.alloc("tab_sinq", TOWN, at=8192)
        A.alloc("tab_cosq", TOWN, at=9216)
        self.wait(te, "dmeta")
        self.wait(te, "sg")

    def phase1(self):
        nc, A, V, SC, G, MM = self.nc, self.A, self.V, self.SC, self.G, self.MM
        st, pst, psc0 = self.st, self.pst, self.psc0
        sy, te, ve, se, gp = nc.sync, nc.tensor, nc.vector, nc.scalar, nc.gpsimd

        self.fence()
        A.alloc("ckv", 4 * 4096, at=10240)
        A.alloc("wckv", 4 * 2048, at=26624)
        A.alloc("xnbf", 2 * D, at=34816)
        A.alloc("xngrp", 2 * 8192, at=38912)
        A.alloc("junkb", 2048, at=87296)
        self.FA.alloc("xtm", 2 * D)

        for m in range(4):
            self.DMA(gp, "dwg", A.bf("wckv", m * 2048, (m + 1) * 2048),
                     self.wckv[m].rearrange("p k j -> p (k j)"))
        dwg_ckv = self.cnt["dwg"]
        self.DMA(gp, "dbgo", A.bf("tab_sink"), self.tabk[:, 0:S])
        self.DMA(gp, "dbgo", A.bf("tab_cosk"), self.tabk[:, S:2 * S])
        self.DMA(gp, "dbgo", A.bf("tab_sinq"), self.tabq[:, 0:TOWN])
        self.DMA(gp, "dbgo", A.bf("tab_cosq"), self.tabq[:, TOWN:2 * TOWN])
        self.tab_cnt = self.cnt["dbgo"]

        xn_done_ss = [0] * 32
        x_dma_cnt = [0] * 32
        tp1_st = [0] * 32
        evac_sv = {}

        def issue_x(gt):
            par = gt % 2
            if gt >= 2:
                sy.wait_ge(self.sem["ss"], xn_done_ss[gt - 2])
                sy.wait_ge(self.sem["dxb%d" % par], x_dma_cnt[gt - 2])
            x_dma_cnt[gt] = self.DMA(sy, "dxb%d" % par,
                                     self.FA.r("xtm", par * D, (par + 1) * D),
                                     self.x_batch[gt * 128:(gt + 1) * 128, :])

        issue_x(0)
        issue_x(1)

        for g in range(8):
            for tt in range(4):
                gt = g * 4 + tt
                par = gt % 2
                xt = self.FA.r("xtm", par * D, (par + 1) * D)
                xnb = A.bf("xnbf", par * D, (par + 1) * D)
                ve.wait_ge(self.sem["dxb%d" % par], x_dma_cnt[gt])
                se.wait_ge(self.sem["dxb%d" % par], x_dma_cnt[gt])
                if gt >= 2:
                    se.wait_ge(self.sem["st"], tp1_st[gt - 2])
                xn_done_ss[gt] = self.ln_tile(xt, xnb)
                if gt + 2 < 32:
                    issue_x(gt + 2)
                self.wait(te, "ss", xn_done_ss[gt])
                self.transpose_2048(xnb, self._grp_dst_fn(g % 2, tt))
                tp1_st[gt] = self.cnt["st"]
            te.wait_ge(self.sem["ss"], max(self._tp_evac_sv))
            gsl_lo, gsl_hi = g * 512, (g + 1) * 512
            xng = A.bf("xngrp", (g % 2) * 8192, ((g % 2) + 1) * 8192)
            for m in range(4):
                slot = (m % 2)
                ps = psc0[:, slot * 512:slot * 512 + 512]
                if g == 0 and m == 0:
                    te.wait_ge(self.sem["dwg"], dwg_ckv)
                if (g, m) in evac_sv:
                    pass
                prev = (g, m - 2) if m >= 2 else ((g - 1, m + 2) if g > 0 else None)
                if prev in evac_sv:
                    te.wait_ge(self.sem["sv"], evac_sv[prev])
                stc = None
                for kk in range(NKCH):
                    stc = MM(ps, A.bf("wckv", m * 2048 + kk * 128, m * 2048 + (kk + 1) * 128),
                             xng[:, kk * 512:(kk + 1) * 512],
                             kk == 0, kk == NKCH - 1, inc=(kk == NKCH - 1))
                self.wait(ve, "st", stc)
                evac_sv[(g, m)] = V("tensor_copy",
                                    A.bf("ckv", m * 4096 + gsl_lo, m * 4096 + gsl_hi), ps)
        self._p1_last_evacs = evac_sv
        self._p1_tail = (xn_done_ss[30], xn_done_ss[31], x_dma_cnt[30], x_dma_cnt[31])
        A.free("xngrp")
        A.free("wckv")

        if self.dbg and self.phases < 2:
            self.dbg_dump("dbg_ckv", [128, 4 * 4096], A.bf("ckv"), waits=("sv",))

    def ln_tile(self, xt, xnb_out):
        st, V, SC = self.st, self.V, self.SC
        ve, se = self.nc.vector, self.nc.scalar
        ssum, ssq = st(self.S_LN), st(self.S_LN + 1)
        mean, var = st(self.S_LN + 2), st(self.S_LN + 3)
        rstd = st(self.S_LN + 4)
        V("reduce_sum", ssum, xt, AX)
        self.wait(se, "sv")
        SC("activation", self.A.bf("junkb"), xt, ACT.Square, accum_out=ssq)
        V("tensor_scalar", mean, ssum, 1.0 / D, None, ALU.mult)
        self.wait(ve, "ss")
        V("tensor_scalar", var, ssq, 1.0 / D, None, ALU.mult)
        V("tensor_mul", ssq, mean, mean)
        V("tensor_sub", var, var, ssq)
        V("tensor_scalar", var, var, EPS, None, ALU.add)
        self.wait(se, "sv")
        SC("activation", var, var, ACT.Sqrt)
        self.wait(ve, "ss")
        V("reciprocal", rstd, var)
        V("tensor_scalar", xt, xt, mean, None, ALU.subtract)
        self.wait(se, "sv")
        return SC("activation", xnb_out, xt, ACT.Copy, bias=0.0, scale=rstd)

    def tp_group(self, blocks, dst):
        te, se = self.nc.tensor, self.nc.scalar
        MM, SC = self.MM, self.SC
        buf = self._tp_toggle
        self._tp_toggle ^= 1
        ps = self.pst if buf == 0 else self.ppv
        if self._tp_evac_sv[buf]:
            te.wait_ge(self.sem["ss"], self._tp_evac_sv[buf])
        stc = None
        n = len(blocks)
        for kk, b in enumerate(blocks):
            stc = MM(ps[:, kk * 128:(kk + 1) * 128], b, self.I_bf,
                     True, True, inc=(kk == n - 1))
        self.wait(se, "st", stc)
        src = ps[:, 0:n * 128]
        if len(dst.shape) == 3:
            src = src.rearrange("p (a b) -> p a b", a=n)
        self._tp_evac_sv[buf] = SC("activation", dst, src, ACT.Copy)
        return self._tp_evac_sv[buf]

    def transpose_2048(self, src_bf, dst_fn):
        for kb in range(0, NKCH, 4):
            self.tp_group([src_bf[:, (kb + kk) * 128:(kb + kk + 1) * 128]
                           for kk in range(4)], dst_fn(kb))

    _tp_toggle = 0

    def _grp_dst_fn(self, par, tt):
        base = self.A.bf("xngrp", par * 8192, (par + 1) * 8192)
        r = base.rearrange("p (k c) -> p k c", c=512)

        def fn(kb):
            return r[:, kb:kb + 4, tt * 128:(tt + 1) * 128]
        return fn

    def phase2(self):
        nc, A, V, SC, G, MM = self.nc, self.A, self.V, self.SC, self.G, self.MM
        st, pst, psc0, psc1 = self.st, self.pst, self.psc0, self.psc1
        sy, te, ve, se, gp = nc.sync, nc.tensor, nc.vector, nc.scalar, nc.gpsimd
        self.ixq_dram = nc.dram_tensor("ixq_scratch", [128, 16384], BF16).ap()

        A.alloc("xnown", 16384, at=56576)
        A.alloc("ixq", 16384, at=38912)
        A.alloc("ixbf", 2048, at=26624)
        FA = self.FA
        FA.alloc("zq", 512)
        FA.alloc("rqf", 512)
        FA.alloc("rqt", 512)
        self.IA.alloc("ri", 512)
        self.wait(se, "st")

        def own_dst(name, j):
            base = A.bf(name)
            r = base.rearrange("p (k c) -> p k c", c=1024)

            def fn(kb):
                return r[:, kb:kb + 4, j * 128:(j + 1) * 128]
            return fn

        x_dma = [0] * 8
        xn_ss = [0] * 8
        cons_ss = [0] * 8
        tpx_st = [0] * 8
        tpi_st = [0] * 8

        p1t = self._p1_tail

        def issue_own(j):
            par = j % 2
            if j >= 2:
                sy.wait_ge(self.sem["ss"], cons_ss[j - 2])
                sy.wait_ge(self.sem["dxb%d" % par], x_dma[j - 2])
            else:
                sy.wait_ge(self.sem["ss"], p1t[j])
                sy.wait_ge(self.sem["dxb%d" % par], p1t[2 + j])
            x_dma[j] = self.DMA(sy, "dxb%d" % par,
                                self.FA.r("xtm", par * D, (par + 1) * D),
                                self.x_own[j * 128:(j + 1) * 128, :])

        issue_own(0)
        issue_own(1)
        for j in range(8):
            par = j % 2
            xt = self.FA.r("xtm", par * D, (par + 1) * D)
            xnb = A.bf("xnbf", par * D, (par + 1) * D)
            ve.wait_ge(self.sem["dxb%d" % par], x_dma[j])
            se.wait_ge(self.sem["dxb%d" % par], x_dma[j])
            if j >= 2:
                se.wait_ge(self.sem["st"], tpx_st[j - 2])
            xn_ss[j] = self.ln_tile(xt, xnb)
            am, am2, rr = st(self.S_AM), st(self.S_AM + 1), st(self.S_AM + 2)
            comb = st(self.S_T0)
            self.wait(ve, "ss")
            V("reduce_max", am, xt, AX, apply_absolute_value=True)
            V("tensor_mul", am, am, st(self.S_LN + 4))
            V("tensor_scalar", am2, am, EPS, None, ALU.max)
            V("reciprocal", rr, am2)
            V("tensor_scalar", st(self.S_SXI + j), am2, 1.0 / 127.0, None, ALU.mult)
            V("tensor_mul", comb, rr, st(self.S_LN + 4))
            V("tensor_scalar", comb, comb, 127.0, None, ALU.mult)
            rq_sv = 0
            if j >= 1:
                ve.wait_ge(self.sem["st"], tpi_st[j - 1])
            for ss_ in range(4):
                sl = slice(ss_ * 512, (ss_ + 1) * 512)
                self.wait(se, "sv")
                SC("activation", FA.r("zq"), xt[:, sl], ACT.Copy,
                   bias=0.0, scale=comb)
                rq_sv = self.emit_round(FA.r("zq"), self.IA.r("ri"), FA.r("rqf"),
                                        FA.r("rqt"), A.bf("ixbf")[:, sl])
            cons_ss[j] = self.cnt["ss"]
            if j + 2 < 8:
                issue_own(j + 2)
            self.wait(te, "ss", xn_ss[j])
            self.transpose_2048(xnb, own_dst("xnown", j))
            tpx_st[j] = self.cnt["st"]
            te.wait_ge(self.sem["sv"], rq_sv)
            self.transpose_2048(A.bf("ixbf"), own_dst("ixq", j))
            tpi_st[j] = self.cnt["st"]
        self.wait(gp, "ss")
        self.DMA(gp, "dwg", self.ixq_dram[:, :], A.bf("ixq"))
        ixq_spill_cnt = self.cnt["dwg"]
        A.free("ixq")
        A.free("ixbf")
        A.free("xnbf")
        A.free("junkb")
        FA.free("xtm")
        FA.free("zq")
        FA.free("rqf")
        FA.free("rqt")
        self.IA.free("ri")
        self.fence()

        if self.dbg:
            self.dbg_dump("dbg_xnown", [128, 16384], A.bf("xnown"), waits=("sv",))

        A.alloc("knope", 16384, at=72960)
        A.alloc("krope", 8192, at=43264)
        A.alloc("vaug", 16640, at=26624)
        A.alloc("wk_b", 4 * 768, at=51456)
        A.alloc("wv_b", 4 * 512, at=54528)
        gp.wait_ge(self.sem["dwg"], ixq_spill_cnt)
        for r in range(4):
            self.DMA(gp, "dwg", A.bf("wk_b", r * 768, (r + 1) * 768),
                     self.wk[r * 128:(r + 1) * 128, :])
        for r in range(4):
            self.DMA(gp, "dwg", A.bf("wv_b", r * 512, (r + 1) * 512),
                     self.wv[r * 128:(r + 1) * 128, :])
        dwg_kv = self.cnt["dwg"]
        te.wait_ge(self.sem["dwg"], dwg_kv)
        self.wait(te, "sv")

        ev_prev = None
        for m in range(6):
            for n in range(8):
                slot = ((m * 8 + n) % 2)
                ps = psc0[:, slot * 512:(slot + 1) * 512]
                if ev_prev is not None and (m * 8 + n) >= 2:
                    te.wait_ge(self.sem["sv"], ev_two_ago)
                stc = None
                for kk in range(4):
                    stc = MM(ps, A.bf("wk_b", kk * 768 + m * 128, kk * 768 + (m + 1) * 128),
                             A.bf("ckv", kk * 4096 + n * 512, kk * 4096 + (n + 1) * 512),
                             kk == 0, kk == 3, inc=(kk == 3))
                self.wait(ve, "st", stc)
                if m < 4:
                    dst = A.bf("knope", m * 4096 + n * 512, m * 4096 + (n + 1) * 512)
                else:
                    dst = A.bf("krope", (m - 4) * 4096 + n * 512, (m - 4) * 4096 + (n + 1) * 512)
                ev = V("tensor_copy", dst, ps)
                ev_two_ago = ev_prev if ev_prev is not None else ev
                ev_prev = ev
        V("memset", A.bf("vaug"), 1.0)
        vr = A.bf("vaug").rearrange("p (t g c) -> p t g c", g=4, c=130)
        for t in range(32):
            slot = t % 2
            ps = psc1[:, slot * 512:(slot + 1) * 512]
            if t >= 2:
                te.wait_ge(self.sem["sv"], ev_two_ago)
            stc = None
            for kk in range(4):
                stc = MM(ps, A.bf("ckv", kk * 4096 + t * 128, kk * 4096 + (t + 1) * 128),
                         A.bf("wv_b", kk * 512, (kk + 1) * 512),
                         kk == 0, kk == 3, inc=(kk == 3))
            self.wait(ve, "st", stc)
            ev = V("tensor_copy", vr[:, t, :, 0:128],
                   ps.rearrange("p (g c) -> p g c", g=4))
            ev_two_ago = ev_prev
            ev_prev = ev
        A.free("ckv")
        A.free("wk_b")
        A.free("wv_b")

        FA.alloc("rsc1", 512)
        FA.alloc("rsc2", 512)
        self.wait(te, "sv")
        self.wait(te, "ss")
        ve.wait_ge(self.sem["dbgo"], self.tab_cnt)
        r2rd = [0, 0]
        tgl = 0
        for gpair in range(2):
            for half in range(2):
                po = half * 64
                for n in range(8):
                    b = tgl
                    tgl ^= 1
                    ps_ = (pst if b == 0 else self.ppv)[po:po + 64, 0:512]
                    kslice = A.bf("krope", gpair * 4096 + n * 512,
                                  gpair * 4096 + (n + 1) * 512)[po:po + 64, :]
                    if r2rd[b]:
                        te.wait_ge(self.sem["sv"], r2rd[b])
                    stc = MM(ps_, self.R[po:po + 64, :], kslice,
                             True, True, inc=True)
                    cosb = A.bf("tab_cosk", n * 512, (n + 1) * 512)[po:po + 64, :]
                    sinb = A.bf("tab_sink", n * 512, (n + 1) * 512)[po:po + 64, :]
                    r1 = FA.r("rsc1")[po:po + 64, :]
                    r2 = FA.r("rsc2")[po:po + 64, :]
                    V("tensor_mul", r1, kslice, cosb)
                    self.wait(ve, "st", stc)
                    r2rd[b] = V("tensor_mul", r2, ps_, sinb)
                    V("tensor_add", kslice, r1, r2)
        A.free("tab_sink")
        A.free("tab_cosk")
        if self.dbg:
            self.dbg_dump("dbg_knope", [128, 16384], A.bf("knope"), waits=("sv",))
            self.dbg_dump("dbg_krope", [128, 8192], A.bf("krope"), waits=("sv",))
            self.dbg_dump("dbg_vaug", [128, 16640], A.bf("vaug"), waits=("sv",))

        A.alloc("qnope", 16384, at=10240)
        A.alloc("qrope", 8192, at=0)
        A.alloc("wqbuf", 2 * 2048, at=51456)
        wq_dma = [0] * 24
        q_done_st = [0] * 24

        def issue_wq(m):
            par = m % 2
            if m >= 2:
                gp.wait_ge(self.sem["st"], q_done_st[m - 2])
                gp.wait_ge(self.sem["dwq%d" % par], wq_dma[m - 2])
            else:
                self.wait(gp, "sv")
                self.wait(gp, "ss")
                self.wait(gp, "st")
            wq_dma[m] = self.DMA(gp, "dwq%d" % par,
                                 A.bf("wqbuf", par * 2048, (par + 1) * 2048),
                                 self.wq[m].rearrange("p k j -> p (k j)"))

        issue_wq(0)
        issue_wq(1)
        self.wait(te, "ss")
        self.wait(te, "sv")
        pfree = [0, 0]
        for m in range(24):
            par = m % 2
            psb = psc0 if par == 0 else psc1
            te.wait_ge(self.sem["dwq%d" % par], wq_dma[m])
            if pfree[par]:
                te.wait_ge(self.sem["ss"], pfree[par])
            stc = None
            for n in range(2):
                ps = psb[:, n * 512:(n + 1) * 512]
                for kk in range(NKCH):
                    stc = MM(ps, A.bf("wqbuf", par * 2048 + kk * 128, par * 2048 + (kk + 1) * 128),
                             A.bf("xnown", kk * 1024 + n * 512, kk * 1024 + (n + 1) * 512),
                             kk == 0, kk == NKCH - 1, inc=(kk == NKCH - 1 and n == 1))
            q_done_st[m] = self.cnt["st"]
            if m + 2 < 24:
                issue_wq(m + 2)
            self.wait(se, "st", q_done_st[m])
            if m < 16:
                pfree[par] = SC("activation", A.bf("qnope", m * 1024, (m + 1) * 1024),
                                psb[:, 0:1024], ACT.Copy, bias=0.0, scale=SCL)
            else:
                p = m - 16
                if m == 16:
                    self.wait(se, "sv")
                pfree[par] = SC("activation", A.bf("qrope", p * 1024, (p + 1) * 1024),
                                psb[:, 0:1024], ACT.Copy, bias=0.0, scale=SCL)
        self.wait(te, "ss")
        self.wait(te, "sv")
        self.wait(ve, "ss")
        r2rd = [0, 0]
        tgl = 0
        for p in range(8):
            for half in range(2):
                po = half * 64
                for n in range(2):
                    b = tgl
                    tgl ^= 1
                    ps_ = (pst if b == 0 else self.ppv)[po:po + 64, 0:512]
                    qsl = A.bf("qrope", p * 1024 + n * 512,
                               p * 1024 + (n + 1) * 512)[po:po + 64, :]
                    if r2rd[b]:
                        te.wait_ge(self.sem["sv"], r2rd[b])
                    stc2 = MM(ps_, self.R[po:po + 64, :], qsl, True, True, inc=True)
                    cosb = A.bf("tab_cosq", n * 512, (n + 1) * 512)[po:po + 64, :]
                    sinb = A.bf("tab_sinq", n * 512, (n + 1) * 512)[po:po + 64, :]
                    r1 = FA.r("rsc1")[po:po + 64, :]
                    r2 = FA.r("rsc2")[po:po + 64, :]
                    V("tensor_mul", r1, qsl, cosb)
                    self.wait(ve, "st", stc2)
                    r2rd[b] = V("tensor_mul", r2, ps_, sinb)
                    V("tensor_add", qsl, r1, r2)
        A.free("xnown")
        A.free("wqbuf")
        A.free("tab_sinq")
        A.free("tab_cosq")
        FA.free("rsc1")
        FA.free("rsc2")

        self.wait(ve, "ss")
        for h in range(16):
            V("reduce_max", st(self.S_QM + h), A.bf("qnope", h * 1024, (h + 1) * 1024),
              AX, apply_absolute_value=True)
        for p in range(8):
            V("reduce_max", st(self.S_QMR + p), A.bf("qrope", p * 1024, (p + 1) * 1024),
              AX, apply_absolute_value=True)
        for g in range(4):
            V("reduce_max", st(self.S_KM + g), A.bf("knope", g * 4096, (g + 1) * 4096),
              AX, apply_absolute_value=True)
        for gpair in range(2):
            V("reduce_max", st(self.S_KMR + gpair),
              A.bf("krope", gpair * 4096, (gpair + 1) * 4096),
              AX, apply_absolute_value=True)
        for h in range(16):
            V("tensor_mul", st(self.S_PR + h), st(self.S_QM + h), st(self.S_KM + h // 4))
        for h in range(16):
            p, po = QR_PAIR[h]
            rp = st(self.S_T0)
            V("tensor_mul", rp, st(self.S_QMR + p), st(self.S_KMR + p // 4))
            V("tensor_mul", rp, rp, st(self.S_HM if po == 0 else self.S_UHM))
            V("tensor_add", st(self.S_PR + h), st(self.S_PR + h), rp)
        self.wait(te, "sv")
        stc = MM(self.pst[0:1, 0:16], st(self.S_ONE), self.stats[:, self.S_PR:self.S_PR + 16],
                 True, True, inc=True)
        self.wait(ve, "st", stc)
        V("tensor_scalar", self.stats[0:1, self.S_PR:self.S_PR + 16],
          self.pst[0:1, 0:16], -1.0, None, ALU.mult)
        self.wait(te, "sv")
        stc = MM(self.pst[:, 0:16], self.stats[0:1, self.S_ONESROW:self.S_ONESROW + 128],
                 self.stats[0:1, self.S_PR:self.S_PR + 16], True, True, inc=True)
        self.wait(ve, "st", stc)
        V("tensor_copy", self.stats[:, self.S_NB:self.S_NB + 16], self.pst[:, 0:16])
        if self.dbg:
            self.dbg_dump("dbg_q", [128, 16384], A.bf("qnope"), waits=("ss",))
            self.dbg_dump("dbg_qr", [128, 8192], A.bf("qrope"), waits=("sv",))
            nb = self.dbg_out("dbg_nb", [128, 16])
            self.wait(sy, "sv")
            self.DMA(sy, "dout", nb[:, :], self.stats[:, self.S_NB:self.S_NB + 16])

    def phase3(self):
        nc, A, V, SC, G, MM = self.nc, self.A, self.V, self.SC, self.G, self.MM
        st, pst = self.st, self.pst
        sy, te, ve, se, gp = nc.sync, nc.tensor, nc.vector, nc.scalar, nc.gpsimd
        A.alloc("ppan", 16896, at=51456)
        A.alloc("atst", 256, at=68352)
        self.fence()
        dat_base = (self.cnt["dho0"], self.cnt["dho1"])
        dat_cnt = {u: dat_base[u % 2] + 16 * (u // 2 + 1) for u in range(128)}

        self.wait(te, "dmeta")
        self.wait(se, "dmeta")
        self.wait(ve, "dmeta")
        rt = te.alloc_register("rt")
        rs = se.alloc_register("rs")
        rv = ve.alloc_register("rv")
        te.reg_load(rt, self.meta_sb[0:1, 0:1])
        se.reg_load(rs, self.meta_sb[0:1, 0:1])
        ve.reg_load(rv, self.meta_sb[0:1, 0:1])

        FA = self.FA
        FA.alloc("wmba", 2 * 1024)
        FA.alloc("wmbb", 2 * 1024)
        FA.alloc("wmbc", 2 * 1408)
        V("memset", self.stats[:, self.S_AW:self.S_AW + 3], 0.0)
        wstr = [
            dict(dram=self.w1, n=88, parts=2, w=1024, buf="wmba",
                 sems=("dwsy0", "dwsy1"), acc=self.S_AW, q="sy"),
            dict(dram=self.w2, n=88, parts=2, w=1024, buf="wmbb",
                 sems=("dw3a", "dw3b"), acc=self.S_AW + 1, q="gp"),
            dict(dram=self.w3, n=64, parts=4, w=1408, buf="wmbc",
                 sems=("dws0", "dws1"), acc=self.S_AW + 2, q="sy"),
        ]
        for s_ in wstr:
            b0, b1 = self.cnt[s_["sems"][0]], self.cnt[s_["sems"][1]]
            s_["dpre"] = [(b0 if i % 2 == 0 else b1) + 16 * (i // 2 + 1)
                          for i in range(s_["n"])]

        sched = []
        for h in range(16):
            blk = []
            for k, s_ in enumerate(wstr):
                blk += [(k, i) for i in range((h * s_["n"]) // 16,
                                              ((h + 1) * s_["n"]) // 16)]
            sched.append(blk)
        nch = [len(x) for x in sched]
        pnch = [sum(nch[:h]) for h in range(16)]

        base = dict(self.cnt)
        red_pre = {}
        _cv = base["sv"]
        for h in range(16):
            for ch in sched[h]:
                _cv += 2
                red_pre[ch] = _cv
            _cv += 24

        def plan(c):
            tiles = owned_tiles(c)
            NKT = tiles[-1] + 1
            sj = [next(j for j in range(8) if tiles[j] >= kt) for kt in range(NKT)]
            poff = [0] * NKT
            acc = 0
            for kt in range(NKT):
                poff[kt] = acc
                acc += (8 - sj[kt]) * 128
            p = dict(tiles=tiles, NKT=NKT, sj=sj, poff=poff,
                     scores_st={}, pvj_st={}, exp_ss={}, mask_sv={}, evj_sv={})
            cst, css, csv = base["st"], base["ss"], base["sv"]
            for h in range(16):
                csv += 2 * nch[h]
                for kt in range(NKT):
                    cst += 1
                    p["scores_st"][(h, kt)] = cst
                    css += 1
                    p["exp_ss"][(h, kt)] = css
                    if kt in tiles:
                        csv += 1
                        p["mask_sv"][(h, kt)] = csv
                for j in range(8):
                    cst += 1
                    p["pvj_st"][(h, j)] = cst
                    csv += 2
                    p["evj_sv"][(h, j)] = csv
            p["end"] = dict(st=cst, ss=css, sv=csv)
            return p

        plans = [plan(c) for c in range(4)]
        ceil = dict(st=max(p["end"]["st"] for p in plans),
                    ss=max(p["end"]["ss"] for p in plans),
                    sv=max(p["end"]["sv"] for p in plans))

        def q_rhs(h, col0, w, which):
            if which == "n":
                return A.bf("qnope", h * 1024 + col0, h * 1024 + col0 + w)
            pq, po = QR_PAIR[h]
            return A.bf("qrope", pq * 1024 + col0, pq * 1024 + col0 + w)[po:po + 64, :]

        for c in range(4):
            p = plans[c]
            tiles, NKT, sj, poff = p["tiles"], p["NKT"], p["sj"], p["poff"]

            self.cnt = dict(base)
            with te.If_eq(rt, c):
                for h in range(16):
                    g = h // 4
                    goff = 64 * (g % 2)
                    for kt in range(NKT):
                        u = h * NKT + kt
                        buf = u % 2
                        psb = (self.psc0 if buf == 0 else self.psc1)
                        j0 = sj[kt]
                        col0, w = j0 * 128, (8 - j0) * 128
                        if u >= 2:
                            te.wait_ge(self.sem["ss"],
                                       p["exp_ss"][(h, kt - 2) if kt >= 2 else (h - 1, NKT - (2 - kt))])
                        kn = A.bf("knope", g * 4096 + kt * 128, g * 4096 + (kt + 1) * 128)
                        kr = A.bf("krope", (g // 2) * 4096 + kt * 128,
                                  (g // 2) * 4096 + (kt + 1) * 128)[goff:goff + 64, :]
                        nchunk = 1 if w <= 512 else 2
                        for ci in range(nchunk):
                            off = ci * 512
                            nw = min(512, w - off)
                            MM(psb[:, off:off + nw], kn,
                               q_rhs(h, col0 + off, nw, "n"), True, False)
                            MM(psb[:, off:off + nw], kr,
                               q_rhs(h, col0 + off, nw, "r"), False, True,
                               inc=(ci == nchunk - 1))
                        assert self.cnt["st"] == p["scores_st"][(h, kt)]
                    for j in range(8):
                        te.wait_ge(self.sem["ss"], p["exp_ss"][(h, tiles[j])])
                        if j in tiles or True:
                            te.wait_ge(self.sem["sv"], p["mask_sv"][(h, tiles[j])])
                        if j >= 3:
                            te.wait_ge(self.sem["sv"], p["evj_sv"][(h, j - 3)])
                        elif h > 0:
                            te.wait_ge(self.sem["sv"],
                                       p["evj_sv"][(h - 1, {0: 6, 1: 7, 2: 5}[j])])
                        bank = self.ppv[:, (j % 3) * 512:(j % 3) * 512 + 129]
                        for kt in range(tiles[j] + 1):
                            MM(bank,
                               A.bf("ppan", poff[kt] + (j - sj[kt]) * 128,
                                    poff[kt] + (j - sj[kt] + 1) * 128),
                               A.bf("vaug", kt * 520 + g * 130,
                                    kt * 520 + g * 130 + 129),
                               start=(kt == 0), stop=(kt == tiles[j]),
                               inc=(kt == tiles[j]))
                        assert self.cnt["st"] == p["pvj_st"][(h, j)]
                te.wait_ge(self.sem["st"], self.cnt["st"])
                for _ in range(ceil["st"] - self.cnt["st"]):
                    te.nop(nofuse=True).then_inc(self.sem["st"], 1)
                te.nop(nofuse=True).then_inc(self.sem["sp3"], 1)


        te.end_ifs()
        te.nop(nofuse=True)
        for c in range(4):
            p = plans[c]
            tiles, NKT, sj, poff = p["tiles"], p["NKT"], p["sj"], p["poff"]

            self.cnt = dict(base)
            with se.If_eq(rs, c):
                for h in range(16):
                    if h > 0:
                        se.wait_ge(self.sem["st"], p["pvj_st"][(h - 1, 7)])
                    for kt in range(NKT):
                        buf = (h * NKT + kt) % 2
                        psb = (self.psc0 if buf == 0 else self.psc1)
                        j0 = sj[kt]
                        w = (8 - j0) * 128
                        se.wait_ge(self.sem["st"], p["scores_st"][(h, kt)])
                        SC("activation", A.bf("ppan", poff[kt], poff[kt] + w),
                           psb[:, 0:w], ACT.Exp,
                           bias=self.stats[:, self.S_NB + h:self.S_NB + h + 1],
                           scale=1.0)
                        assert self.cnt["ss"] == p["exp_ss"][(h, kt)]
                se.wait_ge(self.sem["ss"], self.cnt["ss"])
                for _ in range(ceil["ss"] - self.cnt["ss"]):
                    se.nop(nofuse=True).then_inc(self.sem["ss"], 1)
                se.nop(nofuse=True).then_inc(self.sem["sp3"], 1)


        se.end_ifs()
        se.nop(nofuse=True)
        for c in range(4):
            p = plans[c]
            tiles, NKT, sj, poff = p["tiles"], p["NKT"], p["sj"], p["poff"]

            self.cnt = dict(base)
            with ve.If_eq(rv, c):
                for h in range(16):
                    for ch in sched[h]:
                        s_ = wstr[ch[0]]
                        i = ch[1]
                        wpar = i % 2
                        ve.wait_ge(self.sem[s_["sems"][wpar]], s_["dpre"][i])
                        V("reduce_sum", st(self.S_T0 + 3),
                          self.FA.r(s_["buf"], wpar * s_["w"], (wpar + 1) * s_["w"]),
                          AX, apply_absolute_value=True)
                        got = V("tensor_add", st(s_["acc"]), st(s_["acc"]),
                                st(self.S_T0 + 3))
                        assert got == red_pre[ch]
                    for kt in range(NKT):
                        if kt not in tiles:
                            continue
                        ve.wait_ge(self.sem["ss"], p["exp_ss"][(h, kt)])
                        blk = A.bf("ppan", poff[kt], poff[kt] + 128)
                        V("tensor_mul", blk, blk, self.TRI)
                        assert self.cnt["sv"] == p["mask_sv"][(h, kt)]
                    for j in range(8):
                        u = h * 8 + j
                        par = u % 2
                        ve.wait_ge(self.sem["st"], p["pvj_st"][(h, j)])
                        if u >= 2:
                            ve.wait_ge(self.sem["dho%d" % par], dat_cnt[u - 2])
                        bank = self.ppv[:, (j % 3) * 512:(j % 3) * 512 + 129]
                        V("reciprocal", st(self.S_LI), bank[:, 128:129])
                        V("tensor_scalar",
                          A.bf("atst", par * 128, (par + 1) * 128),
                          bank[:, 0:128], st(self.S_LI), None, ALU.mult)
                        assert self.cnt["sv"] == p["evj_sv"][(h, j)]
                ve.wait_ge(self.sem["sv"], self.cnt["sv"])
                for _ in range(ceil["sv"] - self.cnt["sv"]):
                    ve.nop(nofuse=True).then_inc(self.sem["sv"], 1)
                ve.nop(nofuse=True).then_inc(self.sem["sp3"], 1)

        ve.end_ifs()
        ve.nop(nofuse=True)

        ev_uniform = lambda h, j: (base["sv"] + 24 * h + 2 * (pnch[h] + nch[h])
                                   + 8 + 2 * (j + 1))
        for p in plans:
            for h in range(16):
                for j in range(8):
                    assert p["evj_sv"][(h, j)] == ev_uniform(h, j)

        def emit_issues(hh):
            if hh >= 16:
                return
            for ch in sched[hh]:
                s_ = wstr[ch[0]]
                i = ch[1]
                wpar = i % 2
                q = sy if s_["q"] == "sy" else gp
                if i >= 2:
                    q.wait_ge(self.sem["sv"], red_pre[(ch[0], i - 2)])
                    q.wait_ge(self.sem[s_["sems"][wpar]], s_["dpre"][i - 2])
                else:
                    q.wait_ge(self.sem["sv"], base["sv"])
                    q.wait_ge(self.sem["ss"], base["ss"])
                p_, w_ = s_["parts"], s_["w"]
                src = s_["dram"][(i // p_) * 128:(i // p_ + 1) * 128,
                                 (i % p_) * w_:(i % p_ + 1) * w_]
                got = self.DMA(q, s_["sems"][wpar],
                               self.FA.r(s_["buf"], wpar * w_, (wpar + 1) * w_),
                               src)
                assert got == s_["dpre"][i]

        emit_issues(0)
        for u in range(128):
            h, j = divmod(u, 8)
            par = u % 2
            gp.wait_ge(self.sem["sv"], ev_uniform(h, j))
            got = self.DMA(gp, "dho%d" % par,
                           self.attn_dram[j * 128:(j + 1) * 128,
                                          h * 128:(h + 1) * 128],
                           A.bf("atst", par * 128, (par + 1) * 128))
            assert got == dat_cnt[u], (u, got, dat_cnt[u])
            if j == 7:
                emit_issues(h + 1)

        self.cnt.update(ceil)
        self.cnt["sp3"] = 3
        for e in (te, se, ve, sy, gp):
            e.wait_ge(self.sem["sp3"], 3)
        te.nop()
        se.nop()
        ve.nop()
        A.free("ppan")
        A.free("atst")
        A.free("knope")
        A.free("krope")
        A.free("vaug")
        A.free("qnope")
        A.free("qrope")

    def phase4(self):
        nc, A, V, SC, G, MM = self.nc, self.A, self.V, self.SC, self.G, self.MM
        st, pst, psc0, psc1 = self.st, self.pst, self.psc0, self.psc1
        sy, te, ve, se, gp = nc.sync, nc.tensor, nc.vector, nc.scalar, nc.gpsimd
        A.alloc("attn_fm", 16384, at=16384)
        A.alloc("atld", 4096, at=49152)
        self.fence()
        self.wait(sy, "dho0")
        self.wait(sy, "dho1")
        ar_dma = [0] * 8
        tr_done_st = [0] * 8

        def issue_ar(j):
            par = j % 2
            if j >= 2:
                sy.wait_ge(self.sem["st"], tr_done_st[j - 2])
                sy.wait_ge(self.sem["dxb%d" % par], ar_dma[j - 2])
            ar_dma[j] = self.DMA(sy, "dxb%d" % par,
                                 A.bf("atld", par * 2048, (par + 1) * 2048),
                                 self.attn_dram[j * 128:(j + 1) * 128, :])

        issue_ar(0)
        issue_ar(1)
        fmr = A.bf("attn_fm").rearrange("p (h c) -> p h c", c=1024)
        for j in range(8):
            par = j % 2
            te.wait_ge(self.sem["dxb%d" % par], ar_dma[j])
            for hg in range(4):
                self.tp_group(
                    [A.bf("atld", par * 2048 + (hg * 4 + hh) * 128,
                          par * 2048 + (hg * 4 + hh + 1) * 128) for hh in range(4)],
                    fmr[:, hg * 4:hg * 4 + 4, j * 128:(j + 1) * 128])
            tr_done_st[j] = self.cnt["st"]
            if j + 2 < 8:
                issue_ar(j + 2)
        A.alloc("attn_out", 16384, at=0)
        A.alloc("wo_bf", 16384, at=32768)

        wo_dma = [0] * 4
        wo_done_st = [0] * 4
        ev_sv = 0
        ev_prev2 = 0

        def issue_wo(og):
            par = og % 2
            if og >= 2:
                gp.wait_ge(self.sem["st"], wo_done_st[og - 2])
                gp.wait_ge(self.sem["dwq%d" % par], wo_dma[og - 2])
            else:
                self.wait(gp, "sv")
                self.wait(gp, "ss")
            for k in range(NKCH):
                self.DMA(gp, "dwq%d" % par,
                         A.bf("wo_bf", par * 8192 + k * 512, par * 8192 + (k + 1) * 512),
                         self.wo[k * 128:(k + 1) * 128, og * 512:(og + 1) * 512])
            wo_dma[og] = self.cnt["dwq%d" % par]

        issue_wo(0)
        issue_wo(1)
        for og in range(4):
            par = og % 2
            te.wait_ge(self.sem["dwq%d" % par], wo_dma[og])
            for tt in range(8):
                slot = tt % 2
                ps = psc0[:, slot * 512:(slot + 1) * 512]
                if ev_prev2:
                    te.wait_ge(self.sem["sv"], ev_prev2)
                stc = None
                for k in range(NKCH):
                    stc = MM(ps, A.bf("attn_fm", k * 1024 + tt * 128, k * 1024 + (tt + 1) * 128),
                             A.bf("wo_bf", par * 8192 + k * 512, par * 8192 + (k + 1) * 512),
                             k == 0, k == NKCH - 1, inc=(k == NKCH - 1))
                self.wait(ve, "st", stc)
                ev_prev2 = ev_sv
                ev_sv = V("tensor_copy",
                          A.bf("attn_out", tt * 2048 + og * 512, tt * 2048 + (og + 1) * 512),
                          ps)
            wo_done_st[og] = self.cnt["st"]
            if og + 2 < 4:
                issue_wo(og + 2)
        A.free("attn_fm")
        A.free("wo_bf")
        A.free("atld")
        if self.dbg:
            self.dbg_dump("dbg_attnout", [128, 16384], A.bf("attn_out"), waits=("sv",))

    def phase5(self):
        nc, A, V, SC, G, MM = self.nc, self.A, self.V, self.SC, self.G, self.MM
        st, pst = self.st, self.pst
        sy, te, ve, se, gp = nc.sync, nc.tensor, nc.vector, nc.scalar, nc.gpsimd
        FA = self.FA
        self.wait(te, "sv")
        self.wait(te, "ss")
        stc = MM(pst[0:1, 0:3], st(self.S_ONE), self.stats[:, self.S_AW:self.S_AW + 3],
                 True, True, inc=True)
        self.wait(ve, "st", stc)
        row = self.stats[0:1, self.S_PR:self.S_PR + 16]
        V("tensor_scalar", row[:, 0:3], pst[0:1, 0:3], 1.0 / (HF * D), None, ALU.mult)
        V("tensor_scalar", row[:, 0:3], row[:, 0:3], 1e-5, None, ALU.max)
        V("tensor_scalar", row[:, 3:6], row[:, 0:3], 0.5, None, ALU.mult)
        V("tensor_scalar", row[:, 6:9], row[:, 3:6], -1.0, None, ALU.mult)
        self.wait(te, "sv")
        stc = MM(pst[:, 0:9], self.stats[0:1, self.S_ONESROW:self.S_ONESROW + 128],
                 row[:, 0:9], True, True, inc=True)
        self.wait(ve, "st", stc)
        V("tensor_copy", self.stats[:, self.S_SW:self.S_SW + 9], pst[:, 0:9])
        for j in range(8):
            V("tensor_mul", st(self.S_C1 + j), st(self.S_SXI + j), st(self.S_SW))
            V("tensor_mul", st(self.S_C2 + j), st(self.S_SXI + j), st(self.S_SW + 1))
        FA.free("wmba")
        FA.free("wmbb")
        FA.free("wmbc")

    def phase6(self):
        nc, A, V, SC, G, MM = self.nc, self.A, self.V, self.SC, self.G, self.MM
        st, pst, psc0, psc1 = self.st, self.pst, self.psc0, self.psc1
        sy, te, ve, se, gp = nc.sync, nc.tensor, nc.vector, nc.scalar, nc.gpsimd
        A.alloc("iw1T", 16384, at=16384)
        A.alloc("iw2T", 16384, at=32768)
        A.alloc("ixq", 16384, at=49152)
        A.alloc("wqz", 4096, at=65536)
        A.alloc("wcmpb", 1024, at=69632)
        FA = self.FA
        FA.alloc("wch1", 3 * 1024)
        FA.alloc("wch2", 2 * 1024)
        FA.alloc("ma2", 512)
        FA.alloc("mh0", 512)
        FA.alloc("mh1", 512)
        self.fence()
        self.DMA(gp, "dwg", A.bf("ixq"), self.ixq_dram[:, :])
        ixq_cnt = self.cnt["dwg"]
        te.wait_ge(self.sem["dwg"], ixq_cnt)
        V("memset", self.stats[:, self.S_HAM:self.S_HAM + 8], 0.0)

        wsrc = {1: self.w1, 2: self.w2}
        wchn = {1: "wch1", 2: "wch2"}
        wq_eng = {1: sy, 2: se}
        wq_sem = {1: ("dwsy0", "dwsy1"), 2: ("dws0", "dws1")}
        iwt = {1: "iw1T", 2: "iw2T"}
        thr_col = {1: self.S_SW + 3, 2: self.S_SW + 4}
        nthr_col = {1: self.S_SW + 6, 2: self.S_SW + 7}
        wdma = {1: [0] * 2 * NW1CH, 2: [0] * 2 * NW1CH}
        wcons_sv = {1: [0] * 2 * NW1CH, 2: [0] * 2 * NW1CH}
        wtrans_sv = {}
        qz_last_st = [0, 0]
        mm_done_st = {}

        nslot = {1: 3, 2: 2}

        def issue_w(w, mm):
            slot = mm % nslot[w]
            par = mm % 2
            e = wq_eng[w]
            ns = nslot[w]
            if mm >= ns:
                e.wait_ge(self.sem["sv"], wcons_sv[w][mm - ns])
            if mm >= 2:
                e.wait_ge(self.sem[wq_sem[w][par]], wdma[w][mm - 2])
            if mm < ns:
                self.wait(e, "sv")
                self.wait(e, wq_sem[w][par])
            m, hf = divmod(mm, 2)
            wdma[w][mm] = self.DMA(e, wq_sem[w][par],
                                   FA.r(wchn[w], slot * 1024, (slot + 1) * 1024),
                                   wsrc[w][m * 128:(m + 1) * 128,
                                           hf * 1024:(hf + 1) * 1024])

        for w in (1, 2):
            for _i in range(nslot[w]):
                issue_w(w, _i)

        h_spill = {}
        for og in range(11):
            iwpar = og % 2
            for w in (1, 2):
                for rc in range(4):
                    m = og * 4 + rc
                    qz = A.bf("wqz", (m % 2) * 2048, (m % 2 + 1) * 2048)
                    if qz_last_st[m % 2]:
                        ve.wait_ge(self.sem["st"], qz_last_st[m % 2])
                    for hf in range(2):
                        mm = 2 * m + hf
                        slot = mm % nslot[w]
                        par = mm % 2
                        ve.wait_ge(self.sem[wq_sem[w][par]], wdma[w][mm])
                        wc = FA.r(wchn[w], slot * 1024, (slot + 1) * 1024)
                        V("tensor_scalar", A.bf("wcmpb"), wc,
                          self.stats[:, nthr_col[w]:nthr_col[w] + 1], None, ALU.is_lt)
                        wcons_sv[w][mm] = V(
                            "scalar_tensor_tensor", qz[:, hf * 1024:(hf + 1) * 1024],
                            wc, self.stats[:, thr_col[w]:thr_col[w] + 1], A.bf("wcmpb"),
                            ALU.is_gt, ALU.subtract)
                        if mm + nslot[w] < 2 * NW1CH:
                            issue_w(w, mm + nslot[w])
                    dstr = A.bf(iwt[w], iwpar * 8192, (iwpar + 1) * 8192) \
                        .rearrange("p (k c) -> p k c", c=512)
                    te.wait_ge(self.sem["sv"], wcons_sv[w][2 * m + 1])
                    for kb in range(0, NKCH, 4):
                        if og >= 2 and (w, og - 2) in mm_done_st:
                            te.wait_ge(self.sem["st"], mm_done_st[(w, og - 2)])
                        wtrans_sv[(w, m)] = self.tp_group(
                            [qz[:, (kb + kk) * 128:(kb + kk + 1) * 128]
                             for kk in range(4)],
                            dstr[:, kb:kb + 4, rc * 128:(rc + 1) * 128])
                    qz_last_st[m % 2] = self.cnt["st"]
            for tt in range(8):
                slot = tt % 2
                for w, psb in ((1, psc0), (2, psc1)):
                    ps = psb[:, slot * 512:(slot + 1) * 512]
                    te.wait_ge(self.sem["ss"], wtrans_sv[(w, og * 4 + 3)])
                    if (w, og, tt, "evac") in h_spill:
                        pass
                    if tt >= 2:
                        te.wait_ge(self.sem["ss" if w == 1 else "sv"],
                                   h_spill[(w, og, tt - 2)])
                    stc = None
                    for k in range(NKCH):
                        stc = MM(ps, A.bf("ixq", k * 1024 + tt * 128, k * 1024 + (tt + 1) * 128),
                                 A.bf(iwt[w], iwpar * 8192 + k * 512, iwpar * 8192 + (k + 1) * 512),
                                 k == 0, k == NKCH - 1, inc=(k == NKCH - 1))
                    mm_done_st[(w, og, tt)] = self.cnt["st"]
                mm_done_st[(1, og)] = mm_done_st[(1, og, tt)]
                mm_done_st[(2, og)] = mm_done_st[(2, og, tt)]
                self.wait(se, "st", mm_done_st[(1, og, tt)])
                self.wait(se, "sv")
                h_spill[(1, og, tt)] = SC(
                    "activation", self.ppv[:, 512:1024], psc0[:, slot * 512:(slot + 1) * 512],
                    ACT.Silu, bias=0.0,
                    scale=self.stats[:, self.S_C1 + tt:self.S_C1 + tt + 1])
                self.wait(ve, "st", mm_done_st[(2, og, tt)])
                h_spill[(2, og, tt)] = V(
                    "tensor_scalar", FA.r("ma2"), psc1[:, slot * 512:(slot + 1) * 512],
                    self.stats[:, self.S_C2 + tt:self.S_C2 + tt + 1], None, ALU.mult)
                gi = og * 8 + tt
                mhp = gi % 2
                mh = FA.r("mh0" if mhp == 0 else "mh1")
                if gi >= 2:
                    po, ptt = divmod(gi - 2, 8)
                    ve.wait_ge(self.sem["dho%d" % mhp], h_spill.get(("dma", po, ptt), 0))
                self.wait(ve, "ss")
                V("tensor_mul", mh, FA.r("ma2"), self.ppv[:, 512:1024])
                V("reduce_max", st(self.S_T0), mh, AX, apply_absolute_value=True)
                hs = V("tensor_max", st(self.S_HAM + tt), st(self.S_HAM + tt), st(self.S_T0))
                gp.wait_ge(self.sem["sv"], hs)
                h_spill[("dma", og, tt)] = self.DMA(
                    gp, "dho%d" % mhp, self.h_dram[tt * 128:(tt + 1) * 128,
                                                   og * 512:(og + 1) * 512], mh)
        self.wait(ve, "sg")
        for j in range(8):
            V("tensor_scalar", st(self.S_T0), st(self.S_HAM + j), 1e-5, None, ALU.max)
            V("reciprocal", st(self.S_T0 + 1), st(self.S_T0))
            V("tensor_scalar", st(self.S_SXH + j), st(self.S_T0 + 1), 127.0, None, ALU.mult)
            V("tensor_scalar", st(self.S_T0), st(self.S_T0), 1.0 / 127.0, None, ALU.mult)
            V("tensor_mul", st(self.S_S3 + j), st(self.S_T0), st(self.S_SW + 2))
        for nm in ("iw1T", "iw2T", "wqz", "ixq"):
            A.free(nm)
        A.free("wcmpb")
        for nm in ("wch1", "wch2", "ma2", "mh0", "mh1"):
            FA.free(nm)

    def phase7(self):
        nc, A, V, SC, G, MM = self.nc, self.A, self.V, self.SC, self.G, self.MM
        st, pst, psc0 = self.st, self.pst, self.psc0
        sy, te, ve, se, gp = nc.sync, nc.tensor, nc.vector, nc.scalar, nc.gpsimd
        A.alloc("ihq", 2048, at=83968)
        A.alloc("wqz3", 1024, at=86016)
        A.alloc("ih", 45056, at=16384)
        A.alloc("w3T", 22528, at=61440)
        FA = self.FA
        FA.alloc("hrl", 4 * 512)
        FA.alloc("z7p", 4 * 512)
        FA.alloc("f7p", 4 * 512)
        FA.alloc("t7p", 512)
        self.IA.alloc("i7", 512)
        self.fence()
        for e in (se, sy, ve, te, gp):
            self.wait(e, "dho0")
            self.wait(e, "dho1")
        self.wait(sy, "dwsy0")
        self.wait(sy, "dwsy1")
        self.wait(sy, "dxb0")
        self.wait(sy, "dxb1")
        self.wait(se, "dhi0")
        self.wait(se, "dhi1")

        hdma = {}
        tr_sv = {}
        rnd_sv = {}
        ihr = A.bf("ih").rearrange("p (k c) -> p k c", c=1024)

        def issue_h(og, tt):
            i = og * 8 + tt
            par = i % 2
            q4 = i % 4
            if i >= 2:
                se.wait_ge(self.sem["dhi%d" % par], hdma.get(i - 2, 0))
            if i >= 4:
                se.wait_ge(self.sem["ss"], hdma.get(("cons", i - 4), 0))
            hdma[i] = self.DMA(se, "dhi%d" % par,
                               FA.r("hrl", q4 * 512, (q4 + 1) * 512),
                               self.h_dram[tt * 128:(tt + 1) * 128,
                                           og * 512:(og + 1) * 512])

        for _w in range(4):
            issue_h(_w // 8, _w % 8)
        for og in range(11):
            for tt in range(8):
                i = og * 8 + tt
                par = i % 2
                q4 = i % 4
                se.wait_ge(self.sem["dhi%d" % par], hdma[i])
                if i >= 4:
                    se.wait_ge(self.sem["sv"], rnd_sv[i - 4])
                z7 = FA.r("z7p", q4 * 512, (q4 + 1) * 512)
                f7 = FA.r("f7p", q4 * 512, (q4 + 1) * 512)
                t7 = FA.r("t7p")
                hdma[("cons", i)] = SC(
                    "activation", z7, FA.r("hrl", q4 * 512, (q4 + 1) * 512),
                    ACT.Copy, bias=0.0,
                    scale=self.stats[:, self.S_SXH + tt:self.S_SXH + tt + 1])
                SC("activation", self.IA.r("i7"), z7, ACT.Copy)
                f7ss = SC("activation", f7, self.IA.r("i7"), ACT.Copy)
                if i >= 4:
                    ve.wait_ge(self.sem["st"], tr_sv[("st", i - 4)])
                ve.wait_ge(self.sem["ss"], f7ss)
                ihq = A.bf("ihq", q4 * 512, (q4 + 1) * 512)
                V("scalar_tensor_tensor", t7, z7, -0.5, f7, ALU.add, ALU.is_gt)
                V("scalar_tensor_tensor", z7, z7, 0.5, f7, ALU.add, ALU.is_lt)
                V("tensor_sub", t7, t7, z7)
                rnd_sv[i] = V("tensor_add", ihq, f7, t7)
                if i + 4 < 88:
                    issue_h((i + 4) // 8, (i + 4) % 8)
                te.wait_ge(self.sem["sv"], rnd_sv[i])
                tr_sv[i] = self.tp_group(
                    [ihq[:, kk * 128:(kk + 1) * 128] for kk in range(4)],
                    ihr[:, og * 4:og * 4 + 4, tt * 128:(tt + 1) * 128])
                tr_sv[("st", i)] = self.cnt["st"]
        FA.free("hrl")
        FA.free("z7p")
        FA.free("f7p")
        FA.free("t7p")
        self.IA.free("i7")

        FA.alloc("wch3", 4 * 512)
        FA.alloc("w3s", 512)
        FA.alloc("xsl", 2 * 512)
        FA.alloc("f1", 2 * 512)
        FA.alloc("obuf", 2 * 512)
        w3dma = {}
        w3q_sv = {}
        w3t_st = {}
        out_dma = {}
        w3r = A.bf("w3T").rearrange("p (k c) -> p k c", c=512)
        w3q_eng = (sy, gp)
        w3q_sem = (("dwsy0", "dwsy1"), ("dw3a", "dw3b"))
        for og3 in range(4):
            for mr in range(4):
                m = og3 * 4 + mr
                for sl in range(11):
                    s = m * 11 + sl
                    q4 = s % 4
                    qi = s % 2
                    qe = w3q_eng[qi]
                    qs = w3q_sem[qi][(s // 2) % 2]
                    if s >= 4:
                        qe.wait_ge(self.sem[qs], w3dma.get(s - 4, 0))
                        qe.wait_ge(self.sem["sv"], w3q_sv.get(s - 4, 0))
                    else:
                        self.wait(qe, "ss")
                        self.wait(qe, "sv")
                    w3dma[s] = self.DMA(
                        qe, qs, FA.r("wch3", q4 * 512, (q4 + 1) * 512),
                        self.w3[m * 128:(m + 1) * 128, sl * 512:(sl + 1) * 512])
                    ve.wait_ge(self.sem[qs], w3dma[s])
                    wc = FA.r("wch3", q4 * 512, (q4 + 1) * 512)
                    qz = A.bf("wqz3", qi * 512, (qi + 1) * 512)
                    if s >= 2:
                        ve.wait_ge(self.sem["st"], w3t_st.get(s - 2, 0))
                    V("tensor_scalar", FA.r("w3s"), wc,
                      self.stats[:, self.S_SW + 8:self.S_SW + 9], None, ALU.is_lt)
                    w3q_sv[s] = V(
                        "scalar_tensor_tensor", qz, wc,
                        self.stats[:, self.S_SW + 5:self.S_SW + 6], FA.r("w3s"),
                        ALU.is_gt, ALU.subtract)
                    te.wait_ge(self.sem["sv"], w3q_sv[s])
                    self.tp_group(
                        [qz[:, kk * 128:(kk + 1) * 128] for kk in range(4)],
                        w3r[:, sl * 4:sl * 4 + 4, mr * 128:(mr + 1) * 128])
                    w3t_st[s] = self.cnt["st"]
            self.wait(te, "sv")
            self.wait(te, "ss")
            for tt in range(8):
                i = og3 * 8 + tt
                slot = tt % 2
                ps = psc0[:, slot * 512:(slot + 1) * 512]
                xpar = i % 2
                if i >= 2:
                    sy.wait_ge(self.sem["sv"], out_dma.get(("xcons", i - 2), 0))
                    sy.wait_ge(self.sem["dxb%d" % xpar], out_dma.get(("x", i - 2), 0))
                out_dma[("x", i)] = self.DMA(
                    sy, "dxb%d" % xpar, FA.r("xsl", xpar * 512, (xpar + 1) * 512),
                    self.x_own[tt * 128:(tt + 1) * 128, og3 * 512:(og3 + 1) * 512])
                if i >= 2:
                    te.wait_ge(self.sem["ss"], out_dma.get(("evac", i - 2), 0))
                stc = None
                for k in range(44):
                    stc = MM(ps, ihr[:, k, tt * 128:(tt + 1) * 128],
                             A.bf("w3T", k * 512, (k + 1) * 512),
                             k == 0, k == 43, inc=(k == 43))
                fpar = i % 2
                f1ap = FA.r("f1", fpar * 512, (fpar + 1) * 512)
                self.wait(se, "st", stc)
                if i >= 2:
                    se.wait_ge(self.sem["sv"], out_dma.get(("xcons", i - 2), 0))
                out_dma[("evac", i)] = SC(
                    "activation", f1ap, ps, ACT.Copy, bias=0.0,
                    scale=self.stats[:, self.S_S3 + tt:self.S_S3 + tt + 1])
                obp = i % 2
                if i >= 2:
                    ve.wait_ge(self.sem["do%d" % obp], out_dma.get(("out", i - 2), 0))
                ve.wait_ge(self.sem["ss"], out_dma[("evac", i)])
                ve.wait_ge(self.sem["dxb%d" % xpar], out_dma[("x", i)])
                V("tensor_add", f1ap, f1ap,
                  A.bf("attn_out", tt * 2048 + og3 * 512, tt * 2048 + (og3 + 1) * 512))
                out_dma[("xcons", i)] = V(
                    "tensor_add", FA.r("obuf", obp * 512, (obp + 1) * 512),
                    f1ap, FA.r("xsl", xpar * 512, (xpar + 1) * 512))
                sy.wait_ge(self.sem["sv"], out_dma[("xcons", i)])
                out_dma[("out", i)] = self.DMA(
                    sy, "do%d" % obp, self.out[tt * 128:(tt + 1) * 128,
                                               og3 * 512:(og3 + 1) * 512],
                    FA.r("obuf", obp * 512, (obp + 1) * 512))
        self.wait(sy, "do0")
        self.wait(sy, "do1")


def rope_tables():
    import ctypes
    libm = ctypes.CDLL("libm.so.6")
    libm.powf.restype = ctypes.c_float
    libm.powf.argtypes = [ctypes.c_float, ctypes.c_float]
    libm.cosf.restype = ctypes.c_float
    libm.cosf.argtypes = [ctypes.c_float]
    libm.sinf.restype = ctypes.c_float
    libm.sinf.argtypes = [ctypes.c_float]
    ex = np.arange(0, DR, 2, dtype=np.float32) / np.float32(DR)
    inv = np.array([np.float32(1.0) / np.float32(libm.powf(ROPE_BASE, float(e)))
                    for e in ex], dtype=np.float32)
    t32 = np.arange(S, dtype=np.float32)
    fr = (t32[:, None] * inv[None, :]).astype(np.float32)
    cs = np.fromiter((libm.cosf(float(v)) for v in fr.ravel()),
                     np.float32, fr.size).reshape(S, 32)
    sn = np.fromiter((libm.sinf(float(v)) for v in fr.ravel()),
                     np.float32, fr.size).reshape(S, 32)
    sink = np.ascontiguousarray(np.tile(sn.T, (4, 1)))
    cosk = np.ascontiguousarray(np.tile(cs.T, (4, 1)))
    return sink, cosk


def host_prep(inputs):
    Wqkv = np.asarray(inputs["L0_attn_Wqkv"], dtype=np.float32)
    Wk_up = np.asarray(inputs["L0_attn_Wk_up"], dtype=np.float32)
    qn = [h * DQK + d for h in range(NH) for d in range(DN)]
    qr = []
    for p in range(8):
        for hl in (EVENG, ODDG):
            hh = hl[p]
            qr += [hh * DQK + DN + r for r in range(DR)]
    cols = qn + qr + list(range(NH * DQK, NH * DQK + LORA))
    Wp = np.ascontiguousarray(Wqkv[:, cols])
    wq_t = np.ascontiguousarray(
        Wp[:, :24 * 128].reshape(NKCH, 128, 24, 128).transpose(2, 1, 0, 3))
    wckv_t = np.ascontiguousarray(
        Wp[:, 24 * 128:].reshape(NKCH, 128, 4, 128).transpose(2, 1, 0, 3))
    kn = [g * DQK + d for g in range(NKV) for d in range(DN)]
    kr = []
    for gp_ in range(2):
        for half in range(2):
            g = gp_ * 2 + half
            kr += [g * DQK + DN + r for r in range(DR)]
    wk_p = np.ascontiguousarray(Wk_up[:, kn + kr])
    return wq_t, wckv_t, wk_p


def make_in_maps(inputs):
    wq_t, wckv_t, wk_p = host_prep(inputs)
    sink, cosk = rope_tables()
    tabk = np.ascontiguousarray(np.concatenate([sink, cosk], axis=1))
    x = np.asarray(inputs["L0_attn_hidden_states"], dtype=np.float32)
    ims = []
    for core in range(8):
        b, c = core // 4, core % 4
        tiles = owned_tiles(c)
        rows = np.concatenate([np.arange(t * 128, (t + 1) * 128) for t in tiles])
        ims.append(dict(
            x_own=np.ascontiguousarray(x[b][rows]),
            x_batch=np.ascontiguousarray(x[b]),
            tabk=tabk,
            tabq=np.ascontiguousarray(
                np.concatenate([sink[:, rows], cosk[:, rows]], axis=1)),
            meta=np.array([[c, 0, 0, 0]], dtype=np.int32),
            wq=wq_t, wckv=wckv_t, wk=wk_p,
            wv=np.ascontiguousarray(np.asarray(inputs["L0_attn_Wv_up"], dtype=np.float32)),
            wo=np.ascontiguousarray(np.asarray(inputs["L0_attn_Wo"], dtype=np.float32)),
            w1=np.ascontiguousarray(np.asarray(inputs["L0_ffn_w1"], dtype=np.float32)),
            w2=np.ascontiguousarray(np.asarray(inputs["L0_ffn_w2"], dtype=np.float32)),
            w3=np.ascontiguousarray(np.asarray(inputs["L0_ffn_w3"], dtype=np.float32)),
        ))
    return ims


def build_and_run(inputs, phases=8, dbg=False, cores=None):
    ims = make_in_maps(inputs)
    b = Bld(phases, dbg)
    nc = b.build()
    res = run_bass_kernel_spmd(nc, ims, core_ids=list(range(8)))
    return b, res


def kernel(**inputs):
    ims = make_in_maps(inputs)
    b = Bld(8, False)
    nc = b.build()
    res = run_bass_kernel_spmd(nc, ims, core_ids=list(range(8)))
    out = np.zeros((B, S, D), dtype=np.float32)
    for core in range(8):
        bb, c = core // 4, core % 4
        for j, t in enumerate(owned_tiles(c)):
            out[bb, t * 128:(t + 1) * 128, :] = res.results[core]["out"][j * 128:(j + 1) * 128, :]
    return out


# version 19
# speedup vs baseline: 1.0992x; 1.0481x over previous
import math
import numpy as np
import concourse.bass as bass
import concourse.mybir as mybir
from concourse.bass_utils import run_bass_kernel_spmd
from contextlib import ExitStack

F32 = mybir.dt.float32
BF16 = mybir.dt.bfloat16
I32 = mybir.dt.int32
ALU = mybir.AluOpType
ACT = mybir.ActivationFunctionType
AX = mybir.AxisListType.X

B, S, D = 2, 4096, 2048
NH, NKV, LORA = 16, 4, 512
DR, DN, DVh, DQK = 64, 128, 128, 192
HF = 5632
TOWN = 1024
NKCH = D // 128
SCL = 1.0 / math.sqrt(DQK)
EPS = 1e-5
ROPE_BASE = 10000.0
NW1CH = HF // 128

EVENG = [0, 1, 2, 3, 8, 9, 10, 11]
ODDG = [4, 5, 6, 7, 12, 13, 14, 15]
QR_PAIR = {}
for _i, _h in enumerate(EVENG):
    QR_PAIR[_h] = (_i, 0)
for _i, _h in enumerate(ODDG):
    QR_PAIR[_h] = (_i, 64)


def owned_tiles(c):
    return list(range(4 * c, 4 * c + 4)) + list(range(4 * (7 - c), 4 * (7 - c) + 4))


class Arena:
    def __init__(self, ap, n):
        self.ap, self.n, self.live = ap, n, {}

    def alloc(self, name, elems, align=32, at=None):
        assert name not in self.live, name
        elems = -(-elems // align) * align
        if at is not None:
            for so, se in self.live.values():
                assert at + elems <= so or at >= so + se, \
                    f"arena overlap: {name}@{at}+{elems} vs {so}+{se}"
            assert at + elems <= self.n
            self.live[name] = (at, elems)
            return
        off = 0
        for so, se in sorted(self.live.values()):
            if off + elems <= so:
                break
            off = max(off, so + se)
        assert off + elems <= self.n, f"arena OOM: {name} {elems}@{off} live={self.live}"
        self.live[name] = (off, elems)

    def free(self, name):
        del self.live[name]

    def bf(self, name, lo=0, hi=None):
        off, n = self.live[name]
        return self.ap[:, off + lo: off + (n if hi is None else hi)]

    r = bf


class Bld:

    def __init__(self, phases, dbg):
        self.phases, self.dbg = phases, dbg
        self.nc = bass.Bass("TRN2", target_bir_lowering=False)
        self.sem, self.cnt = {}, {}
        self.dbg_t = {}
        self._tp_evac_sv = [0, 0]

    def mksem(self, *names):
        for n in names:
            self.sem[n] = self.stack.enter_context(self.nc.semaphore(n))
            self.cnt[n] = 0

    def inc(self, instr, s, n=1):
        instr.then_inc(self.sem[s], n)
        self.cnt[s] += n
        return self.cnt[s]

    def wait(self, eng, s, val=None):
        v = self.cnt[s] if val is None else val
        if v > 0:
            eng.wait_ge(self.sem[s], v)

    def _chain(self, eng, sname, fn, *a, **kw):
        if self.cnt[sname] > 0:
            eng.wait_ge(self.sem[sname], self.cnt[sname])
        instr = fn(*a, **kw)
        instr.then_inc(self.sem[sname], 1)
        self.cnt[sname] += 1
        return self.cnt[sname]

    def V(self, op, *a, **kw):
        return self._chain(self.nc.vector, "sv", getattr(self.nc.vector, op), *a, **kw)

    def SC(self, op, *a, **kw):
        return self._chain(self.nc.scalar, "ss", getattr(self.nc.scalar, op), *a, **kw)

    def G(self, op, *a, **kw):
        return self._chain(self.nc.gpsimd, "sg", getattr(self.nc.gpsimd, op), *a, **kw)

    def MM(self, out, lhsT, rhs, start, stop, inc=False):
        instr = self.nc.tensor.matmul(out, lhsT, rhs, start=start, stop=stop)
        if inc:
            return self.inc(instr, "st", 1)
        return None

    def DMA(self, q, s, out, in_):
        instr = q.dma_start(out, in_)
        instr.then_inc(self.sem[s], 16)
        self.cnt[s] += 16
        return self.cnt[s]

    def dbg_out(self, name, shape, dtype=F32):
        if name not in self.dbg_t:
            self.dbg_t[name] = self.nc.dram_tensor(name, shape, dtype,
                                                   kind="ExternalOutput").ap()
        return self.dbg_t[name]

    def dbg_dump(self, name, shape, src_ap, waits=()):
        o = self.dbg_out(name, shape)
        cast = src_ap.dtype != F32
        e = self.nc.gpsimd if cast else self.nc.sync
        s = "dbgo" if cast else "dout"
        for w in waits:
            self.wait(e, w)
        self.DMA(e, s, o[:, :], src_ap)

    def build(self):
        nc = self.nc
        self.x_own = nc.dram_tensor("x_own", [TOWN, D], F32, kind="ExternalInput").ap()
        self.x_batch = nc.dram_tensor("x_batch", [S, D], F32, kind="ExternalInput").ap()
        self.tabk = nc.dram_tensor("tabk", [128, 2 * S], F32, kind="ExternalInput").ap()
        self.tabq = nc.dram_tensor("tabq", [128, 2 * TOWN], F32, kind="ExternalInput").ap()
        self.meta = nc.dram_tensor("meta", [1, 4], I32, kind="ExternalInput").ap()
        self.wq = nc.dram_tensor("wq", [24, 128, NKCH, 128], F32, kind="ExternalInput").ap()
        self.wckv = nc.dram_tensor("wckv", [4, 128, NKCH, 128], F32, kind="ExternalInput").ap()
        self.wk = nc.dram_tensor("wk", [LORA, 768], F32, kind="ExternalInput").ap()
        self.wv = nc.dram_tensor("wv", [LORA, 512], F32, kind="ExternalInput").ap()
        self.wo = nc.dram_tensor("wo", [D, D], F32, kind="ExternalInput").ap()
        self.w1 = nc.dram_tensor("w1", [HF, D], F32, kind="ExternalInput").ap()
        self.w2 = nc.dram_tensor("w2", [HF, D], F32, kind="ExternalInput").ap()
        self.w3 = nc.dram_tensor("w3", [D, HF], F32, kind="ExternalInput").ap()
        self.out = nc.dram_tensor("out", [TOWN, D], F32, kind="ExternalOutput").ap()
        self.h_dram = nc.dram_tensor("h_scratch", [TOWN, HF], F32).ap()
        self.attn_dram = nc.dram_tensor("attn_scratch", [TOWN, D], BF16).ap()

        with ExitStack() as stack:
            self.stack = stack
            ARENA_N = 89344
            FA_N = 7168
            IA_N = 512
            at = stack.enter_context(nc.sbuf_tensor("arena", [128, ARENA_N], BF16))
            self.A = Arena(at[:, :], ARENA_N)
            ft = stack.enter_context(nc.sbuf_tensor("farena", [128, FA_N], F32))
            self.FA = Arena(ft[:, :], FA_N)
            it = stack.enter_context(nc.sbuf_tensor("iarena", [128, IA_N], I32))
            self.IA = Arena(it[:, :], IA_N)
            self.consts = stack.enter_context(nc.sbuf_tensor("consts", [128, 320], BF16))
            self.stats = stack.enter_context(nc.sbuf_tensor("stats", [128, 288], F32))
            self.meta_sb = stack.enter_context(nc.sbuf_tensor("meta_sb", [1, 4], I32))
            self.psc0 = stack.enter_context(nc.psum_tensor("psc0", [128, 1024], F32))
            self.psc1 = stack.enter_context(nc.psum_tensor("psc1", [128, 1024], F32))
            self.ppv = stack.enter_context(nc.psum_tensor("ppv", [128, 1536], F32))
            self.pst = stack.enter_context(nc.psum_tensor("pst", [128, 512], F32))
            self.mksem("sv", "ss", "sg", "st", "dxb0", "dxb1", "dmeta", "dwg",
                       "dws0", "dws1", "dwsy0", "dwsy1", "dho0", "dho1",
                       "dhi0", "dhi1", "dout", "do0", "do1", "dw3a", "dw3b", "dbgo",
                       "sp3", "dwq0", "dwq1")
            self.I_bf = self.consts[:, 0:128]
            self.R = self.consts[:, 128:192]
            self.TRI = self.consts[:, 192:320]
            self.phase0()
            if self.phases >= 1:
                self.phase1()
            if self.phases >= 2:
                self.phase2()
            if self.phases >= 3:
                self.phase3()
            if self.phases >= 4:
                self.phase4()
            if self.phases >= 5:
                self.phase5()
            if self.phases >= 6:
                self.phase6()
            if self.phases >= 7:
                self.phase7()
            self.wait(nc.sync, "dout")
            self.wait(nc.gpsimd, "dbgo")
            self.wait(nc.gpsimd, "dwg")
        return nc

    S_ONE = 0
    S_OROW = 1
    S_LN = 4
    S_AM = 12
    S_SXQ = 16
    S_SXI = 24
    S_NB = 32
    S_QM = 48
    S_KM = 64
    S_QMR = 68
    S_KMR = 76
    S_PR = 80
    S_UHM = 96
    S_HM = 97
    S_LI = 98
    S_AW = 100
    S_SW = 104
    S_C1 = 116
    S_C2 = 124
    S_HAM = 132
    S_SXH = 140
    S_S3 = 148
    S_T0 = 156
    S_ONESROW = 160

    def st(self, col, n=1):
        return self.stats[:, col:col + n]

    def fence(self, engs=None):
        nc = self.nc
        engs = engs or (nc.sync, nc.gpsimd, nc.tensor, nc.vector, nc.scalar)
        for e in engs:
            for s in ("sv", "ss", "sg", "dbgo", "dout"):
                self.wait(e, s)
            if e is not nc.tensor:
                self.wait(e, "st")

    def emit_round(self, z, r0i, r0f, t1, out, t2=None):
        SC, V = self.SC, self.V
        se, ve = self.nc.scalar, self.nc.vector
        self.wait(se, "sv")
        SC("activation", r0i, z, ACT.Copy)
        SC("activation", r0f, r0i, ACT.Copy)
        self.wait(ve, "ss")
        V("scalar_tensor_tensor", t1, z, -0.5, r0f, ALU.add, ALU.is_gt)
        lt = z if t2 is None else t2
        V("scalar_tensor_tensor", lt, z, 0.5, r0f, ALU.add, ALU.is_lt)
        V("tensor_sub", t1, t1, lt)
        return V("tensor_add", out, r0f, t1)

    def phase0(self):
        nc, A, FA, V, SC, G, MM = self.nc, self.A, self.FA, self.V, self.SC, self.G, self.MM
        st, pst = self.st, self.pst
        sy, te, ve, se, gp = nc.sync, nc.tensor, nc.vector, nc.scalar, nc.gpsimd

        self.DMA(sy, "dmeta", self.meta_sb[:, :], self.meta[:, :])

        G("memset", st(self.S_ONE), 1.0)
        G("memset", self.stats[0:1, self.S_ONESROW:self.S_ONESROW + 128], 1.0)
        FA.alloc("cset", 128)
        cs = FA.r("cset")
        G("memset", cs, 1.0)
        G("affine_select", cs, cs, [[1, 128]], ALU.is_equal, 0.0,
          base=0, channel_multiplier=-1)
        self.wait(ve, "sg")
        V("tensor_copy", self.I_bf, cs)
        self.wait(gp, "sv")
        G("memset", cs, 1.0)
        G("affine_select", cs, cs, [[1, 128]], ALU.is_ge, 0.0,
          base=0, channel_multiplier=-1)
        self.wait(ve, "sg")
        V("tensor_copy", self.TRI, cs)
        self.wait(gp, "sv")
        G("memset", cs[:, 0:64], 0.0)
        for bp in (0, 64):
            G("memset", cs[bp:bp + 32, 32:64], 1.0)
            G("affine_select", cs[bp:bp + 32, 32:64], cs[bp:bp + 32, 32:64],
              [[1, 32]], ALU.is_equal, 0.0, base=0, channel_multiplier=-1)
            G("memset", cs[bp + 32:bp + 64, 0:32], -1.0)
            G("affine_select", cs[bp + 32:bp + 64, 0:32], cs[bp + 32:bp + 64, 0:32],
              [[1, 32]], ALU.is_equal, 0.0, base=0, channel_multiplier=-1)
        self.wait(ve, "sg")
        V("tensor_copy", self.R, cs[:, 0:64])
        G("memset", st(self.S_HM), 1.0)
        G("affine_select", st(self.S_HM), st(self.S_HM), [[0, 1]], ALU.is_ge, 0.0,
          base=63, channel_multiplier=-1)
        self.wait(ve, "sg")
        V("tensor_scalar", st(self.S_UHM), st(self.S_HM), -1.0, None, ALU.mult)
        V("tensor_scalar", st(self.S_UHM), st(self.S_UHM), 1.0, None, ALU.add)
        FA.free("cset")

        A.alloc("tab_sink", S, at=0)
        A.alloc("tab_cosk", S, at=4096)
        A.alloc("tab_sinq", TOWN, at=8192)
        A.alloc("tab_cosq", TOWN, at=9216)
        self.wait(te, "dmeta")
        self.wait(te, "sg")

    def phase1(self):
        nc, A, V, SC, G, MM = self.nc, self.A, self.V, self.SC, self.G, self.MM
        st, pst, psc0 = self.st, self.pst, self.psc0
        sy, te, ve, se, gp = nc.sync, nc.tensor, nc.vector, nc.scalar, nc.gpsimd

        self.fence()
        A.alloc("ckv", 4 * 4096, at=10240)
        A.alloc("wckv", 4 * 2048, at=26624)
        A.alloc("xnbf", 2 * D, at=34816)
        A.alloc("xngrp", 2 * 8192, at=38912)
        A.alloc("junkb", 2048, at=87296)
        self.FA.alloc("xtm", 2 * D)

        for m in range(4):
            self.DMA(gp, "dwg", A.bf("wckv", m * 2048, (m + 1) * 2048),
                     self.wckv[m].rearrange("p k j -> p (k j)"))
        dwg_ckv = self.cnt["dwg"]
        self.DMA(gp, "dbgo", A.bf("tab_sink"), self.tabk[:, 0:S])
        self.DMA(gp, "dbgo", A.bf("tab_cosk"), self.tabk[:, S:2 * S])
        self.DMA(gp, "dbgo", A.bf("tab_sinq"), self.tabq[:, 0:TOWN])
        self.DMA(gp, "dbgo", A.bf("tab_cosq"), self.tabq[:, TOWN:2 * TOWN])
        self.tab_cnt = self.cnt["dbgo"]

        xn_done_ss = [0] * 32
        x_dma_cnt = [0] * 32
        tp1_st = [0] * 32
        evac_sv = {}

        def issue_x(gt):
            par = gt % 2
            if gt >= 2:
                sy.wait_ge(self.sem["ss"], xn_done_ss[gt - 2])
                sy.wait_ge(self.sem["dxb%d" % par], x_dma_cnt[gt - 2])
            x_dma_cnt[gt] = self.DMA(sy, "dxb%d" % par,
                                     self.FA.r("xtm", par * D, (par + 1) * D),
                                     self.x_batch[gt * 128:(gt + 1) * 128, :])

        issue_x(0)
        issue_x(1)

        for g in range(8):
            for tt in range(4):
                gt = g * 4 + tt
                par = gt % 2
                xt = self.FA.r("xtm", par * D, (par + 1) * D)
                xnb = A.bf("xnbf", par * D, (par + 1) * D)
                ve.wait_ge(self.sem["dxb%d" % par], x_dma_cnt[gt])
                se.wait_ge(self.sem["dxb%d" % par], x_dma_cnt[gt])
                if gt >= 2:
                    se.wait_ge(self.sem["st"], tp1_st[gt - 2])
                xn_done_ss[gt] = self.ln_tile(xt, xnb)
                if gt + 2 < 32:
                    issue_x(gt + 2)
                self.wait(te, "ss", xn_done_ss[gt])
                self.transpose_2048(xnb, self._grp_dst_fn(g % 2, tt))
                tp1_st[gt] = self.cnt["st"]
            te.wait_ge(self.sem["ss"], max(self._tp_evac_sv))
            gsl_lo, gsl_hi = g * 512, (g + 1) * 512
            xng = A.bf("xngrp", (g % 2) * 8192, ((g % 2) + 1) * 8192)
            for m in range(4):
                slot = (m % 2)
                ps = psc0[:, slot * 512:slot * 512 + 512]
                if g == 0 and m == 0:
                    te.wait_ge(self.sem["dwg"], dwg_ckv)
                if (g, m) in evac_sv:
                    pass
                prev = (g, m - 2) if m >= 2 else ((g - 1, m + 2) if g > 0 else None)
                if prev in evac_sv:
                    te.wait_ge(self.sem["sv"], evac_sv[prev])
                stc = None
                for kk in range(NKCH):
                    stc = MM(ps, A.bf("wckv", m * 2048 + kk * 128, m * 2048 + (kk + 1) * 128),
                             xng[:, kk * 512:(kk + 1) * 512],
                             kk == 0, kk == NKCH - 1, inc=(kk == NKCH - 1))
                self.wait(ve, "st", stc)
                evac_sv[(g, m)] = V("tensor_copy",
                                    A.bf("ckv", m * 4096 + gsl_lo, m * 4096 + gsl_hi), ps)
        self._p1_last_evacs = evac_sv
        self._p1_tail = (xn_done_ss[30], xn_done_ss[31], x_dma_cnt[30], x_dma_cnt[31])
        A.free("xngrp")
        A.free("wckv")

        if self.dbg and self.phases < 2:
            self.dbg_dump("dbg_ckv", [128, 4 * 4096], A.bf("ckv"), waits=("sv",))

    def ln_tile(self, xt, xnb_out):
        st, V, SC = self.st, self.V, self.SC
        ve, se = self.nc.vector, self.nc.scalar
        ssum, ssq = st(self.S_LN), st(self.S_LN + 1)
        mean, var = st(self.S_LN + 2), st(self.S_LN + 3)
        rstd = st(self.S_LN + 4)
        V("reduce_sum", ssum, xt, AX)
        self.wait(se, "sv")
        SC("activation", self.A.bf("junkb"), xt, ACT.Square, accum_out=ssq)
        V("tensor_scalar", mean, ssum, 1.0 / D, None, ALU.mult)
        self.wait(ve, "ss")
        V("tensor_scalar", var, ssq, 1.0 / D, None, ALU.mult)
        V("tensor_mul", ssq, mean, mean)
        V("tensor_sub", var, var, ssq)
        V("tensor_scalar", var, var, EPS, None, ALU.add)
        self.wait(se, "sv")
        SC("activation", var, var, ACT.Sqrt)
        self.wait(ve, "ss")
        V("reciprocal", rstd, var)
        V("tensor_scalar", xt, xt, mean, None, ALU.subtract)
        self.wait(se, "sv")
        return SC("activation", xnb_out, xt, ACT.Copy, bias=0.0, scale=rstd)

    def tp_group(self, blocks, dst):
        te, se = self.nc.tensor, self.nc.scalar
        MM, SC = self.MM, self.SC
        buf = self._tp_toggle
        self._tp_toggle ^= 1
        ps = self.pst if buf == 0 else self.ppv
        if self._tp_evac_sv[buf]:
            te.wait_ge(self.sem["ss"], self._tp_evac_sv[buf])
        stc = None
        n = len(blocks)
        for kk, b in enumerate(blocks):
            stc = MM(ps[:, kk * 128:(kk + 1) * 128], b, self.I_bf,
                     True, True, inc=(kk == n - 1))
        self.wait(se, "st", stc)
        src = ps[:, 0:n * 128]
        if len(dst.shape) == 3:
            src = src.rearrange("p (a b) -> p a b", a=n)
        self._tp_evac_sv[buf] = SC("activation", dst, src, ACT.Copy)
        return self._tp_evac_sv[buf]

    def transpose_2048(self, src_bf, dst_fn):
        for kb in range(0, NKCH, 4):
            self.tp_group([src_bf[:, (kb + kk) * 128:(kb + kk + 1) * 128]
                           for kk in range(4)], dst_fn(kb))

    _tp_toggle = 0

    def _grp_dst_fn(self, par, tt):
        base = self.A.bf("xngrp", par * 8192, (par + 1) * 8192)
        r = base.rearrange("p (k c) -> p k c", c=512)

        def fn(kb):
            return r[:, kb:kb + 4, tt * 128:(tt + 1) * 128]
        return fn

    def phase2(self):
        nc, A, V, SC, G, MM = self.nc, self.A, self.V, self.SC, self.G, self.MM
        st, pst, psc0, psc1 = self.st, self.pst, self.psc0, self.psc1
        sy, te, ve, se, gp = nc.sync, nc.tensor, nc.vector, nc.scalar, nc.gpsimd
        self.ixq_dram = nc.dram_tensor("ixq_scratch", [128, 16384], BF16).ap()

        A.alloc("xnown", 16384, at=56576)
        A.alloc("ixq", 16384, at=38912)
        A.alloc("ixbf", 2048, at=26624)
        FA = self.FA
        FA.alloc("zq", 512)
        FA.alloc("rqf", 512)
        FA.alloc("rqt", 512)
        self.IA.alloc("ri", 512)
        self.wait(se, "st")

        def own_dst(name, j):
            base = A.bf(name)
            r = base.rearrange("p (k c) -> p k c", c=1024)

            def fn(kb):
                return r[:, kb:kb + 4, j * 128:(j + 1) * 128]
            return fn

        x_dma = [0] * 8
        xn_ss = [0] * 8
        cons_ss = [0] * 8
        tpx_st = [0] * 8
        tpi_st = [0] * 8

        p1t = self._p1_tail

        def issue_own(j):
            par = j % 2
            if j >= 2:
                sy.wait_ge(self.sem["ss"], cons_ss[j - 2])
                sy.wait_ge(self.sem["dxb%d" % par], x_dma[j - 2])
            else:
                sy.wait_ge(self.sem["ss"], p1t[j])
                sy.wait_ge(self.sem["dxb%d" % par], p1t[2 + j])
            x_dma[j] = self.DMA(sy, "dxb%d" % par,
                                self.FA.r("xtm", par * D, (par + 1) * D),
                                self.x_own[j * 128:(j + 1) * 128, :])

        issue_own(0)
        issue_own(1)
        for j in range(8):
            par = j % 2
            xt = self.FA.r("xtm", par * D, (par + 1) * D)
            xnb = A.bf("xnbf", par * D, (par + 1) * D)
            ve.wait_ge(self.sem["dxb%d" % par], x_dma[j])
            se.wait_ge(self.sem["dxb%d" % par], x_dma[j])
            if j >= 2:
                se.wait_ge(self.sem["st"], tpx_st[j - 2])
            xn_ss[j] = self.ln_tile(xt, xnb)
            am, am2, rr = st(self.S_AM), st(self.S_AM + 1), st(self.S_AM + 2)
            comb = st(self.S_T0)
            self.wait(ve, "ss")
            V("reduce_max", am, xt, AX, apply_absolute_value=True)
            V("tensor_mul", am, am, st(self.S_LN + 4))
            V("tensor_scalar", am2, am, EPS, None, ALU.max)
            V("reciprocal", rr, am2)
            V("tensor_scalar", st(self.S_SXI + j), am2, 1.0 / 127.0, None, ALU.mult)
            V("tensor_mul", comb, rr, st(self.S_LN + 4))
            V("tensor_scalar", comb, comb, 127.0, None, ALU.mult)
            rq_sv = 0
            if j >= 1:
                ve.wait_ge(self.sem["st"], tpi_st[j - 1])
            for ss_ in range(4):
                sl = slice(ss_ * 512, (ss_ + 1) * 512)
                self.wait(se, "sv")
                SC("activation", FA.r("zq"), xt[:, sl], ACT.Copy,
                   bias=0.0, scale=comb)
                rq_sv = self.emit_round(FA.r("zq"), self.IA.r("ri"), FA.r("rqf"),
                                        FA.r("rqt"), A.bf("ixbf")[:, sl])
            cons_ss[j] = self.cnt["ss"]
            if j + 2 < 8:
                issue_own(j + 2)
            self.wait(te, "ss", xn_ss[j])
            self.transpose_2048(xnb, own_dst("xnown", j))
            tpx_st[j] = self.cnt["st"]
            te.wait_ge(self.sem["sv"], rq_sv)
            self.transpose_2048(A.bf("ixbf"), own_dst("ixq", j))
            tpi_st[j] = self.cnt["st"]
        self.wait(gp, "ss")
        self.DMA(gp, "dwg", self.ixq_dram[:, :], A.bf("ixq"))
        ixq_spill_cnt = self.cnt["dwg"]
        A.free("ixq")
        A.free("ixbf")
        A.free("xnbf")
        A.free("junkb")
        FA.free("xtm")
        FA.free("zq")
        FA.free("rqf")
        FA.free("rqt")
        self.IA.free("ri")
        self.fence()

        if self.dbg:
            self.dbg_dump("dbg_xnown", [128, 16384], A.bf("xnown"), waits=("sv",))

        A.alloc("knope", 16384, at=72960)
        A.alloc("krope", 8192, at=43264)
        A.alloc("vaug", 16640, at=26624)
        A.alloc("wk_b", 4 * 768, at=51456)
        A.alloc("wv_b", 4 * 512, at=54528)
        gp.wait_ge(self.sem["dwg"], ixq_spill_cnt)
        for r in range(4):
            self.DMA(gp, "dwg", A.bf("wk_b", r * 768, (r + 1) * 768),
                     self.wk[r * 128:(r + 1) * 128, :])
        for r in range(4):
            self.DMA(gp, "dwg", A.bf("wv_b", r * 512, (r + 1) * 512),
                     self.wv[r * 128:(r + 1) * 128, :])
        dwg_kv = self.cnt["dwg"]
        te.wait_ge(self.sem["dwg"], dwg_kv)
        self.wait(te, "sv")

        ev_prev = None
        for m in range(6):
            for n in range(8):
                slot = ((m * 8 + n) % 2)
                ps = psc0[:, slot * 512:(slot + 1) * 512]
                if ev_prev is not None and (m * 8 + n) >= 2:
                    te.wait_ge(self.sem["sv"], ev_two_ago)
                stc = None
                for kk in range(4):
                    stc = MM(ps, A.bf("wk_b", kk * 768 + m * 128, kk * 768 + (m + 1) * 128),
                             A.bf("ckv", kk * 4096 + n * 512, kk * 4096 + (n + 1) * 512),
                             kk == 0, kk == 3, inc=(kk == 3))
                self.wait(ve, "st", stc)
                if m < 4:
                    dst = A.bf("knope", m * 4096 + n * 512, m * 4096 + (n + 1) * 512)
                else:
                    dst = A.bf("krope", (m - 4) * 4096 + n * 512, (m - 4) * 4096 + (n + 1) * 512)
                ev = V("tensor_copy", dst, ps)
                ev_two_ago = ev_prev if ev_prev is not None else ev
                ev_prev = ev
        V("memset", A.bf("vaug"), 1.0)
        vr = A.bf("vaug").rearrange("p (t g c) -> p t g c", g=4, c=130)
        for t in range(32):
            slot = t % 2
            ps = psc1[:, slot * 512:(slot + 1) * 512]
            if t >= 2:
                te.wait_ge(self.sem["sv"], ev_two_ago)
            stc = None
            for kk in range(4):
                stc = MM(ps, A.bf("ckv", kk * 4096 + t * 128, kk * 4096 + (t + 1) * 128),
                         A.bf("wv_b", kk * 512, (kk + 1) * 512),
                         kk == 0, kk == 3, inc=(kk == 3))
            self.wait(ve, "st", stc)
            ev = V("tensor_copy", vr[:, t, :, 0:128],
                   ps.rearrange("p (g c) -> p g c", g=4))
            ev_two_ago = ev_prev
            ev_prev = ev
        A.free("ckv")
        A.free("wk_b")
        A.free("wv_b")

        FA.alloc("rsc1", 512)
        FA.alloc("rsc2", 512)
        self.wait(te, "sv")
        self.wait(te, "ss")
        ve.wait_ge(self.sem["dbgo"], self.tab_cnt)
        r2rd = [0, 0]
        tgl = 0
        for gpair in range(2):
            for half in range(2):
                po = half * 64
                for n in range(8):
                    b = tgl
                    tgl ^= 1
                    ps_ = (pst if b == 0 else self.ppv)[po:po + 64, 0:512]
                    kslice = A.bf("krope", gpair * 4096 + n * 512,
                                  gpair * 4096 + (n + 1) * 512)[po:po + 64, :]
                    if r2rd[b]:
                        te.wait_ge(self.sem["sv"], r2rd[b])
                    stc = MM(ps_, self.R[po:po + 64, :], kslice,
                             True, True, inc=True)
                    cosb = A.bf("tab_cosk", n * 512, (n + 1) * 512)[po:po + 64, :]
                    sinb = A.bf("tab_sink", n * 512, (n + 1) * 512)[po:po + 64, :]
                    r1 = FA.r("rsc1")[po:po + 64, :]
                    r2 = FA.r("rsc2")[po:po + 64, :]
                    V("tensor_mul", r1, kslice, cosb)
                    self.wait(ve, "st", stc)
                    r2rd[b] = V("tensor_mul", r2, ps_, sinb)
                    V("tensor_add", kslice, r1, r2)
        A.free("tab_sink")
        A.free("tab_cosk")
        if self.dbg:
            self.dbg_dump("dbg_knope", [128, 16384], A.bf("knope"), waits=("sv",))
            self.dbg_dump("dbg_krope", [128, 8192], A.bf("krope"), waits=("sv",))
            self.dbg_dump("dbg_vaug", [128, 16640], A.bf("vaug"), waits=("sv",))

        A.alloc("qnope", 16384, at=10240)
        A.alloc("qrope", 8192, at=0)
        A.alloc("wqbuf", 2 * 2048, at=51456)
        wq_dma = [0] * 24
        q_done_st = [0] * 24

        def issue_wq(m):
            par = m % 2
            if m >= 2:
                gp.wait_ge(self.sem["st"], q_done_st[m - 2])
                gp.wait_ge(self.sem["dwq%d" % par], wq_dma[m - 2])
            else:
                self.wait(gp, "sv")
                self.wait(gp, "ss")
                self.wait(gp, "st")
            wq_dma[m] = self.DMA(gp, "dwq%d" % par,
                                 A.bf("wqbuf", par * 2048, (par + 1) * 2048),
                                 self.wq[m].rearrange("p k j -> p (k j)"))

        issue_wq(0)
        issue_wq(1)
        self.wait(te, "ss")
        self.wait(te, "sv")
        pfree = [0, 0]
        for m in range(24):
            par = m % 2
            psb = psc0 if par == 0 else psc1
            te.wait_ge(self.sem["dwq%d" % par], wq_dma[m])
            if pfree[par]:
                te.wait_ge(self.sem["ss"], pfree[par])
            stc = None
            for n in range(2):
                ps = psb[:, n * 512:(n + 1) * 512]
                for kk in range(NKCH):
                    stc = MM(ps, A.bf("wqbuf", par * 2048 + kk * 128, par * 2048 + (kk + 1) * 128),
                             A.bf("xnown", kk * 1024 + n * 512, kk * 1024 + (n + 1) * 512),
                             kk == 0, kk == NKCH - 1, inc=(kk == NKCH - 1 and n == 1))
            q_done_st[m] = self.cnt["st"]
            if m + 2 < 24:
                issue_wq(m + 2)
            self.wait(se, "st", q_done_st[m])
            if m < 16:
                pfree[par] = SC("activation", A.bf("qnope", m * 1024, (m + 1) * 1024),
                                psb[:, 0:1024], ACT.Copy, bias=0.0, scale=SCL)
            else:
                p = m - 16
                if m == 16:
                    self.wait(se, "sv")
                pfree[par] = SC("activation", A.bf("qrope", p * 1024, (p + 1) * 1024),
                                psb[:, 0:1024], ACT.Copy, bias=0.0, scale=SCL)
        self.wait(te, "ss")
        self.wait(te, "sv")
        self.wait(ve, "ss")
        r2rd = [0, 0]
        tgl = 0
        for p in range(8):
            for half in range(2):
                po = half * 64
                for n in range(2):
                    b = tgl
                    tgl ^= 1
                    ps_ = (pst if b == 0 else self.ppv)[po:po + 64, 0:512]
                    qsl = A.bf("qrope", p * 1024 + n * 512,
                               p * 1024 + (n + 1) * 512)[po:po + 64, :]
                    if r2rd[b]:
                        te.wait_ge(self.sem["sv"], r2rd[b])
                    stc2 = MM(ps_, self.R[po:po + 64, :], qsl, True, True, inc=True)
                    cosb = A.bf("tab_cosq", n * 512, (n + 1) * 512)[po:po + 64, :]
                    sinb = A.bf("tab_sinq", n * 512, (n + 1) * 512)[po:po + 64, :]
                    r1 = FA.r("rsc1")[po:po + 64, :]
                    r2 = FA.r("rsc2")[po:po + 64, :]
                    V("tensor_mul", r1, qsl, cosb)
                    self.wait(ve, "st", stc2)
                    r2rd[b] = V("tensor_mul", r2, ps_, sinb)
                    V("tensor_add", qsl, r1, r2)
        A.free("xnown")
        A.free("wqbuf")
        A.free("tab_sinq")
        A.free("tab_cosq")
        FA.free("rsc1")
        FA.free("rsc2")

        self.wait(ve, "ss")
        for h in range(16):
            V("reduce_max", st(self.S_QM + h), A.bf("qnope", h * 1024, (h + 1) * 1024),
              AX, apply_absolute_value=True)
        for p in range(8):
            V("reduce_max", st(self.S_QMR + p), A.bf("qrope", p * 1024, (p + 1) * 1024),
              AX, apply_absolute_value=True)
        for g in range(4):
            V("reduce_max", st(self.S_KM + g), A.bf("knope", g * 4096, (g + 1) * 4096),
              AX, apply_absolute_value=True)
        for gpair in range(2):
            V("reduce_max", st(self.S_KMR + gpair),
              A.bf("krope", gpair * 4096, (gpair + 1) * 4096),
              AX, apply_absolute_value=True)
        for h in range(16):
            V("tensor_mul", st(self.S_PR + h), st(self.S_QM + h), st(self.S_KM + h // 4))
        for h in range(16):
            p, po = QR_PAIR[h]
            rp = st(self.S_T0)
            V("tensor_mul", rp, st(self.S_QMR + p), st(self.S_KMR + p // 4))
            V("tensor_mul", rp, rp, st(self.S_HM if po == 0 else self.S_UHM))
            V("tensor_add", st(self.S_PR + h), st(self.S_PR + h), rp)
        self.wait(te, "sv")
        stc = MM(self.pst[0:1, 0:16], st(self.S_ONE), self.stats[:, self.S_PR:self.S_PR + 16],
                 True, True, inc=True)
        self.wait(ve, "st", stc)
        V("tensor_scalar", self.stats[0:1, self.S_PR:self.S_PR + 16],
          self.pst[0:1, 0:16], -1.0, None, ALU.mult)
        self.wait(te, "sv")
        stc = MM(self.pst[:, 0:16], self.stats[0:1, self.S_ONESROW:self.S_ONESROW + 128],
                 self.stats[0:1, self.S_PR:self.S_PR + 16], True, True, inc=True)
        self.wait(ve, "st", stc)
        V("tensor_copy", self.stats[:, self.S_NB:self.S_NB + 16], self.pst[:, 0:16])
        if self.dbg:
            self.dbg_dump("dbg_q", [128, 16384], A.bf("qnope"), waits=("ss",))
            self.dbg_dump("dbg_qr", [128, 8192], A.bf("qrope"), waits=("sv",))
            nb = self.dbg_out("dbg_nb", [128, 16])
            self.wait(sy, "sv")
            self.DMA(sy, "dout", nb[:, :], self.stats[:, self.S_NB:self.S_NB + 16])

    def phase3(self):
        nc, A, V, SC, G, MM = self.nc, self.A, self.V, self.SC, self.G, self.MM
        st, pst = self.st, self.pst
        sy, te, ve, se, gp = nc.sync, nc.tensor, nc.vector, nc.scalar, nc.gpsimd
        A.alloc("ppan", 16896, at=51456)
        A.alloc("atst", 256, at=68352)
        self.fence()
        dat_base = (self.cnt["dho0"], self.cnt["dho1"])
        dat_cnt = {u: dat_base[u % 2] + 16 * (u // 2 + 1) for u in range(128)}

        self.wait(te, "dmeta")
        self.wait(se, "dmeta")
        self.wait(ve, "dmeta")
        rt = te.alloc_register("rt")
        rs = se.alloc_register("rs")
        rv = ve.alloc_register("rv")
        te.reg_load(rt, self.meta_sb[0:1, 0:1])
        se.reg_load(rs, self.meta_sb[0:1, 0:1])
        ve.reg_load(rv, self.meta_sb[0:1, 0:1])

        FA = self.FA
        FA.alloc("wmba", 2 * 1024)
        FA.alloc("wmbb", 2 * 1024)
        FA.alloc("wmbc", 2 * 1408)
        V("memset", self.stats[:, self.S_AW:self.S_AW + 3], 0.0)
        wstr = [
            dict(dram=self.w1, n=88, parts=2, w=1024, buf="wmba",
                 sems=("dwsy0", "dwsy1"), acc=self.S_AW, q="sy"),
            dict(dram=self.w2, n=88, parts=2, w=1024, buf="wmbb",
                 sems=("dw3a", "dw3b"), acc=self.S_AW + 1, q="gp"),
            dict(dram=self.w3, n=64, parts=4, w=1408, buf="wmbc",
                 sems=("dws0", "dws1"), acc=self.S_AW + 2, q="sy"),
        ]
        for s_ in wstr:
            b0, b1 = self.cnt[s_["sems"][0]], self.cnt[s_["sems"][1]]
            s_["dpre"] = [(b0 if i % 2 == 0 else b1) + 16 * (i // 2 + 1)
                          for i in range(s_["n"])]

        sched = []
        for h in range(16):
            blk = []
            for k, s_ in enumerate(wstr):
                blk += [(k, i) for i in range((h * s_["n"]) // 16,
                                              ((h + 1) * s_["n"]) // 16)]
            sched.append(blk)
        nch = [len(x) for x in sched]
        pnch = [sum(nch[:h]) for h in range(16)]

        base = dict(self.cnt)
        red_pre = {}
        _cv = base["sv"]
        for h in range(16):
            for ch in sched[h]:
                _cv += 2
                red_pre[ch] = _cv
            _cv += 24

        def plan(c):
            tiles = owned_tiles(c)
            NKT = tiles[-1] + 1
            sj = [next(j for j in range(8) if tiles[j] >= kt) for kt in range(NKT)]
            poff = [0] * NKT
            acc = 0
            for kt in range(NKT):
                poff[kt] = acc
                acc += (8 - sj[kt]) * 128
            p = dict(tiles=tiles, NKT=NKT, sj=sj, poff=poff,
                     scores_st={}, pvj_st={}, exp_ss={}, mask_sv={}, evj_sv={})
            cst, css, csv = base["st"], base["ss"], base["sv"]
            for h in range(16):
                csv += 2 * nch[h]
                for kt in range(NKT):
                    cst += 1
                    p["scores_st"][(h, kt)] = cst
                    css += 1
                    p["exp_ss"][(h, kt)] = css
                    if kt in tiles:
                        csv += 1
                        p["mask_sv"][(h, kt)] = csv
                for j in range(8):
                    cst += 1
                    p["pvj_st"][(h, j)] = cst
                    csv += 2
                    p["evj_sv"][(h, j)] = csv
            p["end"] = dict(st=cst, ss=css, sv=csv)
            return p

        plans = [plan(c) for c in range(4)]
        ceil = dict(st=max(p["end"]["st"] for p in plans),
                    ss=max(p["end"]["ss"] for p in plans),
                    sv=max(p["end"]["sv"] for p in plans))

        def q_rhs(h, col0, w, which):
            if which == "n":
                return A.bf("qnope", h * 1024 + col0, h * 1024 + col0 + w)
            pq, po = QR_PAIR[h]
            return A.bf("qrope", pq * 1024 + col0, pq * 1024 + col0 + w)[po:po + 64, :]

        for c in range(4):
            p = plans[c]
            tiles, NKT, sj, poff = p["tiles"], p["NKT"], p["sj"], p["poff"]

            self.cnt = dict(base)
            with te.If_eq(rt, c):
                for h in range(16):
                    g = h // 4
                    goff = 64 * (g % 2)
                    for kt in range(NKT):
                        u = h * NKT + kt
                        buf = u % 2
                        psb = (self.psc0 if buf == 0 else self.psc1)
                        j0 = sj[kt]
                        col0, w = j0 * 128, (8 - j0) * 128
                        if u >= 2:
                            te.wait_ge(self.sem["ss"],
                                       p["exp_ss"][(h, kt - 2) if kt >= 2 else (h - 1, NKT - (2 - kt))])
                        kn = A.bf("knope", g * 4096 + kt * 128, g * 4096 + (kt + 1) * 128)
                        kr = A.bf("krope", (g // 2) * 4096 + kt * 128,
                                  (g // 2) * 4096 + (kt + 1) * 128)[goff:goff + 64, :]
                        nchunk = 1 if w <= 512 else 2
                        for ci in range(nchunk):
                            off = ci * 512
                            nw = min(512, w - off)
                            MM(psb[:, off:off + nw], kn,
                               q_rhs(h, col0 + off, nw, "n"), True, False)
                            MM(psb[:, off:off + nw], kr,
                               q_rhs(h, col0 + off, nw, "r"), False, True,
                               inc=(ci == nchunk - 1))
                        assert self.cnt["st"] == p["scores_st"][(h, kt)]
                    for j in range(8):
                        te.wait_ge(self.sem["ss"], p["exp_ss"][(h, tiles[j])])
                        if j in tiles or True:
                            te.wait_ge(self.sem["sv"], p["mask_sv"][(h, tiles[j])])
                        if j >= 3:
                            te.wait_ge(self.sem["sv"], p["evj_sv"][(h, j - 3)])
                        elif h > 0:
                            te.wait_ge(self.sem["sv"],
                                       p["evj_sv"][(h - 1, {0: 6, 1: 7, 2: 5}[j])])
                        bank = self.ppv[:, (j % 3) * 512:(j % 3) * 512 + 129]
                        for kt in range(tiles[j] + 1):
                            MM(bank,
                               A.bf("ppan", poff[kt] + (j - sj[kt]) * 128,
                                    poff[kt] + (j - sj[kt] + 1) * 128),
                               A.bf("vaug", kt * 520 + g * 130,
                                    kt * 520 + g * 130 + 129),
                               start=(kt == 0), stop=(kt == tiles[j]),
                               inc=(kt == tiles[j]))
                        assert self.cnt["st"] == p["pvj_st"][(h, j)]
                te.wait_ge(self.sem["st"], self.cnt["st"])
                for _ in range(ceil["st"] - self.cnt["st"]):
                    te.nop(nofuse=True).then_inc(self.sem["st"], 1)
                te.nop(nofuse=True).then_inc(self.sem["sp3"], 1)


        te.end_ifs()
        te.nop(nofuse=True)
        for c in range(4):
            p = plans[c]
            tiles, NKT, sj, poff = p["tiles"], p["NKT"], p["sj"], p["poff"]

            self.cnt = dict(base)
            with se.If_eq(rs, c):
                for h in range(16):
                    if h > 0:
                        se.wait_ge(self.sem["st"], p["pvj_st"][(h - 1, 7)])
                    for kt in range(NKT):
                        buf = (h * NKT + kt) % 2
                        psb = (self.psc0 if buf == 0 else self.psc1)
                        j0 = sj[kt]
                        w = (8 - j0) * 128
                        se.wait_ge(self.sem["st"], p["scores_st"][(h, kt)])
                        SC("activation", A.bf("ppan", poff[kt], poff[kt] + w),
                           psb[:, 0:w], ACT.Exp,
                           bias=self.stats[:, self.S_NB + h:self.S_NB + h + 1],
                           scale=1.0)
                        assert self.cnt["ss"] == p["exp_ss"][(h, kt)]
                se.wait_ge(self.sem["ss"], self.cnt["ss"])
                for _ in range(ceil["ss"] - self.cnt["ss"]):
                    se.nop(nofuse=True).then_inc(self.sem["ss"], 1)
                se.nop(nofuse=True).then_inc(self.sem["sp3"], 1)


        se.end_ifs()
        se.nop(nofuse=True)
        for c in range(4):
            p = plans[c]
            tiles, NKT, sj, poff = p["tiles"], p["NKT"], p["sj"], p["poff"]

            self.cnt = dict(base)
            with ve.If_eq(rv, c):
                for h in range(16):
                    for ch in sched[h]:
                        s_ = wstr[ch[0]]
                        i = ch[1]
                        wpar = i % 2
                        ve.wait_ge(self.sem[s_["sems"][wpar]], s_["dpre"][i])
                        V("reduce_sum", st(self.S_T0 + 3),
                          self.FA.r(s_["buf"], wpar * s_["w"], (wpar + 1) * s_["w"]),
                          AX, apply_absolute_value=True)
                        got = V("tensor_add", st(s_["acc"]), st(s_["acc"]),
                                st(self.S_T0 + 3))
                        assert got == red_pre[ch]
                    for kt in range(NKT):
                        if kt not in tiles:
                            continue
                        ve.wait_ge(self.sem["ss"], p["exp_ss"][(h, kt)])
                        blk = A.bf("ppan", poff[kt], poff[kt] + 128)
                        V("tensor_mul", blk, blk, self.TRI)
                        assert self.cnt["sv"] == p["mask_sv"][(h, kt)]
                    for j in range(8):
                        u = h * 8 + j
                        par = u % 2
                        ve.wait_ge(self.sem["st"], p["pvj_st"][(h, j)])
                        if u >= 2:
                            ve.wait_ge(self.sem["dho%d" % par], dat_cnt[u - 2])
                        bank = self.ppv[:, (j % 3) * 512:(j % 3) * 512 + 129]
                        V("reciprocal", st(self.S_LI), bank[:, 128:129])
                        V("tensor_scalar",
                          A.bf("atst", par * 128, (par + 1) * 128),
                          bank[:, 0:128], st(self.S_LI), None, ALU.mult)
                        assert self.cnt["sv"] == p["evj_sv"][(h, j)]
                ve.wait_ge(self.sem["sv"], self.cnt["sv"])
                for _ in range(ceil["sv"] - self.cnt["sv"]):
                    ve.nop(nofuse=True).then_inc(self.sem["sv"], 1)
                ve.nop(nofuse=True).then_inc(self.sem["sp3"], 1)

        ve.end_ifs()
        ve.nop(nofuse=True)

        ev_uniform = lambda h, j: (base["sv"] + 24 * h + 2 * (pnch[h] + nch[h])
                                   + 8 + 2 * (j + 1))
        for p in plans:
            for h in range(16):
                for j in range(8):
                    assert p["evj_sv"][(h, j)] == ev_uniform(h, j)

        def emit_issues(hh):
            if hh >= 16:
                return
            for ch in sched[hh]:
                s_ = wstr[ch[0]]
                i = ch[1]
                wpar = i % 2
                q = sy if s_["q"] == "sy" else gp
                if i >= 2:
                    q.wait_ge(self.sem["sv"], red_pre[(ch[0], i - 2)])
                    q.wait_ge(self.sem[s_["sems"][wpar]], s_["dpre"][i - 2])
                else:
                    q.wait_ge(self.sem["sv"], base["sv"])
                    q.wait_ge(self.sem["ss"], base["ss"])
                p_, w_ = s_["parts"], s_["w"]
                src = s_["dram"][(i // p_) * 128:(i // p_ + 1) * 128,
                                 (i % p_) * w_:(i % p_ + 1) * w_]
                got = self.DMA(q, s_["sems"][wpar],
                               self.FA.r(s_["buf"], wpar * w_, (wpar + 1) * w_),
                               src)
                assert got == s_["dpre"][i]

        emit_issues(0)
        for u in range(128):
            h, j = divmod(u, 8)
            par = u % 2
            gp.wait_ge(self.sem["sv"], ev_uniform(h, j))
            got = self.DMA(gp, "dho%d" % par,
                           self.attn_dram[j * 128:(j + 1) * 128,
                                          h * 128:(h + 1) * 128],
                           A.bf("atst", par * 128, (par + 1) * 128))
            assert got == dat_cnt[u], (u, got, dat_cnt[u])
            if j == 7:
                emit_issues(h + 1)

        self.cnt.update(ceil)
        self.cnt["sp3"] = 3
        for e in (te, se, ve, sy, gp):
            e.wait_ge(self.sem["sp3"], 3)
        te.nop()
        se.nop()
        ve.nop()
        A.free("ppan")
        A.free("atst")
        A.free("knope")
        A.free("krope")
        A.free("vaug")
        A.free("qnope")
        A.free("qrope")

    def phase4(self):
        nc, A, V, SC, G, MM = self.nc, self.A, self.V, self.SC, self.G, self.MM
        st, pst, psc0, psc1 = self.st, self.pst, self.psc0, self.psc1
        sy, te, ve, se, gp = nc.sync, nc.tensor, nc.vector, nc.scalar, nc.gpsimd
        A.alloc("attn_fm", 16384, at=16384)
        A.alloc("atld", 4096, at=49152)
        self.fence()
        self.wait(sy, "dho0")
        self.wait(sy, "dho1")
        ar_dma = [0] * 8
        tr_done_st = [0] * 8

        def issue_ar(j):
            par = j % 2
            if j >= 2:
                sy.wait_ge(self.sem["st"], tr_done_st[j - 2])
                sy.wait_ge(self.sem["dxb%d" % par], ar_dma[j - 2])
            ar_dma[j] = self.DMA(sy, "dxb%d" % par,
                                 A.bf("atld", par * 2048, (par + 1) * 2048),
                                 self.attn_dram[j * 128:(j + 1) * 128, :])

        issue_ar(0)
        issue_ar(1)
        fmr = A.bf("attn_fm").rearrange("p (h c) -> p h c", c=1024)
        for j in range(8):
            par = j % 2
            te.wait_ge(self.sem["dxb%d" % par], ar_dma[j])
            for hg in range(4):
                self.tp_group(
                    [A.bf("atld", par * 2048 + (hg * 4 + hh) * 128,
                          par * 2048 + (hg * 4 + hh + 1) * 128) for hh in range(4)],
                    fmr[:, hg * 4:hg * 4 + 4, j * 128:(j + 1) * 128])
            tr_done_st[j] = self.cnt["st"]
            if j + 2 < 8:
                issue_ar(j + 2)
        A.alloc("attn_out", 16384, at=0)
        A.alloc("wo_bf", 16384, at=32768)

        wo_dma = [0] * 4
        wo_done_st = [0] * 4
        ev_sv = 0
        ev_prev2 = 0

        def issue_wo(og):
            par = og % 2
            if og >= 2:
                gp.wait_ge(self.sem["st"], wo_done_st[og - 2])
                gp.wait_ge(self.sem["dwq%d" % par], wo_dma[og - 2])
            else:
                self.wait(gp, "sv")
                self.wait(gp, "ss")
            for k in range(NKCH):
                self.DMA(gp, "dwq%d" % par,
                         A.bf("wo_bf", par * 8192 + k * 512, par * 8192 + (k + 1) * 512),
                         self.wo[k * 128:(k + 1) * 128, og * 512:(og + 1) * 512])
            wo_dma[og] = self.cnt["dwq%d" % par]

        issue_wo(0)
        issue_wo(1)
        for og in range(4):
            par = og % 2
            te.wait_ge(self.sem["dwq%d" % par], wo_dma[og])
            for tt in range(8):
                slot = tt % 2
                ps = psc0[:, slot * 512:(slot + 1) * 512]
                if ev_prev2:
                    te.wait_ge(self.sem["sv"], ev_prev2)
                stc = None
                for k in range(NKCH):
                    stc = MM(ps, A.bf("attn_fm", k * 1024 + tt * 128, k * 1024 + (tt + 1) * 128),
                             A.bf("wo_bf", par * 8192 + k * 512, par * 8192 + (k + 1) * 512),
                             k == 0, k == NKCH - 1, inc=(k == NKCH - 1))
                self.wait(ve, "st", stc)
                ev_prev2 = ev_sv
                ev_sv = V("tensor_copy",
                          A.bf("attn_out", tt * 2048 + og * 512, tt * 2048 + (og + 1) * 512),
                          ps)
            wo_done_st[og] = self.cnt["st"]
            if og + 2 < 4:
                issue_wo(og + 2)
        A.free("attn_fm")
        A.free("wo_bf")
        A.free("atld")
        if self.dbg:
            self.dbg_dump("dbg_attnout", [128, 16384], A.bf("attn_out"), waits=("sv",))

    def phase5(self):
        nc, A, V, SC, G, MM = self.nc, self.A, self.V, self.SC, self.G, self.MM
        st, pst = self.st, self.pst
        sy, te, ve, se, gp = nc.sync, nc.tensor, nc.vector, nc.scalar, nc.gpsimd
        FA = self.FA
        self.wait(te, "sv")
        self.wait(te, "ss")
        stc = MM(pst[0:1, 0:3], st(self.S_ONE), self.stats[:, self.S_AW:self.S_AW + 3],
                 True, True, inc=True)
        self.wait(ve, "st", stc)
        row = self.stats[0:1, self.S_PR:self.S_PR + 16]
        V("tensor_scalar", row[:, 0:3], pst[0:1, 0:3], 1.0 / (HF * D), None, ALU.mult)
        V("tensor_scalar", row[:, 0:3], row[:, 0:3], 1e-5, None, ALU.max)
        V("tensor_scalar", row[:, 3:6], row[:, 0:3], 0.5, None, ALU.mult)
        V("tensor_scalar", row[:, 6:9], row[:, 3:6], -1.0, None, ALU.mult)
        self.wait(te, "sv")
        stc = MM(pst[:, 0:9], self.stats[0:1, self.S_ONESROW:self.S_ONESROW + 128],
                 row[:, 0:9], True, True, inc=True)
        self.wait(ve, "st", stc)
        V("tensor_copy", self.stats[:, self.S_SW:self.S_SW + 9], pst[:, 0:9])
        for j in range(8):
            V("tensor_mul", st(self.S_C1 + j), st(self.S_SXI + j), st(self.S_SW))
            V("tensor_mul", st(self.S_C2 + j), st(self.S_SXI + j), st(self.S_SW + 1))
        FA.free("wmba")
        FA.free("wmbb")
        FA.free("wmbc")

    def phase6(self):
        nc, A, V, SC, G, MM = self.nc, self.A, self.V, self.SC, self.G, self.MM
        st, pst, psc0, psc1 = self.st, self.pst, self.psc0, self.psc1
        sy, te, ve, se, gp = nc.sync, nc.tensor, nc.vector, nc.scalar, nc.gpsimd
        A.alloc("iw1T", 16384, at=16384)
        A.alloc("iw2T", 16384, at=32768)
        A.alloc("ixq", 16384, at=49152)
        A.alloc("wqz", 4096, at=65536)
        A.alloc("wcmpb", 1024, at=69632)
        FA = self.FA
        FA.alloc("wch1", 3 * 1024)
        FA.alloc("wch2", 2 * 1024)
        FA.alloc("ma2", 512)
        FA.alloc("mh0", 512)
        FA.alloc("mh1", 512)
        self.fence()
        self.DMA(gp, "dwg", A.bf("ixq"), self.ixq_dram[:, :])
        ixq_cnt = self.cnt["dwg"]
        te.wait_ge(self.sem["dwg"], ixq_cnt)
        V("memset", self.stats[:, self.S_HAM:self.S_HAM + 8], 0.0)

        wsrc = {1: self.w1, 2: self.w2}
        wchn = {1: "wch1", 2: "wch2"}
        wq_eng = {1: sy, 2: se}
        wq_sem = {1: ("dwsy0", "dwsy1"), 2: ("dws0", "dws1")}
        iwt = {1: "iw1T", 2: "iw2T"}
        thr_col = {1: self.S_SW + 3, 2: self.S_SW + 4}
        nthr_col = {1: self.S_SW + 6, 2: self.S_SW + 7}
        wdma = {1: [0] * 2 * NW1CH, 2: [0] * 2 * NW1CH}
        wcons_sv = {1: [0] * 2 * NW1CH, 2: [0] * 2 * NW1CH}
        wtrans_sv = {}
        qz_last_st = [0, 0]
        mm_done_st = {}

        nslot = {1: 3, 2: 2}

        def issue_w(w, mm):
            slot = mm % nslot[w]
            par = mm % 2
            e = wq_eng[w]
            ns = nslot[w]
            if mm >= ns:
                e.wait_ge(self.sem["sv"], wcons_sv[w][mm - ns])
            if mm >= 2:
                e.wait_ge(self.sem[wq_sem[w][par]], wdma[w][mm - 2])
            if mm < ns:
                self.wait(e, "sv")
                self.wait(e, wq_sem[w][par])
            m, hf = divmod(mm, 2)
            wdma[w][mm] = self.DMA(e, wq_sem[w][par],
                                   FA.r(wchn[w], slot * 1024, (slot + 1) * 1024),
                                   wsrc[w][m * 128:(m + 1) * 128,
                                           hf * 1024:(hf + 1) * 1024])

        for w in (1, 2):
            for _i in range(nslot[w]):
                issue_w(w, _i)

        h_spill = {}
        for og in range(11):
            iwpar = og % 2
            for w in (1, 2):
                for rc in range(4):
                    m = og * 4 + rc
                    qz = A.bf("wqz", (m % 2) * 2048, (m % 2 + 1) * 2048)
                    if qz_last_st[m % 2]:
                        ve.wait_ge(self.sem["st"], qz_last_st[m % 2])
                    for hf in range(2):
                        mm = 2 * m + hf
                        slot = mm % nslot[w]
                        par = mm % 2
                        ve.wait_ge(self.sem[wq_sem[w][par]], wdma[w][mm])
                        wc = FA.r(wchn[w], slot * 1024, (slot + 1) * 1024)
                        V("tensor_scalar", A.bf("wcmpb"), wc,
                          self.stats[:, nthr_col[w]:nthr_col[w] + 1], None, ALU.is_lt)
                        wcons_sv[w][mm] = V(
                            "scalar_tensor_tensor", qz[:, hf * 1024:(hf + 1) * 1024],
                            wc, self.stats[:, thr_col[w]:thr_col[w] + 1], A.bf("wcmpb"),
                            ALU.is_gt, ALU.subtract)
                        if mm + nslot[w] < 2 * NW1CH:
                            issue_w(w, mm + nslot[w])
                    dstr = A.bf(iwt[w], iwpar * 8192, (iwpar + 1) * 8192) \
                        .rearrange("p (k c) -> p k c", c=512)
                    te.wait_ge(self.sem["sv"], wcons_sv[w][2 * m + 1])
                    for kb in range(0, NKCH, 4):
                        if og >= 2 and (w, og - 2) in mm_done_st:
                            te.wait_ge(self.sem["st"], mm_done_st[(w, og - 2)])
                        wtrans_sv[(w, m)] = self.tp_group(
                            [qz[:, (kb + kk) * 128:(kb + kk + 1) * 128]
                             for kk in range(4)],
                            dstr[:, kb:kb + 4, rc * 128:(rc + 1) * 128])
                    qz_last_st[m % 2] = self.cnt["st"]
            for tt in range(8):
                slot = tt % 2
                for w, psb in ((1, psc0), (2, psc1)):
                    ps = psb[:, slot * 512:(slot + 1) * 512]
                    te.wait_ge(self.sem["ss"], wtrans_sv[(w, og * 4 + 3)])
                    if (w, og, tt, "evac") in h_spill:
                        pass
                    if tt >= 2:
                        te.wait_ge(self.sem["ss" if w == 1 else "sv"],
                                   h_spill[(w, og, tt - 2)])
                    stc = None
                    for k in range(NKCH):
                        stc = MM(ps, A.bf("ixq", k * 1024 + tt * 128, k * 1024 + (tt + 1) * 128),
                                 A.bf(iwt[w], iwpar * 8192 + k * 512, iwpar * 8192 + (k + 1) * 512),
                                 k == 0, k == NKCH - 1, inc=(k == NKCH - 1))
                    mm_done_st[(w, og, tt)] = self.cnt["st"]
                mm_done_st[(1, og)] = mm_done_st[(1, og, tt)]
                mm_done_st[(2, og)] = mm_done_st[(2, og, tt)]
                self.wait(se, "st", mm_done_st[(1, og, tt)])
                self.wait(se, "sv")
                h_spill[(1, og, tt)] = SC(
                    "activation", self.ppv[:, 512:1024], psc0[:, slot * 512:(slot + 1) * 512],
                    ACT.Silu, bias=0.0,
                    scale=self.stats[:, self.S_C1 + tt:self.S_C1 + tt + 1])
                self.wait(ve, "st", mm_done_st[(2, og, tt)])
                h_spill[(2, og, tt)] = V(
                    "tensor_scalar", FA.r("ma2"), psc1[:, slot * 512:(slot + 1) * 512],
                    self.stats[:, self.S_C2 + tt:self.S_C2 + tt + 1], None, ALU.mult)
                gi = og * 8 + tt
                mhp = gi % 2
                mh = FA.r("mh0" if mhp == 0 else "mh1")
                if gi >= 2:
                    po, ptt = divmod(gi - 2, 8)
                    ve.wait_ge(self.sem["dho%d" % mhp], h_spill.get(("dma", po, ptt), 0))
                self.wait(ve, "ss")
                V("tensor_mul", mh, FA.r("ma2"), self.ppv[:, 512:1024])
                V("reduce_max", st(self.S_T0), mh, AX, apply_absolute_value=True)
                hs = V("tensor_max", st(self.S_HAM + tt), st(self.S_HAM + tt), st(self.S_T0))
                gp.wait_ge(self.sem["sv"], hs)
                h_spill[("dma", og, tt)] = self.DMA(
                    gp, "dho%d" % mhp, self.h_dram[tt * 128:(tt + 1) * 128,
                                                   og * 512:(og + 1) * 512], mh)
        self.wait(ve, "sg")
        for j in range(8):
            V("tensor_scalar", st(self.S_T0), st(self.S_HAM + j), 1e-5, None, ALU.max)
            V("reciprocal", st(self.S_T0 + 1), st(self.S_T0))
            V("tensor_scalar", st(self.S_SXH + j), st(self.S_T0 + 1), 127.0, None, ALU.mult)
            V("tensor_scalar", st(self.S_T0), st(self.S_T0), 1.0 / 127.0, None, ALU.mult)
            V("tensor_mul", st(self.S_S3 + j), st(self.S_T0), st(self.S_SW + 2))
        for nm in ("iw1T", "iw2T", "wqz", "ixq"):
            A.free(nm)
        A.free("wcmpb")
        for nm in ("wch1", "wch2", "ma2", "mh0", "mh1"):
            FA.free(nm)

    def phase7(self):
        nc, A, V, SC, G, MM = self.nc, self.A, self.V, self.SC, self.G, self.MM
        st, pst, psc0 = self.st, self.pst, self.psc0
        sy, te, ve, se, gp = nc.sync, nc.tensor, nc.vector, nc.scalar, nc.gpsimd
        A.alloc("ihq", 2048, at=83968)
        A.alloc("wqz3", 1024, at=86016)
        A.alloc("ih", 45056, at=16384)
        A.alloc("w3T", 22528, at=61440)
        FA = self.FA
        FA.alloc("hrl", 4 * 512)
        FA.alloc("z7p", 4 * 512)
        FA.alloc("f7p", 4 * 512)
        FA.alloc("t7p", 512)
        self.IA.alloc("i7", 512)
        self.fence()
        for e in (se, sy, ve, te, gp):
            self.wait(e, "dho0")
            self.wait(e, "dho1")
        self.wait(sy, "dwsy0")
        self.wait(sy, "dwsy1")
        self.wait(sy, "dxb0")
        self.wait(sy, "dxb1")
        self.wait(se, "dhi0")
        self.wait(se, "dhi1")

        hdma = {}
        tr_sv = {}
        rnd_sv = {}
        pend = []
        ihr = A.bf("ih").rearrange("p (k c) -> p k c", c=1024)

        def issue_h(og, tt):
            i = og * 8 + tt
            par = i % 2
            q4 = i % 4
            if i >= 2:
                se.wait_ge(self.sem["dhi%d" % par], hdma.get(i - 2, 0))
            if i >= 4:
                se.wait_ge(self.sem["ss"], hdma.get(("cons", i - 4), 0))
            hdma[i] = self.DMA(se, "dhi%d" % par,
                               FA.r("hrl", q4 * 512, (q4 + 1) * 512),
                               self.h_dram[tt * 128:(tt + 1) * 128,
                                           og * 512:(og + 1) * 512])

        for _w in range(4):
            issue_h(_w // 8, _w % 8)
        for og in range(11):
            for tt in range(8):
                i = og * 8 + tt
                par = i % 2
                q4 = i % 4
                se.wait_ge(self.sem["dhi%d" % par], hdma[i])
                if i >= 4:
                    se.wait_ge(self.sem["sv"], rnd_sv[i - 4])
                z7 = FA.r("z7p", q4 * 512, (q4 + 1) * 512)
                f7 = FA.r("f7p", q4 * 512, (q4 + 1) * 512)
                t7 = FA.r("t7p")
                hdma[("cons", i)] = SC(
                    "activation", z7, FA.r("hrl", q4 * 512, (q4 + 1) * 512),
                    ACT.Copy, bias=0.0,
                    scale=self.stats[:, self.S_SXH + tt:self.S_SXH + tt + 1])
                SC("activation", self.IA.r("i7"), z7, ACT.Copy)
                f7ss = SC("activation", f7, self.IA.r("i7"), ACT.Copy)
                if i >= 4:
                    ve.wait_ge(self.sem["st"], tr_sv[("st", i - 4)])
                ve.wait_ge(self.sem["ss"], f7ss)
                ihq = A.bf("ihq", q4 * 512, (q4 + 1) * 512)
                V("scalar_tensor_tensor", t7, z7, -0.5, f7, ALU.add, ALU.is_gt)
                V("scalar_tensor_tensor", z7, z7, 0.5, f7, ALU.add, ALU.is_lt)
                V("tensor_sub", t7, t7, z7)
                rnd_sv[i] = V("tensor_add", ihq, f7, t7)
                if i + 4 < 88:
                    issue_h((i + 4) // 8, (i + 4) % 8)
                pend.append((i, ihq, ihr[:, og * 4:og * 4 + 4,
                                         tt * 128:(tt + 1) * 128]))
                if len(pend) == 4:
                    for pi, pihq, pdst in pend:
                        te.wait_ge(self.sem["sv"], rnd_sv[pi])
                        tr_sv[pi] = self.tp_group(
                            [pihq[:, kk * 128:(kk + 1) * 128] for kk in range(4)],
                            pdst)
                    flush_st = self.cnt["st"]
                    for pi, _, _ in pend:
                        tr_sv[("st", pi)] = flush_st
                    pend = []
        FA.free("hrl")
        FA.free("z7p")
        FA.free("f7p")
        FA.free("t7p")
        self.IA.free("i7")

        FA.alloc("wch3", 4 * 512)
        FA.alloc("w3s", 512)
        FA.alloc("xsl", 2 * 512)
        FA.alloc("f1", 2 * 512)
        FA.alloc("obuf", 2 * 512)
        w3dma = {}
        w3q_sv = {}
        w3t_st = {}
        out_dma = {}
        w3r = A.bf("w3T").rearrange("p (k c) -> p k c", c=512)
        for og3 in range(4):
            for mr in range(4):
                m = og3 * 4 + mr
                for sl in range(11):
                    s = m * 11 + sl
                    par = s % 2
                    q4 = s % 4
                    if s >= 2:
                        sy.wait_ge(self.sem["dwsy%d" % par], w3dma.get(s - 2, 0))
                    if s >= 4:
                        sy.wait_ge(self.sem["sv"], w3q_sv.get(s - 4, 0))
                    elif s < 2:
                        self.wait(sy, "ss")
                        self.wait(sy, "sv")
                    w3dma[s] = self.DMA(
                        sy, "dwsy%d" % par, FA.r("wch3", q4 * 512, (q4 + 1) * 512),
                        self.w3[m * 128:(m + 1) * 128, sl * 512:(sl + 1) * 512])
                    ve.wait_ge(self.sem["dwsy%d" % par], w3dma[s])
                    wc = FA.r("wch3", q4 * 512, (q4 + 1) * 512)
                    qz = A.bf("wqz3", par * 512, (par + 1) * 512)
                    if s >= 2:
                        ve.wait_ge(self.sem["st"], w3t_st.get(s - 2, 0))
                    V("tensor_scalar", FA.r("w3s"), wc,
                      self.stats[:, self.S_SW + 8:self.S_SW + 9], None, ALU.is_lt)
                    w3q_sv[s] = V(
                        "scalar_tensor_tensor", qz, wc,
                        self.stats[:, self.S_SW + 5:self.S_SW + 6], FA.r("w3s"),
                        ALU.is_gt, ALU.subtract)
                    te.wait_ge(self.sem["sv"], w3q_sv[s])
                    self.tp_group(
                        [qz[:, kk * 128:(kk + 1) * 128] for kk in range(4)],
                        w3r[:, sl * 4:sl * 4 + 4, mr * 128:(mr + 1) * 128])
                    w3t_st[s] = self.cnt["st"]
            self.wait(te, "sv")
            self.wait(te, "ss")
            for tt in range(8):
                i = og3 * 8 + tt
                slot = tt % 2
                ps = psc0[:, slot * 512:(slot + 1) * 512]
                xpar = i % 2
                if i >= 2:
                    sy.wait_ge(self.sem["sv"], out_dma.get(("xcons", i - 2), 0))
                    sy.wait_ge(self.sem["dxb%d" % xpar], out_dma.get(("x", i - 2), 0))
                out_dma[("x", i)] = self.DMA(
                    sy, "dxb%d" % xpar, FA.r("xsl", xpar * 512, (xpar + 1) * 512),
                    self.x_own[tt * 128:(tt + 1) * 128, og3 * 512:(og3 + 1) * 512])
                if i >= 2:
                    te.wait_ge(self.sem["ss"], out_dma.get(("evac", i - 2), 0))
                stc = None
                for k in range(44):
                    stc = MM(ps, ihr[:, k, tt * 128:(tt + 1) * 128],
                             A.bf("w3T", k * 512, (k + 1) * 512),
                             k == 0, k == 43, inc=(k == 43))
                fpar = i % 2
                f1ap = FA.r("f1", fpar * 512, (fpar + 1) * 512)
                self.wait(se, "st", stc)
                if i >= 2:
                    se.wait_ge(self.sem["sv"], out_dma.get(("xcons", i - 2), 0))
                out_dma[("evac", i)] = SC(
                    "activation", f1ap, ps, ACT.Copy, bias=0.0,
                    scale=self.stats[:, self.S_S3 + tt:self.S_S3 + tt + 1])
                obp = i % 2
                if i >= 2:
                    ve.wait_ge(self.sem["do%d" % obp], out_dma.get(("out", i - 2), 0))
                ve.wait_ge(self.sem["ss"], out_dma[("evac", i)])
                ve.wait_ge(self.sem["dxb%d" % xpar], out_dma[("x", i)])
                V("tensor_add", f1ap, f1ap,
                  A.bf("attn_out", tt * 2048 + og3 * 512, tt * 2048 + (og3 + 1) * 512))
                out_dma[("xcons", i)] = V(
                    "tensor_add", FA.r("obuf", obp * 512, (obp + 1) * 512),
                    f1ap, FA.r("xsl", xpar * 512, (xpar + 1) * 512))
                sy.wait_ge(self.sem["sv"], out_dma[("xcons", i)])
                out_dma[("out", i)] = self.DMA(
                    sy, "do%d" % obp, self.out[tt * 128:(tt + 1) * 128,
                                               og3 * 512:(og3 + 1) * 512],
                    FA.r("obuf", obp * 512, (obp + 1) * 512))
        self.wait(sy, "do0")
        self.wait(sy, "do1")


def rope_tables():
    import ctypes
    libm = ctypes.CDLL("libm.so.6")
    libm.powf.restype = ctypes.c_float
    libm.powf.argtypes = [ctypes.c_float, ctypes.c_float]
    libm.cosf.restype = ctypes.c_float
    libm.cosf.argtypes = [ctypes.c_float]
    libm.sinf.restype = ctypes.c_float
    libm.sinf.argtypes = [ctypes.c_float]
    ex = np.arange(0, DR, 2, dtype=np.float32) / np.float32(DR)
    inv = np.array([np.float32(1.0) / np.float32(libm.powf(ROPE_BASE, float(e)))
                    for e in ex], dtype=np.float32)
    t32 = np.arange(S, dtype=np.float32)
    fr = (t32[:, None] * inv[None, :]).astype(np.float32)
    cs = np.fromiter((libm.cosf(float(v)) for v in fr.ravel()),
                     np.float32, fr.size).reshape(S, 32)
    sn = np.fromiter((libm.sinf(float(v)) for v in fr.ravel()),
                     np.float32, fr.size).reshape(S, 32)
    sink = np.ascontiguousarray(np.tile(sn.T, (4, 1)))
    cosk = np.ascontiguousarray(np.tile(cs.T, (4, 1)))
    return sink, cosk


def host_prep(inputs):
    Wqkv = np.asarray(inputs["L0_attn_Wqkv"], dtype=np.float32)
    Wk_up = np.asarray(inputs["L0_attn_Wk_up"], dtype=np.float32)
    qn = [h * DQK + d for h in range(NH) for d in range(DN)]
    qr = []
    for p in range(8):
        for hl in (EVENG, ODDG):
            hh = hl[p]
            qr += [hh * DQK + DN + r for r in range(DR)]
    cols = qn + qr + list(range(NH * DQK, NH * DQK + LORA))
    Wp = np.ascontiguousarray(Wqkv[:, cols])
    wq_t = np.ascontiguousarray(
        Wp[:, :24 * 128].reshape(NKCH, 128, 24, 128).transpose(2, 1, 0, 3))
    wckv_t = np.ascontiguousarray(
        Wp[:, 24 * 128:].reshape(NKCH, 128, 4, 128).transpose(2, 1, 0, 3))
    kn = [g * DQK + d for g in range(NKV) for d in range(DN)]
    kr = []
    for gp_ in range(2):
        for half in range(2):
            g = gp_ * 2 + half
            kr += [g * DQK + DN + r for r in range(DR)]
    wk_p = np.ascontiguousarray(Wk_up[:, kn + kr])
    return wq_t, wckv_t, wk_p


def make_in_maps(inputs):
    wq_t, wckv_t, wk_p = host_prep(inputs)
    sink, cosk = rope_tables()
    tabk = np.ascontiguousarray(np.concatenate([sink, cosk], axis=1))
    x = np.asarray(inputs["L0_attn_hidden_states"], dtype=np.float32)
    ims = []
    for core in range(8):
        b, c = core // 4, core % 4
        tiles = owned_tiles(c)
        rows = np.concatenate([np.arange(t * 128, (t + 1) * 128) for t in tiles])
        ims.append(dict(
            x_own=np.ascontiguousarray(x[b][rows]),
            x_batch=np.ascontiguousarray(x[b]),
            tabk=tabk,
            tabq=np.ascontiguousarray(
                np.concatenate([sink[:, rows], cosk[:, rows]], axis=1)),
            meta=np.array([[c, 0, 0, 0]], dtype=np.int32),
            wq=wq_t, wckv=wckv_t, wk=wk_p,
            wv=np.ascontiguousarray(np.asarray(inputs["L0_attn_Wv_up"], dtype=np.float32)),
            wo=np.ascontiguousarray(np.asarray(inputs["L0_attn_Wo"], dtype=np.float32)),
            w1=np.ascontiguousarray(np.asarray(inputs["L0_ffn_w1"], dtype=np.float32)),
            w2=np.ascontiguousarray(np.asarray(inputs["L0_ffn_w2"], dtype=np.float32)),
            w3=np.ascontiguousarray(np.asarray(inputs["L0_ffn_w3"], dtype=np.float32)),
        ))
    return ims


def build_and_run(inputs, phases=8, dbg=False, cores=None):
    ims = make_in_maps(inputs)
    b = Bld(phases, dbg)
    nc = b.build()
    res = run_bass_kernel_spmd(nc, ims, core_ids=list(range(8)))
    return b, res


def kernel(**inputs):
    ims = make_in_maps(inputs)
    b = Bld(8, False)
    nc = b.build()
    res = run_bass_kernel_spmd(nc, ims, core_ids=list(range(8)))
    out = np.zeros((B, S, D), dtype=np.float32)
    for core in range(8):
        bb, c = core // 4, core % 4
        for j, t in enumerate(owned_tiles(c)):
            out[bb, t * 128:(t + 1) * 128, :] = res.results[core]["out"][j * 128:(j + 1) * 128, :]
    return out


# version 20 (final)
# speedup vs baseline: 1.1026x; 1.0031x over previous
import math
import numpy as np
import concourse.bass as bass
import concourse.mybir as mybir
from concourse.bass_utils import run_bass_kernel_spmd
from contextlib import ExitStack

F32 = mybir.dt.float32
BF16 = mybir.dt.bfloat16
I32 = mybir.dt.int32
ALU = mybir.AluOpType
ACT = mybir.ActivationFunctionType
AX = mybir.AxisListType.X

B, S, D = 2, 4096, 2048
NH, NKV, LORA = 16, 4, 512
DR, DN, DVh, DQK = 64, 128, 128, 192
HF = 5632
TOWN = 1024
NKCH = D // 128
SCL = 1.0 / math.sqrt(DQK)
EPS = 1e-5
ROPE_BASE = 10000.0
NW1CH = HF // 128

EVENG = [0, 1, 2, 3, 8, 9, 10, 11]
ODDG = [4, 5, 6, 7, 12, 13, 14, 15]
QR_PAIR = {}
for _i, _h in enumerate(EVENG):
    QR_PAIR[_h] = (_i, 0)
for _i, _h in enumerate(ODDG):
    QR_PAIR[_h] = (_i, 64)


def owned_tiles(c):
    return list(range(4 * c, 4 * c + 4)) + list(range(4 * (7 - c), 4 * (7 - c) + 4))


class Arena:
    def __init__(self, ap, n):
        self.ap, self.n, self.live = ap, n, {}

    def alloc(self, name, elems, align=32, at=None):
        assert name not in self.live, name
        elems = -(-elems // align) * align
        if at is not None:
            for so, se in self.live.values():
                assert at + elems <= so or at >= so + se, \
                    f"arena overlap: {name}@{at}+{elems} vs {so}+{se}"
            assert at + elems <= self.n
            self.live[name] = (at, elems)
            return
        off = 0
        for so, se in sorted(self.live.values()):
            if off + elems <= so:
                break
            off = max(off, so + se)
        assert off + elems <= self.n, f"arena OOM: {name} {elems}@{off} live={self.live}"
        self.live[name] = (off, elems)

    def free(self, name):
        del self.live[name]

    def bf(self, name, lo=0, hi=None):
        off, n = self.live[name]
        return self.ap[:, off + lo: off + (n if hi is None else hi)]

    r = bf


class Bld:

    def __init__(self, phases, dbg):
        self.phases, self.dbg = phases, dbg
        self.nc = bass.Bass("TRN2", target_bir_lowering=False)
        self.sem, self.cnt = {}, {}
        self.dbg_t = {}
        self._tp_evac_sv = [0, 0]

    def mksem(self, *names):
        for n in names:
            self.sem[n] = self.stack.enter_context(self.nc.semaphore(n))
            self.cnt[n] = 0

    def inc(self, instr, s, n=1):
        instr.then_inc(self.sem[s], n)
        self.cnt[s] += n
        return self.cnt[s]

    def wait(self, eng, s, val=None):
        v = self.cnt[s] if val is None else val
        if v > 0:
            eng.wait_ge(self.sem[s], v)

    def _chain(self, eng, sname, fn, *a, **kw):
        if self.cnt[sname] > 0:
            eng.wait_ge(self.sem[sname], self.cnt[sname])
        instr = fn(*a, **kw)
        instr.then_inc(self.sem[sname], 1)
        self.cnt[sname] += 1
        return self.cnt[sname]

    def V(self, op, *a, **kw):
        return self._chain(self.nc.vector, "sv", getattr(self.nc.vector, op), *a, **kw)

    def SC(self, op, *a, **kw):
        return self._chain(self.nc.scalar, "ss", getattr(self.nc.scalar, op), *a, **kw)

    def G(self, op, *a, **kw):
        return self._chain(self.nc.gpsimd, "sg", getattr(self.nc.gpsimd, op), *a, **kw)

    def MM(self, out, lhsT, rhs, start, stop, inc=False):
        instr = self.nc.tensor.matmul(out, lhsT, rhs, start=start, stop=stop)
        if inc:
            return self.inc(instr, "st", 1)
        return None

    def DMA(self, q, s, out, in_):
        instr = q.dma_start(out, in_)
        instr.then_inc(self.sem[s], 16)
        self.cnt[s] += 16
        return self.cnt[s]

    def dbg_out(self, name, shape, dtype=F32):
        if name not in self.dbg_t:
            self.dbg_t[name] = self.nc.dram_tensor(name, shape, dtype,
                                                   kind="ExternalOutput").ap()
        return self.dbg_t[name]

    def dbg_dump(self, name, shape, src_ap, waits=()):
        o = self.dbg_out(name, shape)
        cast = src_ap.dtype != F32
        e = self.nc.gpsimd if cast else self.nc.sync
        s = "dbgo" if cast else "dout"
        for w in waits:
            self.wait(e, w)
        self.DMA(e, s, o[:, :], src_ap)

    def build(self):
        nc = self.nc
        self.x_own = nc.dram_tensor("x_own", [TOWN, D], F32, kind="ExternalInput").ap()
        self.x_batch = nc.dram_tensor("x_batch", [S, D], F32, kind="ExternalInput").ap()
        self.tabk = nc.dram_tensor("tabk", [128, 2 * S], F32, kind="ExternalInput").ap()
        self.tabq = nc.dram_tensor("tabq", [128, 2 * TOWN], F32, kind="ExternalInput").ap()
        self.meta = nc.dram_tensor("meta", [1, 4], I32, kind="ExternalInput").ap()
        self.wq = nc.dram_tensor("wq", [24, 128, NKCH, 128], F32, kind="ExternalInput").ap()
        self.wckv = nc.dram_tensor("wckv", [4, 128, NKCH, 128], F32, kind="ExternalInput").ap()
        self.wk = nc.dram_tensor("wk", [LORA, 768], F32, kind="ExternalInput").ap()
        self.wv = nc.dram_tensor("wv", [LORA, 512], F32, kind="ExternalInput").ap()
        self.wo = nc.dram_tensor("wo", [D, D], F32, kind="ExternalInput").ap()
        self.w1 = nc.dram_tensor("w1", [HF, D], F32, kind="ExternalInput").ap()
        self.w2 = nc.dram_tensor("w2", [HF, D], F32, kind="ExternalInput").ap()
        self.w3 = nc.dram_tensor("w3", [D, HF], F32, kind="ExternalInput").ap()
        self.out = nc.dram_tensor("out", [TOWN, D], F32, kind="ExternalOutput").ap()
        self.h_dram = nc.dram_tensor("h_scratch", [TOWN, HF], F32).ap()
        self.attn_dram = nc.dram_tensor("attn_scratch", [TOWN, D], BF16).ap()

        with ExitStack() as stack:
            self.stack = stack
            ARENA_N = 89344
            FA_N = 7168
            IA_N = 512
            at = stack.enter_context(nc.sbuf_tensor("arena", [128, ARENA_N], BF16))
            self.A = Arena(at[:, :], ARENA_N)
            ft = stack.enter_context(nc.sbuf_tensor("farena", [128, FA_N], F32))
            self.FA = Arena(ft[:, :], FA_N)
            it = stack.enter_context(nc.sbuf_tensor("iarena", [128, IA_N], I32))
            self.IA = Arena(it[:, :], IA_N)
            self.consts = stack.enter_context(nc.sbuf_tensor("consts", [128, 320], BF16))
            self.stats = stack.enter_context(nc.sbuf_tensor("stats", [128, 288], F32))
            self.meta_sb = stack.enter_context(nc.sbuf_tensor("meta_sb", [1, 4], I32))
            self.psc0 = stack.enter_context(nc.psum_tensor("psc0", [128, 1024], F32))
            self.psc1 = stack.enter_context(nc.psum_tensor("psc1", [128, 1024], F32))
            self.ppv = stack.enter_context(nc.psum_tensor("ppv", [128, 1536], F32))
            self.pst = stack.enter_context(nc.psum_tensor("pst", [128, 512], F32))
            self.mksem("sv", "ss", "sg", "st", "dxb0", "dxb1", "dmeta", "dwg",
                       "dws0", "dws1", "dwsy0", "dwsy1", "dho0", "dho1",
                       "dhi0", "dhi1", "dout", "do0", "do1", "dw3a", "dw3b", "dbgo",
                       "sp3", "dwq0", "dwq1")
            self.I_bf = self.consts[:, 0:128]
            self.R = self.consts[:, 128:192]
            self.TRI = self.consts[:, 192:320]
            self.phase0()
            if self.phases >= 1:
                self.phase1()
            if self.phases >= 2:
                self.phase2()
            if self.phases >= 3:
                self.phase3()
            if self.phases >= 4:
                self.phase4()
            if self.phases >= 5:
                self.phase5()
            if self.phases >= 6:
                self.phase6()
            if self.phases >= 7:
                self.phase7()
            self.wait(nc.sync, "dout")
            self.wait(nc.gpsimd, "dbgo")
            self.wait(nc.gpsimd, "dwg")
        return nc

    S_ONE = 0
    S_OROW = 1
    S_LN = 4
    S_AM = 12
    S_SXQ = 16
    S_SXI = 24
    S_NB = 32
    S_QM = 48
    S_KM = 64
    S_QMR = 68
    S_KMR = 76
    S_PR = 80
    S_UHM = 96
    S_HM = 97
    S_LI = 98
    S_AW = 100
    S_SW = 104
    S_C1 = 116
    S_C2 = 124
    S_HAM = 132
    S_SXH = 140
    S_S3 = 148
    S_T0 = 156
    S_ONESROW = 160

    def st(self, col, n=1):
        return self.stats[:, col:col + n]

    def fence(self, engs=None):
        nc = self.nc
        engs = engs or (nc.sync, nc.gpsimd, nc.tensor, nc.vector, nc.scalar)
        for e in engs:
            for s in ("sv", "ss", "sg", "dbgo", "dout"):
                self.wait(e, s)
            if e is not nc.tensor:
                self.wait(e, "st")

    def emit_round(self, z, r0i, r0f, t1, out, t2=None):
        SC, V = self.SC, self.V
        se, ve = self.nc.scalar, self.nc.vector
        self.wait(se, "sv")
        SC("activation", r0i, z, ACT.Copy)
        SC("activation", r0f, r0i, ACT.Copy)
        self.wait(ve, "ss")
        V("scalar_tensor_tensor", t1, z, -0.5, r0f, ALU.add, ALU.is_gt)
        lt = z if t2 is None else t2
        V("scalar_tensor_tensor", lt, z, 0.5, r0f, ALU.add, ALU.is_lt)
        V("tensor_sub", t1, t1, lt)
        return V("tensor_add", out, r0f, t1)

    def phase0(self):
        nc, A, FA, V, SC, G, MM = self.nc, self.A, self.FA, self.V, self.SC, self.G, self.MM
        st, pst = self.st, self.pst
        sy, te, ve, se, gp = nc.sync, nc.tensor, nc.vector, nc.scalar, nc.gpsimd

        self.DMA(sy, "dmeta", self.meta_sb[:, :], self.meta[:, :])

        G("memset", st(self.S_ONE), 1.0)
        G("memset", self.stats[0:1, self.S_ONESROW:self.S_ONESROW + 128], 1.0)
        FA.alloc("cset", 128)
        cs = FA.r("cset")
        G("memset", cs, 1.0)
        G("affine_select", cs, cs, [[1, 128]], ALU.is_equal, 0.0,
          base=0, channel_multiplier=-1)
        self.wait(ve, "sg")
        V("tensor_copy", self.I_bf, cs)
        self.wait(gp, "sv")
        G("memset", cs, 1.0)
        G("affine_select", cs, cs, [[1, 128]], ALU.is_ge, 0.0,
          base=0, channel_multiplier=-1)
        self.wait(ve, "sg")
        V("tensor_copy", self.TRI, cs)
        self.wait(gp, "sv")
        G("memset", cs[:, 0:64], 0.0)
        for bp in (0, 64):
            G("memset", cs[bp:bp + 32, 32:64], 1.0)
            G("affine_select", cs[bp:bp + 32, 32:64], cs[bp:bp + 32, 32:64],
              [[1, 32]], ALU.is_equal, 0.0, base=0, channel_multiplier=-1)
            G("memset", cs[bp + 32:bp + 64, 0:32], -1.0)
            G("affine_select", cs[bp + 32:bp + 64, 0:32], cs[bp + 32:bp + 64, 0:32],
              [[1, 32]], ALU.is_equal, 0.0, base=0, channel_multiplier=-1)
        self.wait(ve, "sg")
        V("tensor_copy", self.R, cs[:, 0:64])
        G("memset", st(self.S_HM), 1.0)
        G("affine_select", st(self.S_HM), st(self.S_HM), [[0, 1]], ALU.is_ge, 0.0,
          base=63, channel_multiplier=-1)
        self.wait(ve, "sg")
        V("tensor_scalar", st(self.S_UHM), st(self.S_HM), -1.0, None, ALU.mult)
        V("tensor_scalar", st(self.S_UHM), st(self.S_UHM), 1.0, None, ALU.add)
        FA.free("cset")

        A.alloc("tab_sink", S, at=0)
        A.alloc("tab_cosk", S, at=4096)
        A.alloc("tab_sinq", TOWN, at=8192)
        A.alloc("tab_cosq", TOWN, at=9216)
        self.wait(te, "dmeta")
        self.wait(te, "sg")

    def phase1(self):
        nc, A, V, SC, G, MM = self.nc, self.A, self.V, self.SC, self.G, self.MM
        st, pst, psc0 = self.st, self.pst, self.psc0
        sy, te, ve, se, gp = nc.sync, nc.tensor, nc.vector, nc.scalar, nc.gpsimd

        self.fence()
        A.alloc("ckv", 4 * 4096, at=10240)
        A.alloc("wckv", 4 * 2048, at=26624)
        A.alloc("xnbf", 2 * D, at=34816)
        A.alloc("xngrp", 2 * 8192, at=38912)
        A.alloc("junkb", 2048, at=87296)
        self.FA.alloc("xtm", 2 * D)

        for m in range(4):
            self.DMA(gp, "dwg", A.bf("wckv", m * 2048, (m + 1) * 2048),
                     self.wckv[m].rearrange("p k j -> p (k j)"))
        dwg_ckv = self.cnt["dwg"]
        self.DMA(gp, "dbgo", A.bf("tab_sink"), self.tabk[:, 0:S])
        self.DMA(gp, "dbgo", A.bf("tab_cosk"), self.tabk[:, S:2 * S])
        self.DMA(gp, "dbgo", A.bf("tab_sinq"), self.tabq[:, 0:TOWN])
        self.DMA(gp, "dbgo", A.bf("tab_cosq"), self.tabq[:, TOWN:2 * TOWN])
        self.tab_cnt = self.cnt["dbgo"]

        xn_done_ss = [0] * 32
        x_dma_cnt = [0] * 32
        tp1_st = [0] * 32
        evac_sv = {}

        def issue_x(gt):
            par = gt % 2
            if gt >= 2:
                sy.wait_ge(self.sem["ss"], xn_done_ss[gt - 2])
                sy.wait_ge(self.sem["dxb%d" % par], x_dma_cnt[gt - 2])
            x_dma_cnt[gt] = self.DMA(sy, "dxb%d" % par,
                                     self.FA.r("xtm", par * D, (par + 1) * D),
                                     self.x_batch[gt * 128:(gt + 1) * 128, :])

        issue_x(0)
        issue_x(1)

        for g in range(8):
            for tt in range(4):
                gt = g * 4 + tt
                par = gt % 2
                xt = self.FA.r("xtm", par * D, (par + 1) * D)
                xnb = A.bf("xnbf", par * D, (par + 1) * D)
                ve.wait_ge(self.sem["dxb%d" % par], x_dma_cnt[gt])
                se.wait_ge(self.sem["dxb%d" % par], x_dma_cnt[gt])
                if gt >= 2:
                    se.wait_ge(self.sem["st"], tp1_st[gt - 2])
                xn_done_ss[gt] = self.ln_tile(xt, xnb)
                if gt + 2 < 32:
                    issue_x(gt + 2)
                self.wait(te, "ss", xn_done_ss[gt])
                self.transpose_2048(xnb, self._grp_dst_fn(g % 2, tt))
                tp1_st[gt] = self.cnt["st"]
            te.wait_ge(self.sem["ss"], max(self._tp_evac_sv))
            gsl_lo, gsl_hi = g * 512, (g + 1) * 512
            xng = A.bf("xngrp", (g % 2) * 8192, ((g % 2) + 1) * 8192)
            for m in range(4):
                slot = (m % 2)
                ps = psc0[:, slot * 512:slot * 512 + 512]
                if g == 0 and m == 0:
                    te.wait_ge(self.sem["dwg"], dwg_ckv)
                if (g, m) in evac_sv:
                    pass
                prev = (g, m - 2) if m >= 2 else ((g - 1, m + 2) if g > 0 else None)
                if prev in evac_sv:
                    te.wait_ge(self.sem["sv"], evac_sv[prev])
                stc = None
                for kk in range(NKCH):
                    stc = MM(ps, A.bf("wckv", m * 2048 + kk * 128, m * 2048 + (kk + 1) * 128),
                             xng[:, kk * 512:(kk + 1) * 512],
                             kk == 0, kk == NKCH - 1, inc=(kk == NKCH - 1))
                self.wait(ve, "st", stc)
                evac_sv[(g, m)] = V("tensor_copy",
                                    A.bf("ckv", m * 4096 + gsl_lo, m * 4096 + gsl_hi), ps)
        self._p1_last_evacs = evac_sv
        self._p1_tail = (xn_done_ss[30], xn_done_ss[31], x_dma_cnt[30], x_dma_cnt[31])
        A.free("xngrp")
        A.free("wckv")

        if self.dbg and self.phases < 2:
            self.dbg_dump("dbg_ckv", [128, 4 * 4096], A.bf("ckv"), waits=("sv",))

    def ln_tile(self, xt, xnb_out):
        st, V, SC = self.st, self.V, self.SC
        ve, se = self.nc.vector, self.nc.scalar
        ssum, ssq = st(self.S_LN), st(self.S_LN + 1)
        mean, var = st(self.S_LN + 2), st(self.S_LN + 3)
        rstd = st(self.S_LN + 4)
        V("reduce_sum", ssum, xt, AX)
        self.wait(se, "sv")
        SC("activation", self.A.bf("junkb"), xt, ACT.Square, accum_out=ssq)
        V("tensor_scalar", mean, ssum, 1.0 / D, None, ALU.mult)
        self.wait(ve, "ss")
        V("tensor_scalar", var, ssq, 1.0 / D, None, ALU.mult)
        V("tensor_mul", ssq, mean, mean)
        V("tensor_sub", var, var, ssq)
        V("tensor_scalar", var, var, EPS, None, ALU.add)
        self.wait(se, "sv")
        SC("activation", var, var, ACT.Sqrt)
        self.wait(ve, "ss")
        V("reciprocal", rstd, var)
        V("tensor_scalar", xt, xt, mean, None, ALU.subtract)
        self.wait(se, "sv")
        return SC("activation", xnb_out, xt, ACT.Copy, bias=0.0, scale=rstd)

    def tp_group(self, blocks, dst):
        te, se = self.nc.tensor, self.nc.scalar
        MM, SC = self.MM, self.SC
        buf = self._tp_toggle
        self._tp_toggle ^= 1
        ps = self.pst if buf == 0 else self.ppv
        if self._tp_evac_sv[buf]:
            te.wait_ge(self.sem["ss"], self._tp_evac_sv[buf])
        stc = None
        n = len(blocks)
        for kk, b in enumerate(blocks):
            stc = MM(ps[:, kk * 128:(kk + 1) * 128], b, self.I_bf,
                     True, True, inc=(kk == n - 1))
        self.wait(se, "st", stc)
        src = ps[:, 0:n * 128]
        if len(dst.shape) == 3:
            src = src.rearrange("p (a b) -> p a b", a=n)
        self._tp_evac_sv[buf] = SC("activation", dst, src, ACT.Copy)
        return self._tp_evac_sv[buf]

    def transpose_2048(self, src_bf, dst_fn):
        for kb in range(0, NKCH, 4):
            self.tp_group([src_bf[:, (kb + kk) * 128:(kb + kk + 1) * 128]
                           for kk in range(4)], dst_fn(kb))

    _tp_toggle = 0

    def _grp_dst_fn(self, par, tt):
        base = self.A.bf("xngrp", par * 8192, (par + 1) * 8192)
        r = base.rearrange("p (k c) -> p k c", c=512)

        def fn(kb):
            return r[:, kb:kb + 4, tt * 128:(tt + 1) * 128]
        return fn

    def phase2(self):
        nc, A, V, SC, G, MM = self.nc, self.A, self.V, self.SC, self.G, self.MM
        st, pst, psc0, psc1 = self.st, self.pst, self.psc0, self.psc1
        sy, te, ve, se, gp = nc.sync, nc.tensor, nc.vector, nc.scalar, nc.gpsimd
        self.ixq_dram = nc.dram_tensor("ixq_scratch", [128, 16384], BF16).ap()

        A.alloc("xnown", 16384, at=56576)
        A.alloc("ixq", 16384, at=38912)
        A.alloc("ixbf", 2048, at=26624)
        FA = self.FA
        FA.alloc("zq", 512)
        FA.alloc("rqf", 512)
        FA.alloc("rqt", 512)
        self.IA.alloc("ri", 512)
        self.wait(se, "st")

        def own_dst(name, j):
            base = A.bf(name)
            r = base.rearrange("p (k c) -> p k c", c=1024)

            def fn(kb):
                return r[:, kb:kb + 4, j * 128:(j + 1) * 128]
            return fn

        x_dma = [0] * 8
        xn_ss = [0] * 8
        cons_ss = [0] * 8
        tpx_st = [0] * 8
        tpi_st = [0] * 8

        p1t = self._p1_tail

        def issue_own(j):
            par = j % 2
            if j >= 2:
                sy.wait_ge(self.sem["ss"], cons_ss[j - 2])
                sy.wait_ge(self.sem["dxb%d" % par], x_dma[j - 2])
            else:
                sy.wait_ge(self.sem["ss"], p1t[j])
                sy.wait_ge(self.sem["dxb%d" % par], p1t[2 + j])
            x_dma[j] = self.DMA(sy, "dxb%d" % par,
                                self.FA.r("xtm", par * D, (par + 1) * D),
                                self.x_own[j * 128:(j + 1) * 128, :])

        issue_own(0)
        issue_own(1)
        for j in range(8):
            par = j % 2
            xt = self.FA.r("xtm", par * D, (par + 1) * D)
            xnb = A.bf("xnbf", par * D, (par + 1) * D)
            ve.wait_ge(self.sem["dxb%d" % par], x_dma[j])
            se.wait_ge(self.sem["dxb%d" % par], x_dma[j])
            if j >= 2:
                se.wait_ge(self.sem["st"], tpx_st[j - 2])
            xn_ss[j] = self.ln_tile(xt, xnb)
            am, am2, rr = st(self.S_AM), st(self.S_AM + 1), st(self.S_AM + 2)
            comb = st(self.S_T0)
            self.wait(ve, "ss")
            V("reduce_max", am, xt, AX, apply_absolute_value=True)
            V("tensor_mul", am, am, st(self.S_LN + 4))
            V("tensor_scalar", am2, am, EPS, None, ALU.max)
            V("reciprocal", rr, am2)
            V("tensor_scalar", st(self.S_SXI + j), am2, 1.0 / 127.0, None, ALU.mult)
            V("tensor_mul", comb, rr, st(self.S_LN + 4))
            V("tensor_scalar", comb, comb, 127.0, None, ALU.mult)
            rq_sv = 0
            if j >= 1:
                ve.wait_ge(self.sem["st"], tpi_st[j - 1])
            for ss_ in range(4):
                sl = slice(ss_ * 512, (ss_ + 1) * 512)
                self.wait(se, "sv")
                SC("activation", FA.r("zq"), xt[:, sl], ACT.Copy,
                   bias=0.0, scale=comb)
                rq_sv = self.emit_round(FA.r("zq"), self.IA.r("ri"), FA.r("rqf"),
                                        FA.r("rqt"), A.bf("ixbf")[:, sl])
            cons_ss[j] = self.cnt["ss"]
            if j + 2 < 8:
                issue_own(j + 2)
            self.wait(te, "ss", xn_ss[j])
            self.transpose_2048(xnb, own_dst("xnown", j))
            tpx_st[j] = self.cnt["st"]
            te.wait_ge(self.sem["sv"], rq_sv)
            self.transpose_2048(A.bf("ixbf"), own_dst("ixq", j))
            tpi_st[j] = self.cnt["st"]
        self.wait(gp, "ss")
        self.DMA(gp, "dwg", self.ixq_dram[:, :], A.bf("ixq"))
        ixq_spill_cnt = self.cnt["dwg"]
        A.free("ixq")
        A.free("ixbf")
        A.free("xnbf")
        A.free("junkb")
        FA.free("xtm")
        FA.free("zq")
        FA.free("rqf")
        FA.free("rqt")
        self.IA.free("ri")
        self.fence()

        if self.dbg:
            self.dbg_dump("dbg_xnown", [128, 16384], A.bf("xnown"), waits=("sv",))

        A.alloc("knope", 16384, at=72960)
        A.alloc("krope", 8192, at=43264)
        A.alloc("vaug", 16640, at=26624)
        A.alloc("wk_b", 4 * 768, at=51456)
        A.alloc("wv_b", 4 * 512, at=54528)
        gp.wait_ge(self.sem["dwg"], ixq_spill_cnt)
        for r in range(4):
            self.DMA(gp, "dwg", A.bf("wk_b", r * 768, (r + 1) * 768),
                     self.wk[r * 128:(r + 1) * 128, :])
        for r in range(4):
            self.DMA(gp, "dwg", A.bf("wv_b", r * 512, (r + 1) * 512),
                     self.wv[r * 128:(r + 1) * 128, :])
        dwg_kv = self.cnt["dwg"]
        te.wait_ge(self.sem["dwg"], dwg_kv)
        self.wait(te, "sv")

        ev_prev = None
        for m in range(6):
            for n in range(8):
                slot = ((m * 8 + n) % 2)
                ps = psc0[:, slot * 512:(slot + 1) * 512]
                if ev_prev is not None and (m * 8 + n) >= 2:
                    te.wait_ge(self.sem["sv"], ev_two_ago)
                stc = None
                for kk in range(4):
                    stc = MM(ps, A.bf("wk_b", kk * 768 + m * 128, kk * 768 + (m + 1) * 128),
                             A.bf("ckv", kk * 4096 + n * 512, kk * 4096 + (n + 1) * 512),
                             kk == 0, kk == 3, inc=(kk == 3))
                self.wait(ve, "st", stc)
                if m < 4:
                    dst = A.bf("knope", m * 4096 + n * 512, m * 4096 + (n + 1) * 512)
                else:
                    dst = A.bf("krope", (m - 4) * 4096 + n * 512, (m - 4) * 4096 + (n + 1) * 512)
                ev = V("tensor_copy", dst, ps)
                ev_two_ago = ev_prev if ev_prev is not None else ev
                ev_prev = ev
        V("memset", A.bf("vaug"), 1.0)
        vr = A.bf("vaug").rearrange("p (t g c) -> p t g c", g=4, c=130)
        for t in range(32):
            slot = t % 2
            ps = psc1[:, slot * 512:(slot + 1) * 512]
            if t >= 2:
                te.wait_ge(self.sem["sv"], ev_two_ago)
            stc = None
            for kk in range(4):
                stc = MM(ps, A.bf("ckv", kk * 4096 + t * 128, kk * 4096 + (t + 1) * 128),
                         A.bf("wv_b", kk * 512, (kk + 1) * 512),
                         kk == 0, kk == 3, inc=(kk == 3))
            self.wait(ve, "st", stc)
            ev = V("tensor_copy", vr[:, t, :, 0:128],
                   ps.rearrange("p (g c) -> p g c", g=4))
            ev_two_ago = ev_prev
            ev_prev = ev
        A.free("ckv")
        A.free("wk_b")
        A.free("wv_b")

        FA.alloc("rsc1", 512)
        FA.alloc("rsc2", 512)
        self.wait(te, "sv")
        self.wait(te, "ss")
        ve.wait_ge(self.sem["dbgo"], self.tab_cnt)
        r2rd = [0, 0]
        tgl = 0
        for gpair in range(2):
            for half in range(2):
                po = half * 64
                for n in range(8):
                    b = tgl
                    tgl ^= 1
                    ps_ = (pst if b == 0 else self.ppv)[po:po + 64, 0:512]
                    kslice = A.bf("krope", gpair * 4096 + n * 512,
                                  gpair * 4096 + (n + 1) * 512)[po:po + 64, :]
                    if r2rd[b]:
                        te.wait_ge(self.sem["sv"], r2rd[b])
                    stc = MM(ps_, self.R[po:po + 64, :], kslice,
                             True, True, inc=True)
                    cosb = A.bf("tab_cosk", n * 512, (n + 1) * 512)[po:po + 64, :]
                    sinb = A.bf("tab_sink", n * 512, (n + 1) * 512)[po:po + 64, :]
                    r1 = FA.r("rsc1")[po:po + 64, :]
                    r2 = FA.r("rsc2")[po:po + 64, :]
                    V("tensor_mul", r1, kslice, cosb)
                    self.wait(ve, "st", stc)
                    r2rd[b] = V("tensor_mul", r2, ps_, sinb)
                    V("tensor_add", kslice, r1, r2)
        A.free("tab_sink")
        A.free("tab_cosk")
        if self.dbg:
            self.dbg_dump("dbg_knope", [128, 16384], A.bf("knope"), waits=("sv",))
            self.dbg_dump("dbg_krope", [128, 8192], A.bf("krope"), waits=("sv",))
            self.dbg_dump("dbg_vaug", [128, 16640], A.bf("vaug"), waits=("sv",))

        A.alloc("qnope", 16384, at=10240)
        A.alloc("qrope", 8192, at=0)
        A.alloc("wqbuf", 2 * 2048, at=51456)
        wq_dma = [0] * 24
        q_done_st = [0] * 24

        def issue_wq(m):
            par = m % 2
            if m >= 2:
                gp.wait_ge(self.sem["st"], q_done_st[m - 2])
                gp.wait_ge(self.sem["dwq%d" % par], wq_dma[m - 2])
            else:
                self.wait(gp, "sv")
                self.wait(gp, "ss")
                self.wait(gp, "st")
            wq_dma[m] = self.DMA(gp, "dwq%d" % par,
                                 A.bf("wqbuf", par * 2048, (par + 1) * 2048),
                                 self.wq[m].rearrange("p k j -> p (k j)"))

        issue_wq(0)
        issue_wq(1)
        self.wait(te, "ss")
        self.wait(te, "sv")
        pfree = [0, 0]
        for m in range(24):
            par = m % 2
            psb = psc0 if par == 0 else psc1
            te.wait_ge(self.sem["dwq%d" % par], wq_dma[m])
            if pfree[par]:
                te.wait_ge(self.sem["ss"], pfree[par])
            stc = None
            for n in range(2):
                ps = psb[:, n * 512:(n + 1) * 512]
                for kk in range(NKCH):
                    stc = MM(ps, A.bf("wqbuf", par * 2048 + kk * 128, par * 2048 + (kk + 1) * 128),
                             A.bf("xnown", kk * 1024 + n * 512, kk * 1024 + (n + 1) * 512),
                             kk == 0, kk == NKCH - 1, inc=(kk == NKCH - 1 and n == 1))
            q_done_st[m] = self.cnt["st"]
            if m + 2 < 24:
                issue_wq(m + 2)
            self.wait(se, "st", q_done_st[m])
            if m < 16:
                pfree[par] = SC("activation", A.bf("qnope", m * 1024, (m + 1) * 1024),
                                psb[:, 0:1024], ACT.Copy, bias=0.0, scale=SCL)
            else:
                p = m - 16
                if m == 16:
                    self.wait(se, "sv")
                pfree[par] = SC("activation", A.bf("qrope", p * 1024, (p + 1) * 1024),
                                psb[:, 0:1024], ACT.Copy, bias=0.0, scale=SCL)
        self.wait(te, "ss")
        self.wait(te, "sv")
        self.wait(ve, "ss")
        r2rd = [0, 0]
        tgl = 0
        for p in range(8):
            for half in range(2):
                po = half * 64
                for n in range(2):
                    b = tgl
                    tgl ^= 1
                    ps_ = (pst if b == 0 else self.ppv)[po:po + 64, 0:512]
                    qsl = A.bf("qrope", p * 1024 + n * 512,
                               p * 1024 + (n + 1) * 512)[po:po + 64, :]
                    if r2rd[b]:
                        te.wait_ge(self.sem["sv"], r2rd[b])
                    stc2 = MM(ps_, self.R[po:po + 64, :], qsl, True, True, inc=True)
                    cosb = A.bf("tab_cosq", n * 512, (n + 1) * 512)[po:po + 64, :]
                    sinb = A.bf("tab_sinq", n * 512, (n + 1) * 512)[po:po + 64, :]
                    r1 = FA.r("rsc1")[po:po + 64, :]
                    r2 = FA.r("rsc2")[po:po + 64, :]
                    V("tensor_mul", r1, qsl, cosb)
                    self.wait(ve, "st", stc2)
                    r2rd[b] = V("tensor_mul", r2, ps_, sinb)
                    V("tensor_add", qsl, r1, r2)
        A.free("xnown")
        A.free("wqbuf")
        A.free("tab_sinq")
        A.free("tab_cosq")
        FA.free("rsc1")
        FA.free("rsc2")

        self.wait(ve, "ss")
        for h in range(16):
            V("reduce_max", st(self.S_QM + h), A.bf("qnope", h * 1024, (h + 1) * 1024),
              AX, apply_absolute_value=True)
        for p in range(8):
            V("reduce_max", st(self.S_QMR + p), A.bf("qrope", p * 1024, (p + 1) * 1024),
              AX, apply_absolute_value=True)
        for g in range(4):
            V("reduce_max", st(self.S_KM + g), A.bf("knope", g * 4096, (g + 1) * 4096),
              AX, apply_absolute_value=True)
        for gpair in range(2):
            V("reduce_max", st(self.S_KMR + gpair),
              A.bf("krope", gpair * 4096, (gpair + 1) * 4096),
              AX, apply_absolute_value=True)
        for h in range(16):
            V("tensor_mul", st(self.S_PR + h), st(self.S_QM + h), st(self.S_KM + h // 4))
        for h in range(16):
            p, po = QR_PAIR[h]
            rp = st(self.S_T0)
            V("tensor_mul", rp, st(self.S_QMR + p), st(self.S_KMR + p // 4))
            V("tensor_mul", rp, rp, st(self.S_HM if po == 0 else self.S_UHM))
            V("tensor_add", st(self.S_PR + h), st(self.S_PR + h), rp)
        self.wait(te, "sv")
        stc = MM(self.pst[0:1, 0:16], st(self.S_ONE), self.stats[:, self.S_PR:self.S_PR + 16],
                 True, True, inc=True)
        self.wait(ve, "st", stc)
        V("tensor_scalar", self.stats[0:1, self.S_PR:self.S_PR + 16],
          self.pst[0:1, 0:16], -1.0, None, ALU.mult)
        self.wait(te, "sv")
        stc = MM(self.pst[:, 0:16], self.stats[0:1, self.S_ONESROW:self.S_ONESROW + 128],
                 self.stats[0:1, self.S_PR:self.S_PR + 16], True, True, inc=True)
        self.wait(ve, "st", stc)
        V("tensor_copy", self.stats[:, self.S_NB:self.S_NB + 16], self.pst[:, 0:16])
        if self.dbg:
            self.dbg_dump("dbg_q", [128, 16384], A.bf("qnope"), waits=("ss",))
            self.dbg_dump("dbg_qr", [128, 8192], A.bf("qrope"), waits=("sv",))
            nb = self.dbg_out("dbg_nb", [128, 16])
            self.wait(sy, "sv")
            self.DMA(sy, "dout", nb[:, :], self.stats[:, self.S_NB:self.S_NB + 16])

    def phase3(self):
        nc, A, V, SC, G, MM = self.nc, self.A, self.V, self.SC, self.G, self.MM
        st, pst = self.st, self.pst
        sy, te, ve, se, gp = nc.sync, nc.tensor, nc.vector, nc.scalar, nc.gpsimd
        A.alloc("ppan", 16896, at=51456)
        A.alloc("atst", 256, at=68352)
        self.fence()
        dat_base = (self.cnt["dho0"], self.cnt["dho1"])
        dat_cnt = {u: dat_base[u % 2] + 16 * (u // 2 + 1) for u in range(128)}

        self.wait(te, "dmeta")
        self.wait(se, "dmeta")
        self.wait(ve, "dmeta")
        rt = te.alloc_register("rt")
        rs = se.alloc_register("rs")
        rv = ve.alloc_register("rv")
        te.reg_load(rt, self.meta_sb[0:1, 0:1])
        se.reg_load(rs, self.meta_sb[0:1, 0:1])
        ve.reg_load(rv, self.meta_sb[0:1, 0:1])

        FA = self.FA
        FA.alloc("wmba", 2 * 1024)
        FA.alloc("wmbb", 2 * 1024)
        FA.alloc("wmbc", 2 * 1408)
        V("memset", self.stats[:, self.S_AW:self.S_AW + 3], 0.0)
        wstr = [
            dict(dram=self.w1, n=88, parts=2, w=1024, buf="wmba",
                 sems=("dwsy0", "dwsy1"), acc=self.S_AW, q="sy"),
            dict(dram=self.w2, n=88, parts=2, w=1024, buf="wmbb",
                 sems=("dw3a", "dw3b"), acc=self.S_AW + 1, q="gp"),
            dict(dram=self.w3, n=64, parts=4, w=1408, buf="wmbc",
                 sems=("dws0", "dws1"), acc=self.S_AW + 2, q="sy"),
        ]
        for s_ in wstr:
            b0, b1 = self.cnt[s_["sems"][0]], self.cnt[s_["sems"][1]]
            s_["dpre"] = [(b0 if i % 2 == 0 else b1) + 16 * (i // 2 + 1)
                          for i in range(s_["n"])]

        sched = []
        for h in range(16):
            blk = []
            for k, s_ in enumerate(wstr):
                blk += [(k, i) for i in range((h * s_["n"]) // 16,
                                              ((h + 1) * s_["n"]) // 16)]
            sched.append(blk)
        nch = [len(x) for x in sched]
        pnch = [sum(nch[:h]) for h in range(16)]

        base = dict(self.cnt)
        red_pre = {}
        _cv = base["sv"]
        for h in range(16):
            for ch in sched[h]:
                _cv += 2
                red_pre[ch] = _cv
            _cv += 24

        def plan(c):
            tiles = owned_tiles(c)
            NKT = tiles[-1] + 1
            sj = [next(j for j in range(8) if tiles[j] >= kt) for kt in range(NKT)]
            poff = [0] * NKT
            acc = 0
            for kt in range(NKT):
                poff[kt] = acc
                acc += (8 - sj[kt]) * 128
            p = dict(tiles=tiles, NKT=NKT, sj=sj, poff=poff,
                     scores_st={}, pvj_st={}, exp_ss={}, mask_sv={}, evj_sv={})
            cst, css, csv = base["st"], base["ss"], base["sv"]
            for h in range(16):
                csv += 2 * nch[h]
                for kt in range(NKT):
                    cst += 1
                    p["scores_st"][(h, kt)] = cst
                    css += 1
                    p["exp_ss"][(h, kt)] = css
                    if kt in tiles:
                        csv += 1
                        p["mask_sv"][(h, kt)] = csv
                for j in range(8):
                    cst += 1
                    p["pvj_st"][(h, j)] = cst
                    csv += 2
                    p["evj_sv"][(h, j)] = csv
            p["end"] = dict(st=cst, ss=css, sv=csv)
            return p

        plans = [plan(c) for c in range(4)]
        ceil = dict(st=max(p["end"]["st"] for p in plans),
                    ss=max(p["end"]["ss"] for p in plans),
                    sv=max(p["end"]["sv"] for p in plans))

        def q_rhs(h, col0, w, which):
            if which == "n":
                return A.bf("qnope", h * 1024 + col0, h * 1024 + col0 + w)
            pq, po = QR_PAIR[h]
            return A.bf("qrope", pq * 1024 + col0, pq * 1024 + col0 + w)[po:po + 64, :]

        for c in range(4):
            p = plans[c]
            tiles, NKT, sj, poff = p["tiles"], p["NKT"], p["sj"], p["poff"]

            self.cnt = dict(base)
            with te.If_eq(rt, c):
                for h in range(16):
                    g = h // 4
                    goff = 64 * (g % 2)
                    for kt in range(NKT):
                        u = h * NKT + kt
                        buf = u % 2
                        psb = (self.psc0 if buf == 0 else self.psc1)
                        j0 = sj[kt]
                        col0, w = j0 * 128, (8 - j0) * 128
                        if u >= 2:
                            te.wait_ge(self.sem["ss"],
                                       p["exp_ss"][(h, kt - 2) if kt >= 2 else (h - 1, NKT - (2 - kt))])
                        kn = A.bf("knope", g * 4096 + kt * 128, g * 4096 + (kt + 1) * 128)
                        kr = A.bf("krope", (g // 2) * 4096 + kt * 128,
                                  (g // 2) * 4096 + (kt + 1) * 128)[goff:goff + 64, :]
                        nchunk = 1 if w <= 512 else 2
                        for ci in range(nchunk):
                            off = ci * 512
                            nw = min(512, w - off)
                            MM(psb[:, off:off + nw], kn,
                               q_rhs(h, col0 + off, nw, "n"), True, False)
                            MM(psb[:, off:off + nw], kr,
                               q_rhs(h, col0 + off, nw, "r"), False, True,
                               inc=(ci == nchunk - 1))
                        assert self.cnt["st"] == p["scores_st"][(h, kt)]
                    for j in range(8):
                        te.wait_ge(self.sem["ss"], p["exp_ss"][(h, tiles[j])])
                        if j in tiles or True:
                            te.wait_ge(self.sem["sv"], p["mask_sv"][(h, tiles[j])])
                        if j >= 3:
                            te.wait_ge(self.sem["sv"], p["evj_sv"][(h, j - 3)])
                        elif h > 0:
                            te.wait_ge(self.sem["sv"],
                                       p["evj_sv"][(h - 1, {0: 6, 1: 7, 2: 5}[j])])
                        bank = self.ppv[:, (j % 3) * 512:(j % 3) * 512 + 129]
                        for kt in range(tiles[j] + 1):
                            MM(bank,
                               A.bf("ppan", poff[kt] + (j - sj[kt]) * 128,
                                    poff[kt] + (j - sj[kt] + 1) * 128),
                               A.bf("vaug", kt * 520 + g * 130,
                                    kt * 520 + g * 130 + 129),
                               start=(kt == 0), stop=(kt == tiles[j]),
                               inc=(kt == tiles[j]))
                        assert self.cnt["st"] == p["pvj_st"][(h, j)]
                te.wait_ge(self.sem["st"], self.cnt["st"])
                for _ in range(ceil["st"] - self.cnt["st"]):
                    te.nop(nofuse=True).then_inc(self.sem["st"], 1)
                te.nop(nofuse=True).then_inc(self.sem["sp3"], 1)


        te.end_ifs()
        te.nop(nofuse=True)
        for c in range(4):
            p = plans[c]
            tiles, NKT, sj, poff = p["tiles"], p["NKT"], p["sj"], p["poff"]

            self.cnt = dict(base)
            with se.If_eq(rs, c):
                for h in range(16):
                    if h > 0:
                        se.wait_ge(self.sem["st"], p["pvj_st"][(h - 1, 7)])
                    for kt in range(NKT):
                        buf = (h * NKT + kt) % 2
                        psb = (self.psc0 if buf == 0 else self.psc1)
                        j0 = sj[kt]
                        w = (8 - j0) * 128
                        se.wait_ge(self.sem["st"], p["scores_st"][(h, kt)])
                        SC("activation", A.bf("ppan", poff[kt], poff[kt] + w),
                           psb[:, 0:w], ACT.Exp,
                           bias=self.stats[:, self.S_NB + h:self.S_NB + h + 1],
                           scale=1.0)
                        assert self.cnt["ss"] == p["exp_ss"][(h, kt)]
                se.wait_ge(self.sem["ss"], self.cnt["ss"])
                for _ in range(ceil["ss"] - self.cnt["ss"]):
                    se.nop(nofuse=True).then_inc(self.sem["ss"], 1)
                se.nop(nofuse=True).then_inc(self.sem["sp3"], 1)


        se.end_ifs()
        se.nop(nofuse=True)
        for c in range(4):
            p = plans[c]
            tiles, NKT, sj, poff = p["tiles"], p["NKT"], p["sj"], p["poff"]

            self.cnt = dict(base)
            with ve.If_eq(rv, c):
                for h in range(16):
                    for ch in sched[h]:
                        s_ = wstr[ch[0]]
                        i = ch[1]
                        wpar = i % 2
                        ve.wait_ge(self.sem[s_["sems"][wpar]], s_["dpre"][i])
                        V("reduce_sum", st(self.S_T0 + 3),
                          self.FA.r(s_["buf"], wpar * s_["w"], (wpar + 1) * s_["w"]),
                          AX, apply_absolute_value=True)
                        got = V("tensor_add", st(s_["acc"]), st(s_["acc"]),
                                st(self.S_T0 + 3))
                        assert got == red_pre[ch]
                    for kt in range(NKT):
                        if kt not in tiles:
                            continue
                        ve.wait_ge(self.sem["ss"], p["exp_ss"][(h, kt)])
                        blk = A.bf("ppan", poff[kt], poff[kt] + 128)
                        V("tensor_mul", blk, blk, self.TRI)
                        assert self.cnt["sv"] == p["mask_sv"][(h, kt)]
                    for j in range(8):
                        u = h * 8 + j
                        par = u % 2
                        ve.wait_ge(self.sem["st"], p["pvj_st"][(h, j)])
                        if u >= 2:
                            ve.wait_ge(self.sem["dho%d" % par], dat_cnt[u - 2])
                        bank = self.ppv[:, (j % 3) * 512:(j % 3) * 512 + 129]
                        V("reciprocal", st(self.S_LI), bank[:, 128:129])
                        V("tensor_scalar",
                          A.bf("atst", par * 128, (par + 1) * 128),
                          bank[:, 0:128], st(self.S_LI), None, ALU.mult)
                        assert self.cnt["sv"] == p["evj_sv"][(h, j)]
                ve.wait_ge(self.sem["sv"], self.cnt["sv"])
                for _ in range(ceil["sv"] - self.cnt["sv"]):
                    ve.nop(nofuse=True).then_inc(self.sem["sv"], 1)
                ve.nop(nofuse=True).then_inc(self.sem["sp3"], 1)

        ve.end_ifs()
        ve.nop(nofuse=True)

        ev_uniform = lambda h, j: (base["sv"] + 24 * h + 2 * (pnch[h] + nch[h])
                                   + 8 + 2 * (j + 1))
        for p in plans:
            for h in range(16):
                for j in range(8):
                    assert p["evj_sv"][(h, j)] == ev_uniform(h, j)

        def emit_issues(hh):
            if hh >= 16:
                return
            for ch in sched[hh]:
                s_ = wstr[ch[0]]
                i = ch[1]
                wpar = i % 2
                q = sy if s_["q"] == "sy" else gp
                if i >= 2:
                    q.wait_ge(self.sem["sv"], red_pre[(ch[0], i - 2)])
                    q.wait_ge(self.sem[s_["sems"][wpar]], s_["dpre"][i - 2])
                else:
                    q.wait_ge(self.sem["sv"], base["sv"])
                    q.wait_ge(self.sem["ss"], base["ss"])
                p_, w_ = s_["parts"], s_["w"]
                src = s_["dram"][(i // p_) * 128:(i // p_ + 1) * 128,
                                 (i % p_) * w_:(i % p_ + 1) * w_]
                got = self.DMA(q, s_["sems"][wpar],
                               self.FA.r(s_["buf"], wpar * w_, (wpar + 1) * w_),
                               src)
                assert got == s_["dpre"][i]

        emit_issues(0)
        for u in range(128):
            h, j = divmod(u, 8)
            par = u % 2
            gp.wait_ge(self.sem["sv"], ev_uniform(h, j))
            got = self.DMA(gp, "dho%d" % par,
                           self.attn_dram[j * 128:(j + 1) * 128,
                                          h * 128:(h + 1) * 128],
                           A.bf("atst", par * 128, (par + 1) * 128))
            assert got == dat_cnt[u], (u, got, dat_cnt[u])
            if j == 7:
                emit_issues(h + 1)

        self.cnt.update(ceil)
        self.cnt["sp3"] = 3
        for e in (te, se, ve, sy, gp):
            e.wait_ge(self.sem["sp3"], 3)
        te.nop()
        se.nop()
        ve.nop()
        A.free("ppan")
        A.free("atst")
        A.free("knope")
        A.free("krope")
        A.free("vaug")
        A.free("qnope")
        A.free("qrope")

    def phase4(self):
        nc, A, V, SC, G, MM = self.nc, self.A, self.V, self.SC, self.G, self.MM
        st, pst, psc0, psc1 = self.st, self.pst, self.psc0, self.psc1
        sy, te, ve, se, gp = nc.sync, nc.tensor, nc.vector, nc.scalar, nc.gpsimd
        A.alloc("attn_fm", 16384, at=16384)
        A.alloc("atld", 4096, at=49152)
        self.fence()
        self.wait(sy, "dho0")
        self.wait(sy, "dho1")
        ar_dma = [0] * 8
        tr_done_st = [0] * 8

        def issue_ar(j):
            par = j % 2
            if j >= 2:
                sy.wait_ge(self.sem["st"], tr_done_st[j - 2])
                sy.wait_ge(self.sem["dxb%d" % par], ar_dma[j - 2])
            ar_dma[j] = self.DMA(sy, "dxb%d" % par,
                                 A.bf("atld", par * 2048, (par + 1) * 2048),
                                 self.attn_dram[j * 128:(j + 1) * 128, :])

        issue_ar(0)
        issue_ar(1)
        fmr = A.bf("attn_fm").rearrange("p (h c) -> p h c", c=1024)
        for j in range(8):
            par = j % 2
            te.wait_ge(self.sem["dxb%d" % par], ar_dma[j])
            for hg in range(4):
                self.tp_group(
                    [A.bf("atld", par * 2048 + (hg * 4 + hh) * 128,
                          par * 2048 + (hg * 4 + hh + 1) * 128) for hh in range(4)],
                    fmr[:, hg * 4:hg * 4 + 4, j * 128:(j + 1) * 128])
            tr_done_st[j] = self.cnt["st"]
            if j + 2 < 8:
                issue_ar(j + 2)
        A.alloc("attn_out", 16384, at=0)
        A.alloc("wo_bf", 16384, at=32768)

        wo_dma = [0] * 4
        wo_done_st = [0] * 4
        ev_sv = 0
        ev_prev2 = 0

        def issue_wo(og):
            par = og % 2
            if og >= 2:
                gp.wait_ge(self.sem["st"], wo_done_st[og - 2])
                gp.wait_ge(self.sem["dwq%d" % par], wo_dma[og - 2])
            else:
                self.wait(gp, "sv")
                self.wait(gp, "ss")
            for k in range(NKCH):
                self.DMA(gp, "dwq%d" % par,
                         A.bf("wo_bf", par * 8192 + k * 512, par * 8192 + (k + 1) * 512),
                         self.wo[k * 128:(k + 1) * 128, og * 512:(og + 1) * 512])
            wo_dma[og] = self.cnt["dwq%d" % par]

        issue_wo(0)
        issue_wo(1)
        for og in range(4):
            par = og % 2
            te.wait_ge(self.sem["dwq%d" % par], wo_dma[og])
            for tt in range(8):
                slot = tt % 2
                ps = psc0[:, slot * 512:(slot + 1) * 512]
                if ev_prev2:
                    te.wait_ge(self.sem["sv"], ev_prev2)
                stc = None
                for k in range(NKCH):
                    stc = MM(ps, A.bf("attn_fm", k * 1024 + tt * 128, k * 1024 + (tt + 1) * 128),
                             A.bf("wo_bf", par * 8192 + k * 512, par * 8192 + (k + 1) * 512),
                             k == 0, k == NKCH - 1, inc=(k == NKCH - 1))
                self.wait(ve, "st", stc)
                ev_prev2 = ev_sv
                ev_sv = V("tensor_copy",
                          A.bf("attn_out", tt * 2048 + og * 512, tt * 2048 + (og + 1) * 512),
                          ps)
            wo_done_st[og] = self.cnt["st"]
            if og + 2 < 4:
                issue_wo(og + 2)
        A.free("attn_fm")
        A.free("wo_bf")
        A.free("atld")
        if self.dbg:
            self.dbg_dump("dbg_attnout", [128, 16384], A.bf("attn_out"), waits=("sv",))

    def phase5(self):
        nc, A, V, SC, G, MM = self.nc, self.A, self.V, self.SC, self.G, self.MM
        st, pst = self.st, self.pst
        sy, te, ve, se, gp = nc.sync, nc.tensor, nc.vector, nc.scalar, nc.gpsimd
        FA = self.FA
        self.wait(te, "sv")
        self.wait(te, "ss")
        stc = MM(pst[0:1, 0:3], st(self.S_ONE), self.stats[:, self.S_AW:self.S_AW + 3],
                 True, True, inc=True)
        self.wait(ve, "st", stc)
        row = self.stats[0:1, self.S_PR:self.S_PR + 16]
        V("tensor_scalar", row[:, 0:3], pst[0:1, 0:3], 1.0 / (HF * D), None, ALU.mult)
        V("tensor_scalar", row[:, 0:3], row[:, 0:3], 1e-5, None, ALU.max)
        V("tensor_scalar", row[:, 3:6], row[:, 0:3], 0.5, None, ALU.mult)
        V("tensor_scalar", row[:, 6:9], row[:, 3:6], -1.0, None, ALU.mult)
        self.wait(te, "sv")
        stc = MM(pst[:, 0:9], self.stats[0:1, self.S_ONESROW:self.S_ONESROW + 128],
                 row[:, 0:9], True, True, inc=True)
        self.wait(ve, "st", stc)
        V("tensor_copy", self.stats[:, self.S_SW:self.S_SW + 9], pst[:, 0:9])
        for j in range(8):
            V("tensor_mul", st(self.S_C1 + j), st(self.S_SXI + j), st(self.S_SW))
            V("tensor_mul", st(self.S_C2 + j), st(self.S_SXI + j), st(self.S_SW + 1))
        FA.free("wmba")
        FA.free("wmbb")
        FA.free("wmbc")

    def phase6(self):
        nc, A, V, SC, G, MM = self.nc, self.A, self.V, self.SC, self.G, self.MM
        st, pst, psc0, psc1 = self.st, self.pst, self.psc0, self.psc1
        sy, te, ve, se, gp = nc.sync, nc.tensor, nc.vector, nc.scalar, nc.gpsimd
        A.alloc("iw1T", 16384, at=16384)
        A.alloc("iw2T", 16384, at=32768)
        A.alloc("ixq", 16384, at=49152)
        A.alloc("wqz", 4096, at=65536)
        A.alloc("wcmpb", 1024, at=69632)
        FA = self.FA
        FA.alloc("wch1", 3 * 1024)
        FA.alloc("wch2", 2 * 1024)
        FA.alloc("ma2", 512)
        FA.alloc("mh0", 512)
        FA.alloc("mh1", 512)
        self.fence()
        self.DMA(gp, "dwg", A.bf("ixq"), self.ixq_dram[:, :])
        ixq_cnt = self.cnt["dwg"]
        te.wait_ge(self.sem["dwg"], ixq_cnt)
        V("memset", self.stats[:, self.S_HAM:self.S_HAM + 8], 0.0)

        wsrc = {1: self.w1, 2: self.w2}
        wchn = {1: "wch1", 2: "wch2"}
        wq_eng = {1: sy, 2: se}
        wq_sem = {1: ("dwsy0", "dwsy1"), 2: ("dws0", "dws1")}
        iwt = {1: "iw1T", 2: "iw2T"}
        thr_col = {1: self.S_SW + 3, 2: self.S_SW + 4}
        nthr_col = {1: self.S_SW + 6, 2: self.S_SW + 7}
        wdma = {1: [0] * 2 * NW1CH, 2: [0] * 2 * NW1CH}
        wcons_sv = {1: [0] * 2 * NW1CH, 2: [0] * 2 * NW1CH}
        wtrans_sv = {}
        qz_last_st = [0, 0]
        mm_done_st = {}

        nslot = {1: 3, 2: 2}

        def issue_w(w, mm):
            slot = mm % nslot[w]
            par = mm % 2
            e = wq_eng[w]
            ns = nslot[w]
            if mm >= ns:
                e.wait_ge(self.sem["sv"], wcons_sv[w][mm - ns])
            if mm >= 2:
                e.wait_ge(self.sem[wq_sem[w][par]], wdma[w][mm - 2])
            if mm < ns:
                self.wait(e, "sv")
                self.wait(e, wq_sem[w][par])
            m, hf = divmod(mm, 2)
            wdma[w][mm] = self.DMA(e, wq_sem[w][par],
                                   FA.r(wchn[w], slot * 1024, (slot + 1) * 1024),
                                   wsrc[w][m * 128:(m + 1) * 128,
                                           hf * 1024:(hf + 1) * 1024])

        for w in (1, 2):
            for _i in range(nslot[w]):
                issue_w(w, _i)

        h_spill = {}
        for og in range(11):
            iwpar = og % 2
            for w in (1, 2):
                for rc in range(4):
                    m = og * 4 + rc
                    qz = A.bf("wqz", (m % 2) * 2048, (m % 2 + 1) * 2048)
                    if qz_last_st[m % 2]:
                        ve.wait_ge(self.sem["st"], qz_last_st[m % 2])
                    for hf in range(2):
                        mm = 2 * m + hf
                        slot = mm % nslot[w]
                        par = mm % 2
                        ve.wait_ge(self.sem[wq_sem[w][par]], wdma[w][mm])
                        wc = FA.r(wchn[w], slot * 1024, (slot + 1) * 1024)
                        V("tensor_scalar", A.bf("wcmpb"), wc,
                          self.stats[:, nthr_col[w]:nthr_col[w] + 1], None, ALU.is_lt)
                        wcons_sv[w][mm] = V(
                            "scalar_tensor_tensor", qz[:, hf * 1024:(hf + 1) * 1024],
                            wc, self.stats[:, thr_col[w]:thr_col[w] + 1], A.bf("wcmpb"),
                            ALU.is_gt, ALU.subtract)
                        if mm + nslot[w] < 2 * NW1CH:
                            issue_w(w, mm + nslot[w])
                    dstr = A.bf(iwt[w], iwpar * 8192, (iwpar + 1) * 8192) \
                        .rearrange("p (k c) -> p k c", c=512)
                    te.wait_ge(self.sem["sv"], wcons_sv[w][2 * m + 1])
                    for kb in range(0, NKCH, 4):
                        if og >= 2 and (w, og - 2) in mm_done_st:
                            te.wait_ge(self.sem["st"], mm_done_st[(w, og - 2)])
                        wtrans_sv[(w, m)] = self.tp_group(
                            [qz[:, (kb + kk) * 128:(kb + kk + 1) * 128]
                             for kk in range(4)],
                            dstr[:, kb:kb + 4, rc * 128:(rc + 1) * 128])
                    qz_last_st[m % 2] = self.cnt["st"]
            for tt in range(8):
                slot = tt % 2
                for w, psb in ((1, psc0), (2, psc1)):
                    ps = psb[:, slot * 512:(slot + 1) * 512]
                    te.wait_ge(self.sem["ss"], wtrans_sv[(w, og * 4 + 3)])
                    if (w, og, tt, "evac") in h_spill:
                        pass
                    if tt >= 2:
                        te.wait_ge(self.sem["ss" if w == 1 else "sv"],
                                   h_spill[(w, og, tt - 2)])
                    stc = None
                    for k in range(NKCH):
                        stc = MM(ps, A.bf("ixq", k * 1024 + tt * 128, k * 1024 + (tt + 1) * 128),
                                 A.bf(iwt[w], iwpar * 8192 + k * 512, iwpar * 8192 + (k + 1) * 512),
                                 k == 0, k == NKCH - 1, inc=(k == NKCH - 1))
                    mm_done_st[(w, og, tt)] = self.cnt["st"]
                mm_done_st[(1, og)] = mm_done_st[(1, og, tt)]
                mm_done_st[(2, og)] = mm_done_st[(2, og, tt)]
                self.wait(se, "st", mm_done_st[(1, og, tt)])
                self.wait(se, "sv")
                h_spill[(1, og, tt)] = SC(
                    "activation", self.ppv[:, 512:1024], psc0[:, slot * 512:(slot + 1) * 512],
                    ACT.Silu, bias=0.0,
                    scale=self.stats[:, self.S_C1 + tt:self.S_C1 + tt + 1])
                self.wait(ve, "st", mm_done_st[(2, og, tt)])
                h_spill[(2, og, tt)] = V(
                    "tensor_scalar", FA.r("ma2"), psc1[:, slot * 512:(slot + 1) * 512],
                    self.stats[:, self.S_C2 + tt:self.S_C2 + tt + 1], None, ALU.mult)
                gi = og * 8 + tt
                mhp = gi % 2
                mh = FA.r("mh0" if mhp == 0 else "mh1")
                if gi >= 2:
                    po, ptt = divmod(gi - 2, 8)
                    ve.wait_ge(self.sem["dho%d" % mhp], h_spill.get(("dma", po, ptt), 0))
                self.wait(ve, "ss")
                V("tensor_mul", mh, FA.r("ma2"), self.ppv[:, 512:1024])
                V("reduce_max", st(self.S_T0), mh, AX, apply_absolute_value=True)
                hs = V("tensor_max", st(self.S_HAM + tt), st(self.S_HAM + tt), st(self.S_T0))
                gp.wait_ge(self.sem["sv"], hs)
                h_spill[("dma", og, tt)] = self.DMA(
                    gp, "dho%d" % mhp, self.h_dram[tt * 128:(tt + 1) * 128,
                                                   og * 512:(og + 1) * 512], mh)
        self.wait(ve, "sg")
        for j in range(8):
            V("tensor_scalar", st(self.S_T0), st(self.S_HAM + j), 1e-5, None, ALU.max)
            V("reciprocal", st(self.S_T0 + 1), st(self.S_T0))
            V("tensor_scalar", st(self.S_SXH + j), st(self.S_T0 + 1), 127.0, None, ALU.mult)
            V("tensor_scalar", st(self.S_T0), st(self.S_T0), 1.0 / 127.0, None, ALU.mult)
            V("tensor_mul", st(self.S_S3 + j), st(self.S_T0), st(self.S_SW + 2))
        for nm in ("iw1T", "iw2T", "wqz", "ixq"):
            A.free(nm)
        A.free("wcmpb")
        for nm in ("wch1", "wch2", "ma2", "mh0", "mh1"):
            FA.free(nm)

    def phase7(self):
        nc, A, V, SC, G, MM = self.nc, self.A, self.V, self.SC, self.G, self.MM
        st, pst, psc0 = self.st, self.pst, self.psc0
        sy, te, ve, se, gp = nc.sync, nc.tensor, nc.vector, nc.scalar, nc.gpsimd
        A.alloc("ihq", 2048, at=83968)
        A.alloc("wqz3", 2048, at=86016)
        A.alloc("ih", 45056, at=16384)
        A.alloc("w3T", 22528, at=61440)
        FA = self.FA
        FA.alloc("hrl", 4 * 512)
        FA.alloc("z7p", 4 * 512)
        FA.alloc("f7p", 4 * 512)
        FA.alloc("t7p", 512)
        self.IA.alloc("i7", 512)
        self.fence()
        for e in (se, sy, ve, te, gp):
            self.wait(e, "dho0")
            self.wait(e, "dho1")
        self.wait(sy, "dwsy0")
        self.wait(sy, "dwsy1")
        self.wait(sy, "dxb0")
        self.wait(sy, "dxb1")
        self.wait(se, "dhi0")
        self.wait(se, "dhi1")

        hdma = {}
        tr_sv = {}
        rnd_sv = {}
        pend = []
        ihr = A.bf("ih").rearrange("p (k c) -> p k c", c=1024)

        def issue_h(og, tt):
            i = og * 8 + tt
            par = i % 2
            q4 = i % 4
            if i >= 2:
                se.wait_ge(self.sem["dhi%d" % par], hdma.get(i - 2, 0))
            if i >= 4:
                se.wait_ge(self.sem["ss"], hdma.get(("cons", i - 4), 0))
            hdma[i] = self.DMA(se, "dhi%d" % par,
                               FA.r("hrl", q4 * 512, (q4 + 1) * 512),
                               self.h_dram[tt * 128:(tt + 1) * 128,
                                           og * 512:(og + 1) * 512])

        for _w in range(4):
            issue_h(_w // 8, _w % 8)
        for og in range(11):
            for tt in range(8):
                i = og * 8 + tt
                par = i % 2
                q4 = i % 4
                se.wait_ge(self.sem["dhi%d" % par], hdma[i])
                if i >= 4:
                    se.wait_ge(self.sem["sv"], rnd_sv[i - 4])
                z7 = FA.r("z7p", q4 * 512, (q4 + 1) * 512)
                f7 = FA.r("f7p", q4 * 512, (q4 + 1) * 512)
                t7 = FA.r("t7p")
                hdma[("cons", i)] = SC(
                    "activation", z7, FA.r("hrl", q4 * 512, (q4 + 1) * 512),
                    ACT.Copy, bias=0.0,
                    scale=self.stats[:, self.S_SXH + tt:self.S_SXH + tt + 1])
                SC("activation", self.IA.r("i7"), z7, ACT.Copy)
                f7ss = SC("activation", f7, self.IA.r("i7"), ACT.Copy)
                if i >= 4:
                    ve.wait_ge(self.sem["st"], tr_sv[("st", i - 4)])
                ve.wait_ge(self.sem["ss"], f7ss)
                ihq = A.bf("ihq", q4 * 512, (q4 + 1) * 512)
                V("scalar_tensor_tensor", t7, z7, -0.5, f7, ALU.add, ALU.is_gt)
                V("scalar_tensor_tensor", z7, z7, 0.5, f7, ALU.add, ALU.is_lt)
                V("tensor_sub", t7, t7, z7)
                rnd_sv[i] = V("tensor_add", ihq, f7, t7)
                if i + 4 < 88:
                    issue_h((i + 4) // 8, (i + 4) % 8)
                pend.append((i, ihq, ihr[:, og * 4:og * 4 + 4,
                                         tt * 128:(tt + 1) * 128]))
                if len(pend) == 4:
                    for pi, pihq, pdst in pend:
                        te.wait_ge(self.sem["sv"], rnd_sv[pi])
                        tr_sv[pi] = self.tp_group(
                            [pihq[:, kk * 128:(kk + 1) * 128] for kk in range(4)],
                            pdst)
                    flush_st = self.cnt["st"]
                    for pi, _, _ in pend:
                        tr_sv[("st", pi)] = flush_st
                    pend = []
        FA.free("hrl")
        FA.free("z7p")
        FA.free("f7p")
        FA.free("t7p")
        self.IA.free("i7")

        FA.alloc("wch3", 4 * 512)
        FA.alloc("w3s", 512)
        FA.alloc("xsl", 2 * 512)
        FA.alloc("f1", 2 * 512)
        FA.alloc("obuf", 2 * 512)
        w3dma = {}
        w3q_sv = {}
        w3t_st = {}
        w3pend = []
        out_dma = {}
        w3r = A.bf("w3T").rearrange("p (k c) -> p k c", c=512)
        for og3 in range(4):
            for mr in range(4):
                m = og3 * 4 + mr
                for sl in range(11):
                    s = m * 11 + sl
                    par = s % 2
                    q4 = s % 4
                    if s >= 2:
                        sy.wait_ge(self.sem["dwsy%d" % par], w3dma.get(s - 2, 0))
                    if s >= 4:
                        sy.wait_ge(self.sem["sv"], w3q_sv.get(s - 4, 0))
                    elif s < 2:
                        self.wait(sy, "ss")
                        self.wait(sy, "sv")
                    w3dma[s] = self.DMA(
                        sy, "dwsy%d" % par, FA.r("wch3", q4 * 512, (q4 + 1) * 512),
                        self.w3[m * 128:(m + 1) * 128, sl * 512:(sl + 1) * 512])
                    ve.wait_ge(self.sem["dwsy%d" % par], w3dma[s])
                    wc = FA.r("wch3", q4 * 512, (q4 + 1) * 512)
                    qz = A.bf("wqz3", par * 512, (par + 1) * 512)
                    if s >= 2:
                        ve.wait_ge(self.sem["st"], w3t_st.get(s - 2, 0))
                    V("tensor_scalar", FA.r("w3s"), wc,
                      self.stats[:, self.S_SW + 8:self.S_SW + 9], None, ALU.is_lt)
                    w3q_sv[s] = V(
                        "scalar_tensor_tensor", qz, wc,
                        self.stats[:, self.S_SW + 5:self.S_SW + 6], FA.r("w3s"),
                        ALU.is_gt, ALU.subtract)
                    te.wait_ge(self.sem["sv"], w3q_sv[s])
                    self.tp_group(
                        [qz[:, kk * 128:(kk + 1) * 128] for kk in range(4)],
                        w3r[:, sl * 4:sl * 4 + 4, mr * 128:(mr + 1) * 128])
                    w3t_st[s] = self.cnt["st"]
            self.wait(te, "sv")
            self.wait(te, "ss")
            for tt in range(8):
                i = og3 * 8 + tt
                slot = tt % 2
                ps = psc0[:, slot * 512:(slot + 1) * 512]
                xpar = i % 2
                if i >= 2:
                    sy.wait_ge(self.sem["sv"], out_dma.get(("xcons", i - 2), 0))
                    sy.wait_ge(self.sem["dxb%d" % xpar], out_dma.get(("x", i - 2), 0))
                out_dma[("x", i)] = self.DMA(
                    sy, "dxb%d" % xpar, FA.r("xsl", xpar * 512, (xpar + 1) * 512),
                    self.x_own[tt * 128:(tt + 1) * 128, og3 * 512:(og3 + 1) * 512])
                if i >= 2:
                    te.wait_ge(self.sem["ss"], out_dma.get(("evac", i - 2), 0))
                stc = None
                for k in range(44):
                    stc = MM(ps, ihr[:, k, tt * 128:(tt + 1) * 128],
                             A.bf("w3T", k * 512, (k + 1) * 512),
                             k == 0, k == 43, inc=(k == 43))
                fpar = i % 2
                f1ap = FA.r("f1", fpar * 512, (fpar + 1) * 512)
                self.wait(se, "st", stc)
                if i >= 2:
                    se.wait_ge(self.sem["sv"], out_dma.get(("xcons", i - 2), 0))
                out_dma[("evac", i)] = SC(
                    "activation", f1ap, ps, ACT.Copy, bias=0.0,
                    scale=self.stats[:, self.S_S3 + tt:self.S_S3 + tt + 1])
                obp = i % 2
                if i >= 2:
                    ve.wait_ge(self.sem["do%d" % obp], out_dma.get(("out", i - 2), 0))
                ve.wait_ge(self.sem["ss"], out_dma[("evac", i)])
                ve.wait_ge(self.sem["dxb%d" % xpar], out_dma[("x", i)])
                V("tensor_add", f1ap, f1ap,
                  A.bf("attn_out", tt * 2048 + og3 * 512, tt * 2048 + (og3 + 1) * 512))
                out_dma[("xcons", i)] = V(
                    "tensor_add", FA.r("obuf", obp * 512, (obp + 1) * 512),
                    f1ap, FA.r("xsl", xpar * 512, (xpar + 1) * 512))
                sy.wait_ge(self.sem["sv"], out_dma[("xcons", i)])
                out_dma[("out", i)] = self.DMA(
                    sy, "do%d" % obp, self.out[tt * 128:(tt + 1) * 128,
                                               og3 * 512:(og3 + 1) * 512],
                    FA.r("obuf", obp * 512, (obp + 1) * 512))
        self.wait(sy, "do0")
        self.wait(sy, "do1")


def rope_tables():
    import ctypes
    libm = ctypes.CDLL("libm.so.6")
    libm.powf.restype = ctypes.c_float
    libm.powf.argtypes = [ctypes.c_float, ctypes.c_float]
    libm.cosf.restype = ctypes.c_float
    libm.cosf.argtypes = [ctypes.c_float]
    libm.sinf.restype = ctypes.c_float
    libm.sinf.argtypes = [ctypes.c_float]
    ex = np.arange(0, DR, 2, dtype=np.float32) / np.float32(DR)
    inv = np.array([np.float32(1.0) / np.float32(libm.powf(ROPE_BASE, float(e)))
                    for e in ex], dtype=np.float32)
    t32 = np.arange(S, dtype=np.float32)
    fr = (t32[:, None] * inv[None, :]).astype(np.float32)
    cs = np.fromiter((libm.cosf(float(v)) for v in fr.ravel()),
                     np.float32, fr.size).reshape(S, 32)
    sn = np.fromiter((libm.sinf(float(v)) for v in fr.ravel()),
                     np.float32, fr.size).reshape(S, 32)
    sink = np.ascontiguousarray(np.tile(sn.T, (4, 1)))
    cosk = np.ascontiguousarray(np.tile(cs.T, (4, 1)))
    return sink, cosk


def host_prep(inputs):
    Wqkv = np.asarray(inputs["L0_attn_Wqkv"], dtype=np.float32)
    Wk_up = np.asarray(inputs["L0_attn_Wk_up"], dtype=np.float32)
    qn = [h * DQK + d for h in range(NH) for d in range(DN)]
    qr = []
    for p in range(8):
        for hl in (EVENG, ODDG):
            hh = hl[p]
            qr += [hh * DQK + DN + r for r in range(DR)]
    cols = qn + qr + list(range(NH * DQK, NH * DQK + LORA))
    Wp = np.ascontiguousarray(Wqkv[:, cols])
    wq_t = np.ascontiguousarray(
        Wp[:, :24 * 128].reshape(NKCH, 128, 24, 128).transpose(2, 1, 0, 3))
    wckv_t = np.ascontiguousarray(
        Wp[:, 24 * 128:].reshape(NKCH, 128, 4, 128).transpose(2, 1, 0, 3))
    kn = [g * DQK + d for g in range(NKV) for d in range(DN)]
    kr = []
    for gp_ in range(2):
        for half in range(2):
            g = gp_ * 2 + half
            kr += [g * DQK + DN + r for r in range(DR)]
    wk_p = np.ascontiguousarray(Wk_up[:, kn + kr])
    return wq_t, wckv_t, wk_p


def make_in_maps(inputs):
    wq_t, wckv_t, wk_p = host_prep(inputs)
    sink, cosk = rope_tables()
    tabk = np.ascontiguousarray(np.concatenate([sink, cosk], axis=1))
    x = np.asarray(inputs["L0_attn_hidden_states"], dtype=np.float32)
    ims = []
    for core in range(8):
        b, c = core // 4, core % 4
        tiles = owned_tiles(c)
        rows = np.concatenate([np.arange(t * 128, (t + 1) * 128) for t in tiles])
        ims.append(dict(
            x_own=np.ascontiguousarray(x[b][rows]),
            x_batch=np.ascontiguousarray(x[b]),
            tabk=tabk,
            tabq=np.ascontiguousarray(
                np.concatenate([sink[:, rows], cosk[:, rows]], axis=1)),
            meta=np.array([[c, 0, 0, 0]], dtype=np.int32),
            wq=wq_t, wckv=wckv_t, wk=wk_p,
            wv=np.ascontiguousarray(np.asarray(inputs["L0_attn_Wv_up"], dtype=np.float32)),
            wo=np.ascontiguousarray(np.asarray(inputs["L0_attn_Wo"], dtype=np.float32)),
            w1=np.ascontiguousarray(np.asarray(inputs["L0_ffn_w1"], dtype=np.float32)),
            w2=np.ascontiguousarray(np.asarray(inputs["L0_ffn_w2"], dtype=np.float32)),
            w3=np.ascontiguousarray(np.asarray(inputs["L0_ffn_w3"], dtype=np.float32)),
        ))
    return ims


def build_and_run(inputs, phases=8, dbg=False, cores=None):
    ims = make_in_maps(inputs)
    b = Bld(phases, dbg)
    nc = b.build()
    res = run_bass_kernel_spmd(nc, ims, core_ids=list(range(8)))
    return b, res


def kernel(**inputs):
    ims = make_in_maps(inputs)
    b = Bld(8, False)
    nc = b.build()
    res = run_bass_kernel_spmd(nc, ims, core_ids=list(range(8)))
    out = np.zeros((B, S, D), dtype=np.float32)
    for core in range(8):
        bb, c = core // 4, core % 4
        for j, t in enumerate(owned_tiles(c)):
            out[bb, t * 128:(t + 1) * 128, :] = res.results[core]["out"][j * 128:(j + 1) * 128, :]
    return out
